# version 21
# baseline (speedup 1.0000x reference)
import numpy as np
from contextlib import ExitStack

import concourse.bass as bass
import concourse.mybir as mybir
from concourse.bass_utils import run_bass_kernel_spmd

F32 = mybir.dt.float32
BF16 = mybir.dt.bfloat16
F16 = mybir.dt.float16
I32 = mybir.dt.int32
U16 = mybir.dt.uint16
ALU = mybir.AluOpType
ACT = mybir.ActivationFunctionType
AX = mybir.AxisListType

H = 2048
PROJ = 8512
INTER = 4096
CONVD = 4352
NH = 64
HD = 64
NST = 128
FI = 5632
GUD = 11264
L = 128
NHT = H // 128
NFT_G = INTER // 128
NOT_GU = GUD // 128
NPAIR_GU = FI // 128
NOT_DN = H // 128
NIT_DN = FI // 128


class Sem:
    def __init__(self, st, nc, name):
        self.h = st.enter_context(nc.semaphore(name))
        self.n = 0

    def bump(self, k=1):
        self.n += k
        return self.n


class DmaStream:

    def __init__(self, st, nc, name, depth=2):
        self.sems = [Sem(st, nc, f"{name}{i}") for i in range(depth)]
        self.i = 0

    def issue(self, eng, out, in_, **kw):
        s = self.sems[self.i % len(self.sems)]
        self.i += 1
        if s.n > 0:
            eng.wait_ge(s.h, s.n)
        eng.dma_start(out, in_, **kw).then_inc(s.h, 16)
        s.n += 16
        return (s, s.n)

    def plan(self, n):
        sims = [s.n for s in self.sems]
        toks = []
        for k in range(n):
            j = (self.i + k) % len(self.sems)
            sims[j] += 16
            toks.append((self.sems[j], sims[j]))
        return toks

    @staticmethod
    def wait(eng, tok):
        eng.wait_ge(tok[0].h, tok[1])


class Env:
    pass


def _quiesce_stores(nc, E):
    with nc.Block() as blk:
        @blk.sync
        def _(sy):
            for strm in (E.dst, E.dst2, E.dst3):
                for s in strm.sems:
                    if s.n > 0:
                        sy.wait_ge(s.h, s.n)


def build(T=2048, phases=(0, 1, 2, 3, 4)):
    nck = T // L
    ntt = T // 128
    nc = bass.Bass("TRN2", target_bir_lowering=False)
    E = Env()
    E.T, E.nck, E.ntt = T, nck, ntt

    E.x_in = nc.dram_tensor("x_in", [T, H], F32, kind="ExternalInput").ap()
    E.winT = nc.dram_tensor("winT", [H, PROJ], F32, kind="ExternalInput").ap()
    E.woutT = nc.dram_tensor("woutT", [INTER, H], F32, kind="ExternalInput").ap()
    E.convw = nc.dram_tensor("convw", [128, 136], F32, kind="ExternalInput").ap()
    E.convb = nc.dram_tensor("convb", [128, 34], F32, kind="ExternalInput").ap()
    E.dtb_in = nc.dram_tensor("dtb", [1, NH], F32, kind="ExternalInput").ap()
    E.alog_in = nc.dram_tensor("alog", [1, NH], F32, kind="ExternalInput").ap()
    E.dvec_in = nc.dram_tensor("dvec", [1, NH], F32, kind="ExternalInput").ap()
    E.nw_in = nc.dram_tensor("nw", [128, 32], F32, kind="ExternalInput").ap()
    E.gup_in = nc.dram_tensor("gup", [GUD, H // 8], I32, kind="ExternalInput").ap()
    E.gus_in = nc.dram_tensor("gus", [GUD, 16], F16, kind="ExternalInput").ap()
    E.dnp_in = nc.dram_tensor("dnp", [H, FI // 8], I32, kind="ExternalInput").ap()
    E.dns_in = nc.dram_tensor("dns", [H, 44], F16, kind="ExternalInput").ap()
    E.out = nc.dram_tensor("out", [T, H], F32, kind="ExternalOutput").ap()

    E.xn_sc = nc.dram_tensor("xn_sc", [NHT, 128, T], BF16).ap()
    E.sg_sc = nc.dram_tensor("sg_sc", [NFT_G, 128, T], BF16).ap()
    E.hs_sc = nc.dram_tensor("hs_sc", [NFT_G, 128, T], BF16).ap()
    E.b_sc = nc.dram_tensor("b_sc", [128, T], BF16).ap()
    E.c_sc = nc.dram_tensor("c_sc", [128, T], BF16).ap()
    E.ynw_sc = nc.dram_tensor("ynw_sc", [nck, 32, 128, 128], BF16).ap()
    E.ffn_sc = nc.dram_tensor("ffn_sc", [ntt, 128, H], BF16).ap()
    E.wgu_sc = nc.dram_tensor("wgu_sc", [NOT_GU, 128, H], BF16).ap()
    E.wdn_sc = nc.dram_tensor("wdn_sc", [4, 128, NIT_DN, 512], BF16).ap()

    st = ExitStack()
    with st:
        e = st.enter_context

        E.iot_r = e(nc.sbuf_tensor("iot_r", [128, 128], F32))
        E.iot_c = e(nc.sbuf_tensor("iot_c", [128, 1], F32))
        E.ident_b = e(nc.sbuf_tensor("ident_b", [128, 128], BF16))
        E.ident_f = e(nc.sbuf_tensor("ident_f", [128, 128], F32))
        E.i64 = e(nc.sbuf_tensor("i64", [64, 64], F32))
        E.triu_f = e(nc.sbuf_tensor("triu_f", [128, 128], F32))
        E.negm_f = e(nc.sbuf_tensor("negm_f", [128, 128], F32))
        E.ones_b = e(nc.sbuf_tensor("ones_b", [128, 1], BF16))
        E.ones64 = e(nc.sbuf_tensor("ones64", [64, 128], F32))
        E.c_one = e(nc.sbuf_tensor("c_one", [128, 1], F32))
        E.c_epsln = e(nc.sbuf_tensor("c_epsln", [128, 1], F32))
        E.c_epsat = e(nc.sbuf_tensor("c_epsat", [128, 1], F32))
        E.c_epsff = e(nc.sbuf_tensor("c_epsff", [128, 1], F32))
        E.dtb_rep = e(nc.sbuf_tensor("dtb_rep", [128, NH], F32))
        E.nega_rep = e(nc.sbuf_tensor("nega_rep", [128, NH], F32))
        E.d_rep = e(nc.sbuf_tensor("d_rep", [128, NH], F32))
        E.dfull = e(nc.sbuf_tensor("dfull", [128, 32], BF16))
        E.convw_sb = e(nc.sbuf_tensor("convw_sb", [128, 136], F32))
        E.convb_sb = e(nc.sbuf_tensor("convb_sb", [128, 34], F32))
        E.nw_bf = e(nc.sbuf_tensor("nw_bf", [128, 32], BF16))
        E.dt_all = e(nc.sbuf_tensor("dt_all", [128, NH], F32))
        E.dtbf_all = e(nc.sbuf_tensor("dtbf_all", [128, nck * NH], BF16))
        E.sfm_all = e(nc.sbuf_tensor("sfm_all", [64, nck * 128], F32))
        E.inv_at = e(nc.sbuf_tensor("inv_at", [128, nck], F32))
        E.inv_ff = e(nc.sbuf_tensor("inv_ff", [128, ntt], F32))

        for nm in ("pe", "dv", "ac", "g2", "pe2", "dv2", "ac2", "pe3", "dv3",
                   "qpe", "qac", "qdv"):
            setattr(E, "s_" + nm, Sem(st, nc, "s_" + nm))
        E.dcn = DmaStream(st, nc, "dcn", 2)
        E.dgp = DmaStream(st, nc, "dgp", 2)
        E.dx = DmaStream(st, nc, "dx", 2)
        E.dx2 = DmaStream(st, nc, "dx2", 2)
        E.dst = DmaStream(st, nc, "dst", 3)
        E.dst2 = DmaStream(st, nc, "dst2", 2)
        E.dst3 = DmaStream(st, nc, "dst3", 2)
        E.dxp = DmaStream(st, nc, "dxp", 2)
        E.dxp2 = DmaStream(st, nc, "dxp2", 2)
        E.dxp3 = DmaStream(st, nc, "dxp3", 2)

        _consts(nc, E)
        if 0 in phases:
            _p0_ln(nc, st, E)
        if 1 in phases:
            _p1_ffn(nc, st, E, fb0_only=(3 in phases))
        if 2 in phases:
            _p2_inproj(nc, st, E)
        if 3 in phases:
            _p3_scan(nc, st, E, with_p1=(1 in phases))
        if 4 in phases:
            _p4_out(nc, st, E)

    nc.finalize()
    return nc


def _consts(nc, E):
    with nc.Block() as blk:
        @blk.gpsimd
        def _(g):
            g.iota(E.iot_r[:], [[1, 128]], channel_multiplier=0,
                   allow_small_or_imprecise_dtypes=True)
            g.iota(E.iot_c[:], [[0, 1]], channel_multiplier=1,
                   allow_small_or_imprecise_dtypes=True)
            E.tok_nw = E.dgp.issue(g, E.nw_bf[:], E.nw_in[:])

        @blk.sync
        def _(sy):
            E.tok_cw = E.dcn.issue(sy, E.convw_sb[:], E.convw[:])
            E.tok_cb = E.dcn.issue(sy, E.convb_sb[:], E.convb[:])

        @blk.vector
        def _(v):
            v.memset(E.ones64[:], 1.0)
            v.memset(E.c_one[:], 1.0)
            v.memset(E.c_epsln[:], 1e-5)
            v.memset(E.c_epsat[:], 1e-5)
            v.memset(E.c_epsff[:], 1e-6)
            v.memset(E.ones_b[:], 1.0)

    with nc.Block() as blk:
        @blk.vector
        def _(v):
            v.tensor_scalar(E.ident_b[:], E.iot_r[:], E.iot_c[:], None, ALU.is_equal)
            v.tensor_scalar(E.ident_f[:], E.iot_r[:], E.iot_c[:], None, ALU.is_equal)
            v.tensor_scalar(E.i64[:], E.iot_r[0:64, 0:64], E.iot_c[0:64, :], None,
                            ALU.is_equal)
            v.tensor_scalar(E.triu_f[:], E.iot_r[:], E.iot_c[:], None, ALU.is_ge)
            v.tensor_scalar(E.negm_f[:], E.iot_r[:], E.iot_c[:], -3.0e38,
                            ALU.is_lt, ALU.mult)

    with (nc.sbuf_tensor("cn_vrow", [1, 3 * NH], F32) as vrow,
          nc.sbuf_tensor("cn_ones1f", [1, 128], F32) as ones1f,
          nc.psum_tensor("cn_ps", [128, 3 * NH], F32) as ps_rep):
        toks = []
        with nc.Block() as blk:
            @blk.sync
            def _(sy):
                toks.append(E.dcn.issue(sy, vrow[0:1, 0:NH], E.dtb_in[:]))
                toks.append(E.dcn.issue(sy, vrow[0:1, NH:2 * NH], E.alog_in[:]))
                toks.append(E.dcn.issue(sy, vrow[0:1, 2 * NH:3 * NH], E.dvec_in[:]))

            @blk.vector
            def _(v):
                v.memset(ones1f[:], 1.0)

        with nc.Block() as blk:
            @blk.tensor
            def _(t):
                for tk in toks:
                    DmaStream.wait(t, tk)
                t.matmul(ps_rep[:], ones1f[:], vrow[:], start=True, stop=True)

        with nc.Block() as blk:
            @blk.vector
            def _(v):
                v.tensor_copy(E.dtb_rep[:], ps_rep[:, 0:NH])
                v.tensor_copy(E.d_rep[:], ps_rep[:, 2 * NH:3 * NH])

            @blk.scalar
            def _(s):
                s.activation(E.nega_rep[:], ps_rep[:, NH:2 * NH], ACT.Exp,
                             scale=1.0)

        with nc.Block() as blk:
            @blk.vector
            def _(v):
                v.tensor_scalar(E.nega_rep[:], E.nega_rep[:], -1.0, None, ALU.mult)
                v.tensor_copy(E.dfull[0:64, :], E.d_rep[0:64, 0:NH:2])
                v.tensor_copy(E.dfull[64:128, :], E.d_rep[64:128, 1:NH:2])


def _p0_ln(nc, st, E):
    ps_ = ExitStack()
    e = ps_.enter_context
    ntt = E.ntt
    xt = [e(nc.sbuf_tensor(f"p0_x{i}", [128, H], F32)) for i in range(2)]
    sq = [e(nc.sbuf_tensor(f"p0_sq{i}", [128, H], F32)) for i in range(2)]
    mcol = e(nc.sbuf_tensor("p0_m", [128, 2], F32))
    m2col = e(nc.sbuf_tensor("p0_m2", [128, 2], F32))
    vcol = e(nc.sbuf_tensor("p0_v", [128, 2], F32))
    icol = e(nc.sbuf_tensor("p0_i", [128, 2], F32))
    xn_tm = [e(nc.sbuf_tensor(f"p0_xn{i}", [128, H], BF16)) for i in range(2)]
    stg = [e(nc.sbuf_tensor(f"p0_st{i}", [128, H], BF16)) for i in range(2)]
    pst = [e(nc.psum_tensor(f"p0_ps{i}", [128, 2048], BF16)) for i in range(2)]

    sqd = [E.s_ac.bump() for _ in range(ntt)]
    std = [E.s_dv.bump() for _ in range(ntt)]
    inv = [E.s_ac2.bump() for _ in range(ntt)]
    xnd = [E.s_dv2.bump() for _ in range(ntt)]
    tpd = [E.s_pe.bump() for _ in range(ntt)]
    evd = [E.s_g2.bump() for _ in range(ntt)]
    xtok = []
    sttok = []

    with nc.Block() as blk:
        @blk.sync
        def _(sy):
            for i in range(ntt):
                if i >= 2:
                    sy.wait_ge(E.s_dv2.h, xnd[i - 2])
                    sy.wait_ge(E.s_ac.h, sqd[i - 2])
                xtok.append(E.dx.issue(
                    sy, xt[i % 2][:], E.x_in[i * 128:(i + 1) * 128, :]))

        @blk.scalar
        def _(s):
            for i in range(ntt):
                p = i % 2
                DmaStream.wait(s, xtok[i])
                if i >= 2:
                    s.wait_ge(E.s_dv.h, std[i - 2])
                s.activation(sq[p][:], xt[p][:], ACT.Square).then_inc(E.s_ac.h, 1)
                if i >= 1:
                    s.wait_ge(E.s_g2.h, evd[i - 1])
                    sttok.append(E.dst.issue(
                        s,
                        E.xn_sc[:, :, (i - 1) * 128:i * 128].transpose([1, 0, 2]),
                        stg[(i - 1) % 2][:].rearrange("p (a t) -> p a t", a=NHT)))
                s.wait_ge(E.s_dv.h, std[i])
                s.activation(icol[:, p:p + 1], vcol[:, p:p + 1], ACT.Ln,
                             bias=E.c_epsln[:])
                s.drain()
                s.activation(icol[:, p:p + 1], icol[:, p:p + 1], ACT.Exp,
                             scale=-0.5).then_inc(E.s_ac2.h, 1)
            s.wait_ge(E.s_g2.h, evd[ntt - 1])
            sttok.append(E.dst.issue(
                s,
                E.xn_sc[:, :, (ntt - 1) * 128:ntt * 128].transpose([1, 0, 2]),
                stg[(ntt - 1) % 2][:].rearrange("p (a t) -> p a t", a=NHT)))

        @blk.vector
        def _(v):
            for i in range(ntt):
                p = i % 2
                if i >= 1:
                    v.wait_ge(E.s_pe.h, tpd[i - 1])
                    if i >= 3:
                        DmaStream.wait(v, sttok[i - 3])
                    v.tensor_copy(stg[(i - 1) % 2][:], pst[(i - 1) % 2][:]) \
                        .then_inc(E.s_g2.h, 1)
                DmaStream.wait(v, xtok[i])
                v.wait_ge(E.s_ac.h, sqd[i])
                if i >= 2:
                    v.wait_ge(E.s_ac2.h, inv[i - 2])
                    v.drain()
                v.reduce_sum(mcol[:, p:p + 1], xt[p][:], axis=AX.X)
                v.reduce_sum(m2col[:, p:p + 1], sq[p][:], axis=AX.X)
                v.drain()
                v.tensor_scalar(mcol[:, p:p + 1], mcol[:, p:p + 1], -1.0 / H,
                                None, ALU.mult)
                v.tensor_scalar(m2col[:, p:p + 1], m2col[:, p:p + 1], 1.0 / H,
                                None, ALU.mult)
                v.drain()
                v.tensor_scalar(vcol[:, p:p + 1], mcol[:, p:p + 1],
                                mcol[:, p:p + 1], -1.0, ALU.mult, ALU.mult)
                v.drain()
                v.tensor_tensor(vcol[:, p:p + 1], vcol[:, p:p + 1],
                                m2col[:, p:p + 1], ALU.add).then_inc(E.s_dv.h, 1)
                v.wait_ge(E.s_ac2.h, inv[i])
                if i >= 2:
                    v.wait_ge(E.s_pe.h, tpd[i - 2])
                v.tensor_scalar(xn_tm[p][:], xt[p][:], mcol[:, p:p + 1],
                                icol[:, p:p + 1], ALU.add, ALU.mult) \
                    .then_inc(E.s_dv2.h, 1)
            v.wait_ge(E.s_pe.h, tpd[ntt - 1])
            if ntt >= 3:
                DmaStream.wait(v, sttok[ntt - 3])
            v.tensor_copy(stg[(ntt - 1) % 2][:], pst[(ntt - 1) % 2][:]) \
                .then_inc(E.s_g2.h, 1)

        @blk.tensor
        def _(t):
            for i in range(ntt):
                p = i % 2
                t.wait_ge(E.s_dv2.h, xnd[i])
                if i >= 2:
                    t.wait_ge(E.s_g2.h, evd[i - 2])
                for ht in range(NHT):
                    ins = t.transpose(pst[p][:, ht * 128:(ht + 1) * 128],
                                      xn_tm[p][:, ht * 128:(ht + 1) * 128],
                                      E.ident_b[:])
                ins.then_inc(E.s_pe.h, 1)

    _quiesce_stores(nc, E)
    ps_.close()


def _p1_ffn(nc, st, E, fb0_only=False):
    ps_ = ExitStack()
    e = ps_.enter_context
    T = E.T
    TBF = min(512, T)
    nfb = T // TBF
    E.p1_TBF = TBF
    E.p1_nfb = nfb
    ntt_fb = TBF // 128

    h_fm = e(nc.sbuf_tensor("p1_h", [128, NIT_DN * TBF], BF16))
    ssq = e(nc.sbuf_tensor("p1_ssq", [128, ntt_fb], F32))
    ps_g = e(nc.psum_tensor("p1_psg", [128, TBF], F32))
    ps_u = e(nc.psum_tensor("p1_psu", [128, TBF], F32))
    psT = e(nc.psum_tensor("p1_psT", [128, 2048], BF16))
    psD = e(nc.psum_tensor("p1_psD", [128, 1024], F32))

    fbs = range(1) if (fb0_only and nfb > 1) else range(nfb)
    for fb in fbs:
        deq = (fb == 0)
        _p1_gate_up(nc, E, fb, TBF, deq, h_fm, ps_g, ps_u, psT)
        _p1_down(nc, E, fb, TBF, deq, h_fm, psD, psT, ssq)

    _quiesce_stores(nc, E)
    ps_.close()


def _p1_gate_up(nc, E, fb, TBF, deq, h_fm, ps_g, ps_u, psT):
    T = E.T
    with ExitStack() as ps_:
        e = ps_.enter_context
        xnf = e(nc.sbuf_tensor(f"p1_xn{fb}", [128, NHT * TBF], BF16))
        wt = [[e(nc.sbuf_tensor(f"p1_w{fb}_{a}{i}", [128, 2048], BF16))
               for i in range(2)] for a in range(2)]
        sgst = [e(nc.sbuf_tensor(f"p1_sg{fb}_{i}", [128, TBF], BF16)) for i in range(2)]
        if deq:
            pkb = [e(nc.sbuf_tensor(f"p1_pk{fb}_{i}", [128, 256], I32)) for i in range(2)]
            scb = [e(nc.sbuf_tensor(f"p1_sc{fb}_{i}", [128, 16], F16)) for i in range(2)]
            scf = [e(nc.sbuf_tensor(f"p1_scf{fb}_{i}", [128, 16], BF16))
                   for i in range(2)]
            nib = e(nc.sbuf_tensor(f"p1_nib{fb}", [128, 2048], U16))
            tq1 = e(nc.sbuf_tensor(f"p1_tq1{fb}", [128, 2048], U16))
            tq2 = [e(nc.sbuf_tensor(f"p1_tq2{fb}_{i}", [128, 2048], U16))
                   for i in range(2)]
            tq3 = e(nc.sbuf_tensor(f"p1_tq3{fb}", [128, 2048], U16))
            wbf = e(nc.sbuf_tensor(f"p1_wbf{fb}", [128, 2048], BF16))

        SPE, SDV, SAC, SGP = E.s_pe, E.s_dv, E.s_ac, E.s_g2
        ntile = 2 * NPAIR_GU
        ots = []
        for j in range(NPAIR_GU):
            ots.append(j)
            ots.append(j + NPAIR_GU)

        def wslot(i):
            return wt[i % 2][(i // 2) % 2]

        sch = [{} for _ in range(ntile)]
        for i in range(ntile):
            d = sch[i]
            if deq:
                d["gp_nib"] = SDV.bump(8)
                d["dv_pre"] = SDV.bump(1)
                d["g_deq"] = SGP.bump(1)
                if i >= 2 and (i - 2) % 2 == 1:
                    sch[i - 2]["dv_h"] = SDV.bump()
                d["t_tp"] = SPE.bump()
                d["a_ev"] = SAC.bump()
            d["t_mm"] = SPE.bump()
            if i % 2 == 1:
                d["a_silu"] = SAC.bump()
                if not deq:
                    d["dv_h"] = SDV.bump()
        if deq:
            sch[ntile - 1]["dv_h"] = SDV.bump()

        xtok = [None]
        ptok, sctok, wtok, wstok = [], [], [], []

        with nc.Block() as blk:
            @blk.sync
            def _(sy):
                xtok[0] = E.dx.issue(
                    sy, xnf[:].rearrange("p (a t) -> p a t", a=NHT),
                    E.xn_sc[:, :, fb * TBF:(fb + 1) * TBF].transpose([1, 0, 2]))
                for i, ot in enumerate(ots):
                    if deq:
                        if i >= 2:
                            sy.wait_ge(SDV.h, sch[i - 2]["gp_nib"])
                            sy.wait_ge(SDV.h, sch[i - 2]["dv_pre"])
                        ptok.append(E.dx2.issue(sy, pkb[i % 2][:],
                                                E.gup_in[ot * 128:(ot + 1) * 128, :]))
                        sctok.append(E.dcn.issue(sy, scb[i % 2][:],
                                                 E.gus_in[ot * 128:(ot + 1) * 128, :]))
                    else:
                        if fb == 1 and i == 0:
                            for tk in E.p1_gu_wstok[-3:]:
                                DmaStream.wait(sy, tk)
                        if i >= 4:
                            sy.wait_ge(SPE.h, sch[i - 4]["t_mm"])
                        wtok.append(E.dx2.issue(sy, wslot(i)[:], E.wgu_sc[ot]))

            @blk.gpsimd
            def _(g):
                if not deq:
                    return
                for i, ot in enumerate(ots):
                    g.wait_ge(SDV.h, sch[i]["dv_pre"])
                    if i >= 1:
                        g.wait_ge(SPE.h, sch[i - 1]["t_tp"])
                    scv = scf[i % 2][:].unsqueeze(2) \
                        .to_broadcast([128, 16, 128])
                    g.tensor_tensor(
                        wbf[:].rearrange("p (a i) -> p a i", a=16),
                        tq2[i % 2][:].bitcast(BF16).rearrange(
                            "p (a i) -> p a i", a=16),
                        scv, ALU.mult).then_inc(SGP.h, 1)
                    g.drain()

            @blk.vector
            def _(v):
                for i, ot in enumerate(ots):
                    if deq:
                        DmaStream.wait(v, ptok[i])
                        pku = pkb[i % 2][:].bitcast(U16)
                        for k in range(8):
                            v.tensor_scalar(nib[:, k::8],
                                            pku[:, (1 if k >= 4 else 0)::2],
                                            4 * (k % 4), 15,
                                            ALU.logical_shift_right,
                                            ALU.bitwise_and).then_inc(SDV.h, 1)
                        v.drain()
                        if i >= 2:
                            v.wait_ge(SGP.h, sch[i - 2]["g_deq"])
                        v.tensor_scalar(tq3[:], nib[:], 12, 0x8000,
                                        ALU.logical_shift_left, ALU.bitwise_and)
                        v.tensor_scalar(tq1[:], nib[:], 7, None,
                                        ALU.bitwise_and)
                        v.drain()
                        v.tensor_scalar(tq2[i % 2][:], tq1[:], 252, None,
                                        ALU.add)
                        v.drain()
                        v.scalar_tensor_tensor(tq1[:], tq1[:], 252.0,
                                               tq2[i % 2][:],
                                               ALU.mult, ALU.min)
                        v.drain()
                        v.scalar_tensor_tensor(tq2[i % 2][:], tq1[:], 64.0,
                                               tq3[:], ALU.mult, ALU.add)
                        v.drain()
                        DmaStream.wait(v, sctok[i])
                        v.tensor_copy(scf[i % 2][:], scb[i % 2][:]) \
                            .then_inc(SDV.h, 1)
                    ih = i - 2 if deq else i
                    if ih >= 0 and ih % 2 == 1:
                        j = ih // 2
                        v.wait_ge(SAC.h, sch[ih]["a_silu"])
                        v.wait_ge(SPE.h, sch[ih]["t_mm"])
                        v.tensor_tensor(h_fm[:, j * TBF:(j + 1) * TBF],
                                        sgst[j % 2][:], ps_u[:], ALU.mult) \
                            .then_inc(SDV.h, 1)
                    if deq and i == len(ots) - 1:
                        j = i // 2
                        v.wait_ge(SAC.h, sch[i]["a_silu"])
                        v.wait_ge(SPE.h, sch[i]["t_mm"])
                        v.tensor_tensor(h_fm[:, j * TBF:(j + 1) * TBF],
                                        sgst[j % 2][:], ps_u[:], ALU.mult) \
                            .then_inc(SDV.h, 1)

            @blk.scalar
            def _(s):
                for i, ot in enumerate(ots):
                    if deq:
                        s.wait_ge(SPE.h, sch[i]["t_tp"])
                        if i >= 4:
                            s.wait_ge(SPE.h, sch[i - 4]["t_mm"])
                        s.activation(wslot(i)[:], psT[:], ACT.Copy) \
                            .then_inc(SAC.h, 1)
                        s.drain()
                        wstok.append(E.dst.issue(s, E.wgu_sc[ot], wslot(i)[:]))
                    if i % 2 == 1:
                        j = i // 2
                        s.wait_ge(SPE.h, sch[i - 1]["t_mm"])
                        if j >= 2:
                            s.wait_ge(SDV.h, sch[2 * (j - 2) + 1]["dv_h"])
                        s.activation(sgst[j % 2][:], ps_g[:], ACT.Silu) \
                            .then_inc(SAC.h, 1)

            @blk.tensor
            def _(t):
                DmaStream.wait(t, xtok[0])
                for i, ot in enumerate(ots):
                    if deq:
                        t.wait_ge(SGP.h, sch[i]["g_deq"])
                        if i >= 1:
                            t.wait_ge(SAC.h, sch[i - 1]["a_ev"])
                        for ht in range(NHT):
                            ins = t.transpose(psT[:, ht * 128:(ht + 1) * 128],
                                              wbf[:, ht * 128:(ht + 1) * 128],
                                              E.ident_b[:])
                        ins.then_inc(SPE.h, 1)
                        t.wait_ge(SAC.h, sch[i]["a_ev"])
                    else:
                        DmaStream.wait(t, wtok[i])
                    if i % 2 == 0 and i >= 2:
                        t.wait_ge(SAC.h, sch[i - 1]["a_silu"])
                    if i % 2 == 1 and i >= 3:
                        t.wait_ge(SDV.h, sch[i - 2]["dv_h"])
                    psu = ps_g if (i % 2 == 0) else ps_u
                    for ht in range(NHT):
                        ins = t.matmul(psu[:],
                                       wslot(i)[:, ht * 128:(ht + 1) * 128],
                                       xnf[:, ht * TBF:(ht + 1) * TBF],
                                       start=(ht == 0), stop=(ht == NHT - 1))
                    ins.then_inc(SPE.h, 1)

        if deq:
            E.p1_gu_wstok = wstok
        _quiesce_stores(nc, E)


def _p1_down(nc, E, fb, TBF, deq, h_fm, psD, psT, ssq):
    T = E.T
    ntt_fb = TBF // 128
    with ExitStack() as ps_:
        e = ps_.enter_context
        wdn = e(nc.sbuf_tensor(f"p1_wdn{fb}", [128, NIT_DN * 512], BF16))
        sqb = e(nc.sbuf_tensor(f"p1_sqb{fb}", [128, 512], F32))
        sscol = e(nc.sbuf_tensor(f"p1_sscol{fb}", [128, 1], F32))
        fst = [e(nc.sbuf_tensor(f"p1_fs{fb}_{i}", [128, 512], BF16)) for i in range(4)]
        if deq:
            pkb = [e(nc.sbuf_tensor(f"p1_dpk{fb}_{i}", [128, 704], I32)) for i in range(2)]
            scb = [e(nc.sbuf_tensor(f"p1_dsc{fb}_{i}", [128, 44], F16)) for i in range(2)]
            scf = [e(nc.sbuf_tensor(f"p1_dscf{fb}_{i}", [128, 44], BF16))
                   for i in range(2)]
            nib = e(nc.sbuf_tensor(f"p1_dnib{fb}", [128, 5632], U16))
            tq1 = e(nc.sbuf_tensor(f"p1_dtq1{fb}", [128, 5632], U16))
            tq2 = [e(nc.sbuf_tensor(f"p1_dtq2{fb}_{i}", [128, 5632], U16))
                   for i in range(2)]
            tq3 = e(nc.sbuf_tensor(f"p1_dtq3{fb}", [128, 5632], U16))
            wbf = e(nc.sbuf_tensor(f"p1_dwbf{fb}", [128, 5632], BF16))

        SPE, SDV, SAC, SGP = E.s_pe, E.s_dv, E.s_ac, E.s_g2
        if deq:
            E.p1_dn_wstok = {}
        ftoks_all = E.dst2.plan(4 * ntt_fb)

        sch_all = []
        mm_all, sq_all, ss_all, ev_all = [], [], [], []
        for oq in range(4):
            if deq:
                for i in range(4):
                    d = {}
                    d["gp_nib"] = SDV.bump(8)
                    d["dv_pre"] = SDV.bump(1)
                    d["g_deq"] = SGP.bump(1)
                    d["t_tp"] = SPE.bump(3)
                    d["a_ev"] = SAC.bump(3)
                    sch_all.append(d)
            mm_all.append([E.s_pe.bump() for _ in range(ntt_fb)])
            sq_all.append([E.s_ac.bump() for _ in range(ntt_fb)])
            ss_d, ev_d = [], []
            for _tt in range(ntt_fb):
                ss_d.append(E.s_dv.bump(2))
                ev_d.append(E.s_dv.bump())
            ss_all.append(ss_d)
            ev_all.append(ev_d)
            if oq == 3:
                inv_d = E.s_ac.bump()
        ptok, sctok, wltok, ftok = [], [], [], []

        with nc.Block() as blk:
            @blk.sync
            def _(sy):
                for oq in range(4):
                    if deq:
                        for i in range(4):
                            gi = oq * 4 + i
                            if gi >= 2:
                                sy.wait_ge(SDV.h, sch_all[gi - 2]["gp_nib"])
                                sy.wait_ge(SDV.h, sch_all[gi - 2]["dv_pre"])
                            ptok.append(E.dx2.issue(
                                sy, pkb[gi % 2][:],
                                E.dnp_in[gi * 128:(gi + 1) * 128, :]))
                            sctok.append(E.dcn.issue(
                                sy, scb[gi % 2][:],
                                E.dns_in[gi * 128:(gi + 1) * 128, :]))
                    else:
                        if fb == 1:
                            for _tk in E.p1_dn_wstok[oq]:
                                DmaStream.wait(sy, _tk)
                        if oq >= 1:
                            sy.wait_ge(SPE.h, mm_all[oq - 1][ntt_fb - 1])
                        wltok.append(E.dx2.issue(
                            sy, wdn[:].rearrange("p (a i) -> p a i", a=NIT_DN),
                            E.wdn_sc[oq]))

            @blk.gpsimd
            def _(g):
                def fst_stores(oq):
                    for tt in range(ntt_fb):
                        g.wait_ge(SDV.h, ev_all[oq][tt])
                        ttg = fb * ntt_fb + tt
                        tk = E.dst2.issue(
                            g, E.ffn_sc[ttg, :, oq * 512:(oq + 1) * 512],
                            fst[tt][:])
                        assert tk == ftoks_all[oq * ntt_fb + tt]
                        ftok.append(tk)

                for oq in range(4):
                    if deq:
                        for i in range(4):
                            gi = oq * 4 + i
                            d = sch_all[gi]
                            g.wait_ge(SDV.h, d["dv_pre"])
                            if gi >= 1:
                                g.wait_ge(SPE.h, sch_all[gi - 1]["t_tp"])
                            scv = scf[gi % 2][:].unsqueeze(2) \
                                .to_broadcast([128, 44, 128])
                            g.tensor_tensor(
                                wbf[:].rearrange("p (a i) -> p a i", a=44),
                                tq2[gi % 2][:].bitcast(BF16).rearrange(
                                    "p (a i) -> p a i", a=44),
                                scv, ALU.mult).then_inc(SGP.h, 1)
                            g.drain()
                        if oq >= 1:
                            fst_stores(oq - 1)
                    else:
                        fst_stores(oq)
                if deq:
                    fst_stores(3)

            @blk.vector
            def _(v):
                for oq in range(4):
                    if deq:
                        for i in range(4):
                            gi = oq * 4 + i
                            DmaStream.wait(v, ptok[gi])
                            pku = pkb[gi % 2][:].bitcast(U16)
                            for k in range(8):
                                v.tensor_scalar(nib[:, k::8],
                                                pku[:, (1 if k >= 4 else 0)::2],
                                                4 * (k % 4), 15,
                                                ALU.logical_shift_right,
                                                ALU.bitwise_and) \
                                    .then_inc(SDV.h, 1)
                            v.drain()
                            if gi >= 2:
                                v.wait_ge(SGP.h, sch_all[gi - 2]["g_deq"])
                            v.tensor_scalar(tq3[:], nib[:], 12, 0x8000,
                                            ALU.logical_shift_left,
                                            ALU.bitwise_and)
                            v.tensor_scalar(tq1[:], nib[:], 7, None,
                                            ALU.bitwise_and)
                            v.drain()
                            v.tensor_scalar(tq2[gi % 2][:], tq1[:], 252, None,
                                            ALU.add)
                            v.drain()
                            v.scalar_tensor_tensor(tq1[:], tq1[:], 252.0,
                                                   tq2[gi % 2][:],
                                                   ALU.mult, ALU.min)
                            v.drain()
                            v.scalar_tensor_tensor(tq2[gi % 2][:], tq1[:],
                                                   64.0, tq3[:],
                                                   ALU.mult, ALU.add)
                            v.drain()
                            DmaStream.wait(v, sctok[gi])
                            v.tensor_copy(scf[gi % 2][:], scb[gi % 2][:]) \
                                .then_inc(SDV.h, 1)
                    for tt in range(ntt_fb):
                        v.wait_ge(SAC.h, sq_all[oq][tt])
                        v.reduce_sum(sscol[:], sqb[:], axis=AX.X) \
                            .then_inc(SDV.h, 1)
                        v.drain()
                        if oq == 0:
                            v.tensor_copy(ssq[:, tt:tt + 1], sscol[:]) \
                                .then_inc(SDV.h, 1)
                        else:
                            v.tensor_tensor(ssq[:, tt:tt + 1], ssq[:, tt:tt + 1],
                                            sscol[:], ALU.add).then_inc(SDV.h, 1)
                        v.drain()
                        v.wait_ge(SPE.h, mm_all[oq][tt])
                        gidx = oq * ntt_fb + tt
                        if gidx >= 4:
                            DmaStream.wait(v, ftoks_all[gidx - 4])
                        v.tensor_copy(fst[tt][:],
                                      psD[:, (tt % 2) * 512:(tt % 2) * 512 + 512]) \
                            .then_inc(SDV.h, 1)

            @blk.scalar
            def _(s):
                for oq in range(4):
                    if deq:
                        wstore = []
                        for i in range(4):
                            gi = oq * 4 + i
                            base = sch_all[gi]["t_tp"] - 3
                            for w in range(3):
                                nt = 16 if w < 2 else 12
                                if oq >= 1 and i == 0:
                                    DmaStream.wait(
                                        s, E.p1_dn_wstok[oq - 1][w])
                                    if w == 0:
                                        s.wait_ge(SPE.h,
                                                  mm_all[oq - 1][ntt_fb - 1])
                                s.wait_ge(SPE.h, base + w + 1)
                                s.activation(
                                    wdn[:].rearrange(
                                        "p (a i) -> p a i", a=NIT_DN)
                                    [:, w * 16:w * 16 + nt, i * 128:(i + 1) * 128],
                                    psT[:].rearrange("p (a i) -> p a i", a=16)
                                    [:, 0:nt, :],
                                    ACT.Copy).then_inc(SAC.h, 1)
                                if i == 3:
                                    s.drain()
                                    wstore.append(E.dst.issue(
                                        s,
                                        E.wdn_sc[oq][:, w * 16:w * 16 + nt],
                                        wdn[:].rearrange(
                                            "p (a i) -> p a i", a=NIT_DN)
                                        [:, w * 16:w * 16 + nt, :]))
                        E.p1_dn_wstok[oq] = wstore
                    for tt in range(ntt_fb):
                        s.wait_ge(SPE.h, mm_all[oq][tt])
                        if tt >= 1:
                            s.wait_ge(SDV.h, ss_all[oq][tt - 1])
                        elif oq >= 1:
                            s.wait_ge(SDV.h, ss_all[oq - 1][ntt_fb - 1])
                        s.activation(sqb[:],
                                     psD[:, (tt % 2) * 512:(tt % 2) * 512 + 512],
                                     ACT.Square).then_inc(SAC.h, 1)
                    if oq == 3:
                        for tt in range(ntt_fb):
                            s.wait_ge(SDV.h, ss_all[oq][tt])
                        ttg0 = fb * ntt_fb
                        s.activation(E.inv_ff[:, ttg0:ttg0 + ntt_fb],
                                     ssq[:, 0:ntt_fb], ACT.Ln,
                                     bias=E.c_epsff[:], scale=1.0 / H)
                        s.drain()
                        s.activation(E.inv_ff[:, ttg0:ttg0 + ntt_fb],
                                     E.inv_ff[:, ttg0:ttg0 + ntt_fb],
                                     ACT.Exp, scale=-0.5).then_inc(SAC.h, 1)

            @blk.tensor
            def _(t):
                for oq in range(4):
                    if deq:
                        for i in range(4):
                            gi = oq * 4 + i
                            t.wait_ge(SGP.h, sch_all[gi]["g_deq"])
                            base_ac = sch_all[gi]["a_ev"] - 3
                            for w in range(3):
                                nt = 16 if w < 2 else 12
                                if gi == 0 and w == 0:
                                    pass
                                elif w == 0:
                                    t.wait_ge(SAC.h,
                                              sch_all[gi - 1]["a_ev"])
                                else:
                                    t.wait_ge(SAC.h, base_ac + w)
                                for k in range(nt):
                                    it = w * 16 + k
                                    ins = t.transpose(
                                        psT[:, k * 128:(k + 1) * 128],
                                        wbf[:, it * 128:(it + 1) * 128],
                                        E.ident_b[:])
                                ins.then_inc(SPE.h, 1)
                        t.wait_ge(SAC.h, sch_all[oq * 4 + 3]["a_ev"])
                    else:
                        DmaStream.wait(t, wltok[oq])
                    for tt in range(ntt_fb):
                        if tt >= 2:
                            t.wait_ge(SDV.h, ev_all[oq][tt - 2])
                            t.wait_ge(SAC.h, sq_all[oq][tt - 2])
                        elif oq >= 1:
                            t.wait_ge(SDV.h, ev_all[oq - 1][tt + 2])
                            t.wait_ge(SAC.h, sq_all[oq - 1][tt + 2])
                        for it in range(NIT_DN):
                            ins = t.matmul(
                                psD[:, (tt % 2) * 512:(tt % 2) * 512 + 512],
                                h_fm[:, it * TBF + tt * 128:
                                     it * TBF + (tt + 1) * 128],
                                wdn[:, it * 512:(it + 1) * 512],
                                start=(it == 0), stop=(it == NIT_DN - 1))
                        ins.then_inc(SPE.h, 1)

        E.p1_last_toks = ftok
        _quiesce_stores(nc, E)


def _p2_inproj(nc, st, E):
    ps_ = ExitStack()
    e = ps_.enter_context
    T, nck = E.T, E.nck
    TB = min(512, T)
    ntb = T // TB

    xn_sb = e(nc.sbuf_tensor("p2_xn", [128, NHT * T], BF16))
    wdt_sb = e(nc.sbuf_tensor("p2_wdt", [128, NHT * 64], BF16))
    loop_st = ExitStack()
    e2 = loop_st.enter_context
    wtb = [e2(nc.sbuf_tensor(f"p2_wt{i}", [128, NHT * 128], BF16)) for i in range(2)]
    gst = [e2(nc.sbuf_tensor(f"p2_gs{i}", [128, T], BF16)) for i in range(2)]
    cvs = [e2(nc.sbuf_tensor(f"p2_cv{i}", [128, T + 3], BF16)) for i in range(2)]
    hst = [e2(nc.sbuf_tensor(f"p2_hs{i}", [128, T], BF16)) for i in range(2)]
    diag = [e2(nc.sbuf_tensor(f"p2_dg{i}", [128, 4 * 128], BF16)) for i in range(2)]
    ps_ip = e2(nc.psum_tensor("p2_psip", [128, T if T <= 2048 else 2048], F32))
    ps_cv = e2(nc.psum_tensor("p2_pscv", [128, T if T <= 2048 else 2048], F32))

    NFT = 66

    mmd = [E.s_pe.bump() for _ in range(NFT)]
    evg = {}
    for ft in range(NFT):
        if ft < 32:
            evg[ft] = ("ac", E.s_ac.bump())
        else:
            evg[ft] = ("dv", E.s_dv.bump())
    cvd = {ft: E.s_pe2.bump() for ft in range(32, NFT)}
    sld = {ft: E.s_ac2.bump() for ft in range(32, NFT)}
    dgd = {ft: E.s_dv2.bump(4) for ft in range(32, NFT)}
    wtok = []
    stok = []

    with nc.Block() as blk:
        @blk.sync
        def _(sy):
            E.p2_xntok = E.dx.issue(
                sy, xn_sb[:].rearrange("p (a t) -> p a t", a=NHT), E.xn_sc[:, :, :].transpose([1, 0, 2]))

        @blk.gpsimd
        def _(g):
            E.p2_wdtok = E.dgp.issue(
                g, wdt_sb[:].rearrange("p (a f) -> p a f", a=NHT),
                E.winT[:, PROJ - 64:PROJ].rearrange("(a p) f -> p a f", p=128))
            for ft in range(NFT):
                if ft >= 2:
                    g.wait_ge(E.s_pe.h, mmd[ft - 2])
                wtok.append(E.dgp.issue(
                    g, wtb[ft % 2][:].rearrange("p (a f) -> p a f", a=NHT),
                    E.winT[:, ft * 128:(ft + 1) * 128].rearrange(
                        "(a p) f -> p a f", p=128)))

        @blk.vector
        def _(v):
            v.memset(cvs[0][:, 0:3], 0.0)
            v.memset(cvs[1][:, 0:3], 0.0)
            for ft in range(32, NFT):
                p = ft % 2
                if ft >= 34:
                    v.wait_ge(E.s_pe2.h, cvd[ft - 2])
                for k in range(4):
                    v.tensor_scalar(diag[p][:, k * 128:(k + 1) * 128],
                                    E.ident_b[:],
                                    E.convw_sb[:, (ft - 32) * 4 + k:(ft - 32) * 4 + k + 1],
                                    None, ALU.mult).then_inc(E.s_dv2.h, 1)
                v.wait_ge(E.s_pe.h, mmd[ft])
                if ft >= 34:
                    v.wait_ge(E.s_ac2.h, sld[ft - 2])
                v.tensor_copy(cvs[p][:, 3:T + 3], ps_ip[:]).then_inc(E.s_dv.h, 1)

        @blk.tensor
        def _(t):
            DmaStream.wait(t, E.p2_xntok)
            for ft in range(NFT):
                p = ft % 2
                DmaStream.wait(t, wtok[ft])
                if ft >= 1:
                    kind, val = evg[ft - 1]
                    t.wait_ge(getattr(E, "s_" + kind).h, val)
                for tb in range(ntb):
                    for ht in range(NHT):
                        ins = t.matmul(
                            ps_ip[:, tb * TB:(tb + 1) * TB],
                            wtb[p][:, ht * 128:(ht + 1) * 128],
                            xn_sb[:, ht * T + tb * TB: ht * T + (tb + 1) * TB],
                            start=(ht == 0), stop=(ht == NHT - 1))
                ins.then_inc(E.s_pe.h, 1)
                if ft >= 32:
                    t.wait_ge(E.s_dv.h, evg[ft][1])
                    t.wait_ge(E.s_dv2.h, dgd[ft])
                    if ft >= 33:
                        t.wait_ge(E.s_ac2.h, sld[ft - 1])
                    for tb in range(ntb):
                        for k in range(4):
                            ins = t.matmul(
                                ps_cv[:, tb * TB:(tb + 1) * TB],
                                diag[p][:, k * 128:(k + 1) * 128],
                                cvs[p][:, tb * TB + k: tb * TB + k + TB],
                                start=(k == 0), stop=(k == 3))
                    ins.then_inc(E.s_pe2.h, 1)

        @blk.scalar
        def _(s):
            for ft in range(NFT):
                p = ft % 2
                if ft < 32:
                    s.wait_ge(E.s_pe.h, mmd[ft])
                    if ft >= 2:
                        DmaStream.wait(s, stok[ft - 2])
                    s.activation(gst[p][:], ps_ip[:], ACT.Silu) \
                        .then_inc(E.s_ac.h, 1)
                    s.drain()
                    stok.append(E.dst.issue(s, E.sg_sc[ft, :, :], gst[p][:]))
                else:
                    s.wait_ge(E.s_pe2.h, cvd[ft])
                    if ft >= 34:
                        DmaStream.wait(s, stok[ft - 2])
                    s.activation(hst[p][:], ps_cv[:], ACT.Silu,
                                 bias=E.convb_sb[:, ft - 32:ft - 31]) \
                        .then_inc(E.s_ac2.h, 1)
                    s.drain()
                    if ft < 64:
                        dst_ap = E.hs_sc[ft - 32, :, :]
                    elif ft == 64:
                        dst_ap = E.b_sc[:, :]
                    else:
                        dst_ap = E.c_sc[:, :]
                    stok.append(E.dst.issue(s, dst_ap, hst[p][:]))

    _quiesce_stores(nc, E)
    loop_st.close()
    dtw = e(nc.sbuf_tensor("p2_dtw", [128, 64], F32))
    ab3 = e(nc.sbuf_tensor("p2_ab", [128, 64], F32))
    rl = e(nc.sbuf_tensor("p2_rl", [128, 64], F32))
    atw = e(nc.sbuf_tensor("p2_at", [128, 64], F32))
    stw = e(nc.sbuf_tensor("p2_stw", [128, 64], F32))
    ps_dt = e(nc.psum_tensor("p2_psdt", [128, 64], F32))
    ps_sf = e(nc.psum_tensor("p2_pssf", [64, 128], F32))

    with nc.Block() as blk:
        dt_mm = [E.s_pe.bump() for _ in range(nck)]
        dt_z = [E.s_dv.bump() for _ in range(nck)]
        dt_sp = [E.s_ac.bump() for _ in range(nck)]
        dt_fin = [E.s_dv2.bump() for _ in range(nck)]
        s_mm = [E.s_pe2.bump() for _ in range(nck)]
        s_ev = [E.s_dv3.bump() for _ in range(nck)]
        sf_tp = [E.s_pe3.bump() for _ in range(nck)]
        sf_ev = [E.s_g2.bump() for _ in range(nck)]

        @blk.tensor
        def _(t):
            DmaStream.wait(t, E.p2_wdtok)
            for ck in range(nck):
                if ck >= 1:
                    t.wait_ge(E.s_dv3.h, s_ev[ck - 1])
                for ht in range(NHT):
                    ins = t.matmul(ps_dt[:],
                                   xn_sb[:, ht * T + ck * 128: ht * T + (ck + 1) * 128],
                                   wdt_sb[:, ht * 64:(ht + 1) * 64],
                                   start=(ht == 0), stop=(ht == NHT - 1))
                ins.then_inc(E.s_pe.h, 1)
                t.wait_ge(E.s_dv2.h, dt_fin[ck])
                ins = t.matmul(ps_dt[:], E.triu_f[:], atw[:], start=True,
                               stop=True)
                ins.then_inc(E.s_pe2.h, 1)
                t.wait_ge(E.s_dv3.h, s_ev[ck])
                if ck >= 1:
                    t.wait_ge(E.s_g2.h, sf_ev[ck - 1])
                t.transpose(ps_sf[:], stw[:], E.ident_f[:]).then_inc(E.s_pe3.h, 1)

        @blk.vector
        def _(v):
            for ck in range(nck):
                v.wait_ge(E.s_pe.h, dt_mm[ck])
                if ck >= 1:
                    v.wait_ge(E.s_ac.h, dt_sp[ck - 1])
                v.tensor_tensor(dtw[:], ps_dt[:], E.dtb_rep[:], ALU.add) \
                    .then_inc(E.s_dv.h, 1)
                v.wait_ge(E.s_ac.h, dt_sp[ck])
                v.drain()
                v.tensor_relu(rl[:], dtw[:])
                v.drain()
                v.tensor_tensor(E.dt_all[:], rl[:], ab3[:],
                                ALU.add)
                v.drain()
                v.tensor_copy(E.dtbf_all[:, ck * 64:(ck + 1) * 64],
                              E.dt_all[:])
                v.tensor_tensor(atw[:], E.dt_all[:],
                                E.nega_rep[:], ALU.mult).then_inc(E.s_dv2.h, 1)
                v.wait_ge(E.s_pe2.h, s_mm[ck])
                v.tensor_copy(stw[:], ps_dt[:]).then_inc(E.s_dv3.h, 1)
                v.wait_ge(E.s_pe3.h, sf_tp[ck])
                v.tensor_copy(E.sfm_all[:, ck * 128:(ck + 1) * 128], ps_sf[:]) \
                    .then_inc(E.s_g2.h, 1)

        @blk.scalar
        def _(s):
            for ck in range(nck):
                s.wait_ge(E.s_dv.h, dt_z[ck])
                if ck >= 1:
                    s.wait_ge(E.s_dv2.h, dt_fin[ck - 1])
                s.activation(ab3[:], dtw[:], ACT.Abs)
                s.drain()
                s.activation(ab3[:], ab3[:], ACT.Exp, scale=-1.0)
                s.drain()
                s.activation(ab3[:], ab3[:], ACT.Ln, bias=E.c_one[:]) \
                    .then_inc(E.s_ac.h, 1)

    _quiesce_stores(nc, E)
    ps_.close()


def _p3_scan(nc, st, E, with_p1=False):
    ps_ = ExitStack()
    e = ps_.enter_context
    T, nck = E.T, E.nck
    NB = 8

    hsb = [e(nc.sbuf_tensor(f"p3_hs{i}", [128, 4096], BF16)) for i in range(2)]
    sgb = [e(nc.sbuf_tensor(f"p3_sg{i}", [128, 4096], BF16)) for i in range(2)]
    bcb = [e(nc.sbuf_tensor(f"p3_bc{i}", [128, 256], BF16)) for i in range(2)]
    hs_tm = e(nc.sbuf_tensor("p3_hstm", [128, 4096], BF16))
    b_tm = e(nc.sbuf_tensor("p3_btm", [128, 128], BF16))
    cbtm = e(nc.sbuf_tensor("p3_cbtm", [128, 128], BF16))
    sneg = e(nc.sbuf_tensor("p3_sneg", [64, 128], F32))
    esr = e(nc.sbuf_tensor("p3_esr", [128, 1024], BF16))
    esl = e(nc.sbuf_tensor("p3_esl", [128, NH], F32))
    dexp = e(nc.sbuf_tensor("p3_dexp", [128, 1024], BF16))
    cbdt = e(nc.sbuf_tensor("p3_cbdt", [128, 1024], BF16))
    scq = [e(nc.sbuf_tensor(f"p3_scq{i}", [128, 1024], BF16)) for i in range(2)]
    cexq = [e(nc.sbuf_tensor(f"p3_cexq{i}", [128, 1024], BF16)) for i in range(2)]
    bdq = [e(nc.sbuf_tensor(f"p3_bdq{i}", [128, 1024], BF16)) for i in range(2)]
    wex = e(nc.sbuf_tensor("p3_wex", [128, 8], BF16))
    ssel = [e(nc.sbuf_tensor(f"p3_ssel{i}", [64, 1024], F32)) for i in range(2)]
    wcol = e(nc.sbuf_tensor("p3_wcol", [128, 8], BF16))
    s_all = e(nc.sbuf_tensor("p3_sall", [128, 4096], F32))
    s_bfc = e(nc.sbuf_tensor("p3_sbf", [128, 4096], BF16))
    yfm = e(nc.sbuf_tensor("p3_yfm", [128, 4096], BF16))
    psA = e(nc.psum_tensor("p3_psA", [128, 1024], F32))
    psB = e(nc.psum_tensor("p3_psB", [64, 1024], F32))
    psC = e(nc.psum_tensor("p3_psC", [128, 512], F32))

    htok, sgtok, bctok = [], [], []
    sttok = E.dst3.plan(E.nck)

    SPE, SDV, SAC, SGP = E.s_pe, E.s_dv, E.s_ac, E.s_g2
    sch = []
    for ck in range(nck):
        d = {}
        d["t_tp0"] = SPE.bump()
        d["v_etp0"] = SDV.bump()
        d["t_tp1"] = SPE.bump()
        d["v_sbf"] = SDV.bump()
        d["v_etp1"] = SDV.bump()
        d["t_cbt"] = SPE.bump()
        d["v_cbt"] = SDV.bump()
        d["v_sneg"] = SDV.bump()
        for s in range(NB + 1):
            if s < NB:
                d[f"t1_{s}"] = SPE.bump()
                d[f"t2_{s}"] = SPE.bump()
            if s >= 1:
                d[f"t5_{s - 1}"] = SPE.bump()
        for q in range(NB):
            d[f"a1_{q}"] = SAC.bump(2)
            d[f"a2_{q}"] = SAC.bump(2)
            if q >= 1:
                d[f"a3_{q - 1}"] = SAC.bump(2)
        d[f"a3_{NB - 1}"] = SAC.bump(2)
        for s in range(NB + 1):
            if s < NB:
                d[f"g0_{s}"] = SGP.bump()
            if s >= 1:
                d[f"g1_{s - 1}"] = SGP.bump()
        for q in range(NB):
            if q >= 1:
                d[f"v2_{q - 1}"] = SDV.bump(2)
            d[f"v1_{q}"] = SDV.bump(4)
        d[f"v2_{NB - 1}"] = SDV.bump(2)
        d["v_y2"] = SDV.bump(5)
        d["t_rms"] = SPE.bump()
        d["a_inv"] = SAC.bump()
        sch.append(d)

    def pget(dd, key, q, ck):
        if q >= 0:
            return sch[ck][f"{key}_{q}"]
        if ck >= 1:
            return sch[ck - 1][f"{key}_{q + NB}"]
        return None

    p1w = _P1Work(nc, e, E, nck) if (with_p1 and E.p1_nfb > 1) else None

    with nc.Block() as blk:
        @blk.sync
        def _(sy):
            for ck in range(nck):
                if ck >= 2:
                    sy.wait_ge(SPE.h, sch[ck - 2]["t_rms"])
                    sy.wait_ge(SDV.h, sch[ck - 2]["v_y2"])
                    sy.wait_ge(SAC.h, sch[ck - 2]["a_inv"])
                htok.append(E.dx.issue(
                    sy, hsb[ck % 2][:].rearrange("p (a i) -> p a i", a=32),
                    E.hs_sc[:, :, ck * 128:(ck + 1) * 128].transpose([1, 0, 2])))
                sgtok.append(E.dx2.issue(
                    sy, sgb[ck % 2][:].rearrange("p (a i) -> p a i", a=32),
                    E.sg_sc[:, :, ck * 128:(ck + 1) * 128].transpose([1, 0, 2])))
                bctok.append((
                    E.dcn.issue(sy, bcb[ck % 2][:, 0:128],
                                E.b_sc[:, ck * 128:(ck + 1) * 128]),
                    E.dcn.issue(sy, bcb[ck % 2][:, 128:256],
                                E.c_sc[:, ck * 128:(ck + 1) * 128])))
                if p1w:
                    for it in p1w.items_for(ck):
                        p1w.emit_sync(sy, it)

        @blk.gpsimd
        def _(g):
            for ck in range(nck):
                d = sch[ck]
                p = ck % 2
                p1it = p1w.items_for(ck) if p1w else []
                np1 = len(p1it)
                for s in range(NB + 1):
                    if s < NB:
                        q = s
                        hq = q * 8
                        pq = q % 2
                        t2p = pget(d, "t2", q - 2, ck)
                        if t2p is not None:
                            g.wait_ge(SPE.h, t2p)
                        i_v = E.i64[:, hq:hq + 8].unsqueeze(2) \
                            .to_broadcast([64, 8, 128])
                        s_v = E.sfm_all[0:64, ck * 128:(ck + 1) * 128] \
                            .unsqueeze(1).to_broadcast([64, 8, 128])
                        g.tensor_tensor(
                            ssel[pq][:].rearrange("p (a i) -> p a i", a=8),
                            i_v, s_v, ALU.mult).then_inc(SGP.h, 1)
                        g.drain()
                    if s >= 1:
                        q = s - 1
                        hq = q * 8
                        pq = q % 2
                        g.wait_ge(SAC.h, d[f"a1_{q}"])
                        t5p = pget(d, "t5", q - 2, ck)
                        if t5p is not None:
                            g.wait_ge(SPE.h, t5p)
                        c_v = bcb[p][:, 128:256].unsqueeze(1) \
                            .to_broadcast([128, 8, 128])
                        g.tensor_tensor(
                            cexq[pq][:].rearrange("p (a i) -> p a i", a=8),
                            c_v, esr[:].rearrange("p (a i) -> p a i", a=8),
                            ALU.mult).then_inc(SGP.h, 1)
                        g.drain()
                    if p1w and s < NB:
                        for it in p1w.gslice(ck * NB + s - 2):
                            p1w.emit_gp(g, it)
                if p1w and ck == nck - 1:
                    for gg in range(p1w.G - 2, p1w.G):
                        for it in p1w.gslice(gg):
                            p1w.emit_gp(g, it)

        @blk.tensor
        def _(t):
            for ck in range(nck):
                d = sch[ck]
                p = ck % 2
                p1it = p1w.items_for(ck) if p1w else []
                np1 = len(p1it)
                DmaStream.wait(t, htok[ck])
                DmaStream.wait(t, bctok[ck][0])
                DmaStream.wait(t, bctok[ck][1])
                if ck >= 1:
                    t.wait_ge(SAC.h, sch[ck - 1]["a_inv"])
                    t.wait_ge(SDV.h, sch[ck - 1]["v_y2"])
                    t.wait_ge(SAC.h, sch[ck - 1][f"a2_{NB - 1}"])
                    t.wait_ge(SDV.h, sch[ck - 1][f"v1_{NB - 1}"])
                for ft in range(16):
                    ins = t.transpose(psA[:].bitcast(BF16)
                                      [:, ft * 128:(ft + 1) * 128],
                                      hsb[p][:, ft * 128:(ft + 1) * 128],
                                      E.ident_b[:])
                ins.then_inc(SPE.h, 1)
                t.wait_ge(SDV.h, d["v_etp0"])
                for ft in range(16):
                    ins = t.transpose(psA[:].bitcast(BF16)
                                      [:, ft * 128:(ft + 1) * 128],
                                      hsb[p][:, 2048 + ft * 128:
                                             2048 + (ft + 1) * 128],
                                      E.ident_b[:])
                ins.then_inc(SPE.h, 1)
                t.wait_ge(SDV.h, d["v_etp1"])
                ins = t.transpose(psA[:].bitcast(BF16)[:, 0:128],
                                  bcb[p][:, 0:128], E.ident_b[:])
                t.matmul(psA[:, 64:192], bcb[p][:, 0:128],
                         bcb[p][:, 128:256], start=True, stop=True) \
                    .then_inc(SPE.h, 1)
                for s in range(NB + 1):
                    if s < NB:
                        q = s
                        pq = q % 2
                        a2p = pget(d, "a2", q - 1, ck)
                        if a2p is not None:
                            t.wait_ge(SAC.h, a2p)
                        if q == 0:
                            t.wait_ge(SDV.h, d["v_cbt"])
                        t.wait_ge(SGP.h, d[f"g0_{q}"])
                        for bb in range(2):
                            ins = t.matmul(psA[:, bb * 512:(bb + 1) * 512],
                                           E.ones64[:],
                                           ssel[pq][:, bb * 512:(bb + 1) * 512],
                                           start=True, stop=True)
                        ins.then_inc(SPE.h, 1)
                    if s < NB:
                        q = s
                        hq = q * 8
                        pq = q % 2
                        t.wait_ge(SAC.h, d[f"a1_{q}"])
                        if q == 0:
                            t.wait_ge(SDV.h, d["v_sneg"])
                        for bb in range(2):
                            t.matmul(psA[:, bb * 512:(bb + 1) * 512],
                                     E.ones64[:],
                                     ssel[pq][:, bb * 512:(bb + 1) * 512],
                                     start=True, stop=False)
                        for bb in range(2):
                            iv = E.i64[:, hq + bb * 4:hq + bb * 4 + 4] \
                                .unsqueeze(2).to_broadcast([64, 4, 128])
                            t.matmul(psA[:, bb * 512:(bb + 1) * 512],
                                     sneg[:], iv, start=False, stop=False)
                        for bb in range(2):
                            rhm = E.negm_f[:].unsqueeze(1) \
                                .to_broadcast([128, 4, 128])
                            ins = t.matmul(psA[:, bb * 512:(bb + 1) * 512],
                                           E.ident_f[:], rhm,
                                           start=False, stop=True)
                        ins.then_inc(SPE.h, 1)
                    if s >= 1:
                        q = s - 1
                        hq = q * 8
                        pq = q % 2
                        t.wait_ge(SDV.h, d[f"v1_{q}"])
                        t.wait_ge(SGP.h, d[f"g1_{q}"])
                        if q == 0:
                            t.wait_ge(SDV.h, d["v_sbf"])
                        a3p = pget(d, "a3", q - 1, ck)
                        if a3p is not None:
                            t.wait_ge(SAC.h, a3p)
                        v2p = pget(d, "v2", q - 1, ck)
                        if v2p is not None:
                            t.wait_ge(SDV.h, v2p)
                        for hh in range(8):
                            h = hq + hh
                            ysl = psB[0:64, hh * 128:(hh + 1) * 128]
                            t.matmul(ysl, s_bfc[:, h * 64:(h + 1) * 64],
                                     cexq[pq][:, hh * 128:(hh + 1) * 128],
                                     start=(hh in (0, 4)), stop=False)
                            t.matmul(ysl, hs_tm[:, h * 64:(h + 1) * 64],
                                     scq[pq][:, hh * 128:(hh + 1) * 128],
                                     start=False, stop=(hh in (3, 7)))
                            dsl = psC[:, hh * 64:(hh + 1) * 64]
                            ins = t.matmul(dsl,
                                           bdq[pq][:, hh * 128:(hh + 1) * 128],
                                           hs_tm[:, h * 64:(h + 1) * 64],
                                           start=(hh == 0), stop=(hh == 7))
                        ins.then_inc(SPE.h, 1)
                    if p1w and s < NB - 1:
                        for it in p1w.gslice(ck * NB + s):
                            p1w.emit_pe(t, it)
                last_sl = p1w.gslice(ck * NB + NB - 1) if p1w else []
                for it in last_sl[:1]:
                    p1w.emit_pe(t, it)
                t.wait_ge(SDV.h, d["v_y2"])
                for ft in range(32):
                    ins = t.matmul(psA[:, 0:1],
                                   sgb[p][:, ft * 128:(ft + 1) * 128],
                                   E.ones_b[:],
                                   start=(ft == 0), stop=(ft == 31))
                ins.then_inc(SPE.h, 1)
                for it in last_sl[1:]:
                    p1w.emit_pe(t, it)

        @blk.vector
        def _(v):
            for ck in range(nck):
                d = sch[ck]
                p = ck % 2
                p1it = p1w.items_for(ck) if p1w else []
                np1 = len(p1it)
                if ck == 0:
                    v.memset(s_all[:], 0.0)
                    v.drain()
                if p1w:
                    for qq in (0, 1):
                        for it in p1w.gslice(ck * NB + qq - 2):
                            p1w.emit_dv(v, it)
                v.wait_ge(SPE.h, d["t_tp0"])
                v.tensor_copy(hs_tm[:, 0:2048], psA[:].bitcast(BF16)) \
                    .then_inc(SDV.h, 1)
                v.wait_ge(SPE.h, d["t_tp1"])
                v.tensor_copy(s_bfc[:], s_all[:]).then_inc(SDV.h, 1)
                v.tensor_copy(hs_tm[:, 2048:4096], psA[:].bitcast(BF16)) \
                    .then_inc(SDV.h, 1)
                v.wait_ge(SPE.h, d["t_cbt"])
                v.tensor_copy(b_tm[:], psA[:].bitcast(BF16)[:, 0:128])
                v.tensor_tensor(cbtm[:], psA[:, 64:192], E.triu_f[:],
                                ALU.mult).then_inc(SDV.h, 1)
                v.tensor_scalar(sneg[:], E.sfm_all[0:64, ck * 128:(ck + 1) * 128],
                                -1.0, None, ALU.mult).then_inc(SDV.h, 1)
                def emit_ssl(qq):
                    hq2 = qq * 8
                    v.wait_ge(SPE.h, d[f"t5_{qq}"])
                    esl_v = esl[:, hq2:hq2 + 8].unsqueeze(2) \
                        .to_broadcast([128, 8, 64])
                    ssl = s_all[:, hq2 * 64:hq2 * 64 + 512]
                    v.tensor_tensor(ssl.rearrange("p (a i) -> p a i", a=8),
                                    ssl.rearrange("p (a i) -> p a i", a=8),
                                    esl_v, ALU.mult).then_inc(SDV.h, 1)
                    v.drain()
                    v.tensor_tensor(ssl, ssl, psC[:], ALU.add) \
                        .then_inc(SDV.h, 1)
                    v.drain()

                for q in range(NB):
                    if p1w and q >= 2:
                        for it in p1w.gslice(ck * NB + q - 2):
                            p1w.emit_dv(v, it)
                    if q >= 1:
                        emit_ssl(q - 1)
                    hq = q * 8
                    pq = q % 2
                    v.wait_ge(SAC.h, d[f"a2_{q}"])
                    t5p = pget(d, "t5", q - 2, ck)
                    if t5p is not None:
                        v.wait_ge(SPE.h, t5p)
                    dtv_b = E.dtbf_all[:, ck * NH + hq:ck * NH + hq + 8] \
                        .unsqueeze(2).to_broadcast([128, 8, 128])
                    cb_v = cbtm[:].unsqueeze(1).to_broadcast([128, 8, 128])
                    v.tensor_tensor(cbdt[:].rearrange("p (a i) -> p a i", a=8),
                                    cb_v, dtv_b, ALU.mult).then_inc(SDV.h, 1)
                    v.drain()
                    v.tensor_tensor(scq[pq][:], dexp[:], cbdt[:], ALU.mult) \
                        .then_inc(SDV.h, 1)
                    v.tensor_tensor(wcol[:], wex[:],
                                    E.dtbf_all[:, ck * NH + hq:ck * NH + hq + 8],
                                    ALU.mult).then_inc(SDV.h, 1)
                    v.drain()
                    wc_v = wcol[:].unsqueeze(2).to_broadcast([128, 8, 128])
                    bt_v = b_tm[:].unsqueeze(1).to_broadcast([128, 8, 128])
                    v.tensor_tensor(bdq[pq][:].rearrange("p (a i) -> p a i", a=8),
                                    bt_v, wc_v, ALU.mult).then_inc(SDV.h, 1)
                emit_ssl(NB - 1)
                DmaStream.wait(v, sgtok[ck])
                v.wait_ge(SAC.h, d[f"a3_{NB - 1}"])
                df_v = E.dfull[:].unsqueeze(2).to_broadcast([128, 32, 128])
                v.tensor_tensor(hsb[p][:].rearrange("p (a i) -> p a i", a=32),
                                hsb[p][:].rearrange("p (a i) -> p a i", a=32),
                                df_v, ALU.mult).then_inc(SDV.h, 1)
                v.drain()
                v.tensor_tensor(yfm[:], yfm[:], hsb[p][:], ALU.add) \
                    .then_inc(SDV.h, 1)
                v.drain()
                v.tensor_tensor(yfm[:], yfm[:], sgb[p][:], ALU.mult) \
                    .then_inc(SDV.h, 1)
                v.drain()
                v.tensor_tensor(sgb[p][:], yfm[:], yfm[:], ALU.mult) \
                    .then_inc(SDV.h, 1)
                nw_v = E.nw_bf[:].unsqueeze(2).to_broadcast([128, 32, 128])
                v.drain()
                v.tensor_tensor(yfm[:].rearrange("p (a i) -> p a i", a=32),
                                yfm[:].rearrange("p (a i) -> p a i", a=32),
                                nw_v, ALU.mult).then_inc(SDV.h, 1)
                if p1w and ck == nck - 1:
                    for g in range(p1w.G - 2, p1w.G):
                        for it in p1w.gslice(g):
                            p1w.emit_dv(v, it)

        @blk.scalar
        def _(s):
            for ck in range(nck):
                d = sch[ck]
                p1it = p1w.items_for(ck) if p1w else []
                np1 = len(p1it)
                for q in range(NB):
                    if p1w:
                        for it in p1w.gslice(ck * NB + q - 1):
                            p1w.emit_ac(s, it)
                    hq = q * 8
                    pq = q % 2
                    s.wait_ge(SPE.h, d[f"t1_{q}"])
                    g1p = pget(d, "g1", q - 1, ck)
                    if g1p is not None:
                        s.wait_ge(SGP.h, g1p)
                    v1p = pget(d, "v1", q - 1, ck)
                    if v1p is not None:
                        s.wait_ge(SDV.h, v1p)
                    if ck >= 1:
                        s.wait_ge(SDV.h, sch[ck - 1][f"v2_{q}"])
                    s.activation(esr[:], psA[:], ACT.Exp)
                    s.activation(esl[:, hq:hq + 8], psA[:, 127::128],
                                 ACT.Exp).then_inc(SAC.h, 2)
                    s.wait_ge(SPE.h, d[f"t2_{q}"])
                    s.activation(dexp[:], psA[:], ACT.Exp)
                    s.activation(wex[:], psA[:, 127::128], ACT.Exp) \
                        .then_inc(SAC.h, 2)
                    qq = q - 1
                    if qq >= 0:
                        if qq == 0 and ck >= 1:
                            DmaStream.wait(s, sttok[ck - 1])
                        _p3_yevac(s, E, d, yfm, psB, qq, SPE, SAC, None)
                _p3_yevac(s, E, d, yfm, psB, NB - 1, SPE, SAC, None)
                if p1w and ck == nck - 1:
                    for g in range(p1w.G - 1, p1w.G):
                        for it in p1w.gslice(g):
                            p1w.emit_ac(s, it)
                s.wait_ge(SPE.h, d["t_rms"])
                s.activation(E.inv_at[:, ck:ck + 1], psA[:, 0:1], ACT.Ln,
                             bias=E.c_epsat[:], scale=1.0 / INTER)
                s.drain()
                s.activation(E.inv_at[:, ck:ck + 1], E.inv_at[:, ck:ck + 1],
                             ACT.Exp, scale=-0.5).then_inc(SAC.h, 1)
                s.wait_ge(SDV.h, d["v_y2"])
                tk = E.dst3.issue(
                    s, E.ynw_sc[ck].transpose([1, 0, 2]),
                    yfm[:].rearrange("p (a i) -> p a i", a=32))
                assert tk == sttok[ck], (tk, sttok[ck])

    _quiesce_stores(nc, E)
    ps_.close()


class _P1Work:

    def __init__(self, nc, e, E, nck):
        self.nc, self.E = nc, E
        TBF = E.p1_TBF
        self.TBF = TBF
        self.nttf = TBF // 128
        self.h_sc = nc.dram_tensor("p13_h", [NIT_DN, 128, TBF], BF16).ap()
        self.xnf = e(nc.sbuf_tensor("q1_xn", [128, NHT * TBF], BF16))
        self.wtg = [e(nc.sbuf_tensor(f"q1_wg{i}", [128, 1024], BF16))
                    for i in range(2)]
        self.wtu = [e(nc.sbuf_tensor(f"q1_wu{i}", [128, 1024], BF16))
                    for i in range(2)]
        self.sgst = [e(nc.sbuf_tensor(f"q1_sg{i}", [128, TBF], BF16))
                     for i in range(2)]
        self.hstg = [e(nc.sbuf_tensor(f"q1_hs{i}", [128, TBF], BF16))
                     for i in range(2)]
        self.wdn = e(nc.sbuf_tensor("q1_wdn", [128, NIT_DN * 512], BF16))
        self.hslab = [e(nc.sbuf_tensor(f"q1_hl{i}", [128, 22 * 128], BF16))
                      for i in range(2)]
        self.sqb = [e(nc.sbuf_tensor(f"q1_sqb{i}", [128, 512], F32))
                    for i in range(2)]
        self.sscol = e(nc.sbuf_tensor("q1_ssc", [128, 1], F32))
        self.ssq = e(nc.sbuf_tensor("q1_ssq", [128, 4], F32))
        self.fst = [e(nc.sbuf_tensor(f"q1_fs{i}", [128, 512], BF16))
                    for i in range(2)]
        self.pool = e(nc.psum_tensor("q1_pool", [128, 1536], F32))
        self.slot_state = [None, None, None]
        self.slot_i = 0

        items = []
        pending_inv = None
        for fb in range(1, E.p1_nfb):
            items.append(("xnf", fb))
            if pending_inv is not None:
                items.append(pending_inv)
                pending_inv = None
            for j in range(NPAIR_GU):
                items.append(("gu", fb, j))
            for oq in range(4):
                items.append(("dnl", fb, oq))
                for tt in range(self.nttf):
                    items.append(("dn", fb, oq, tt))
            pending_inv = ("inv", fb)
        if pending_inv is not None:
            items.append(("pad", 0))
            items.append(pending_inv)
        self.items = items
        self.G = nck * 8
        self.slices = []
        n = len(items)
        for ck in range(nck):
            lo = n * ck // nck
            hi = n * (ck + 1) // nck
            self.slices.append(items[lo:hi])

        QPE, QAC, QDV = E.s_qpe, E.s_qac, E.s_qdv
        self.sch = {}
        ndst2 = sum(1 for it in items if it[0] in ("gu", "dn"))
        self.d2tok = E.dst2.plan(ndst2)
        d2i = 0
        si = 0
        for it in items:
            d = {}
            if it[0] == "gu":
                d["slot_g"] = si % 3
                d["slot_u"] = (si + 1) % 3
                si += 2
                d["pe_gh"] = QPE.bump()
                d["pe_g"] = QPE.bump()
                d["pe_uh"] = QPE.bump()
                d["pe_u"] = QPE.bump()
                d["silu"] = QAC.bump()
                d["ucp"] = QAC.bump()
                d["htt"] = QDV.bump()
                d["hst"] = self.d2tok[d2i]
                d2i += 1
            elif it[0] == "dn":
                d["slot"] = si % 3
                si += 1
                d["pe_h"] = QPE.bump()
                d["pe"] = QPE.bump()
                d["sq"] = QAC.bump()
                d["fcp"] = QAC.bump()
                d["dv"] = QDV.bump(2)
                d["fst"] = self.d2tok[d2i]
                d2i += 1
            elif it[0] == "inv":
                d["inv"] = QAC.bump()
            self.sch[it] = d
        self.toks = {}
        self.xnf_tok = {}

    def items_for(self, ck):
        return self.slices[ck]

    def slot(self, i):
        return self.pool[:, 512 * i:512 * (i + 1)]

    def wait_slot(self, eng, i):
        st_ = self.slot_state[i]
        if st_ is not None:
            for sem, val in st_:
                eng.wait_ge(sem.h, val)

    def gslice(self, g):
        if g < 0 or g >= self.G:
            return []
        n = len(self.items)
        return self.items[n * g // self.G:n * (g + 1) // self.G]

    def emit_sync(self, sy, it):
        E = self.E
        if it[0] == "xnf":
            fb = it[1]
            if fb >= 2:
                pg = self.sch[("gu", fb - 1, NPAIR_GU - 1)]
                sy.wait_ge(E.s_qpe.h, pg["pe_u"])
            self.xnf_tok[fb] = E.dxp.issue(
                sy, self.xnf[:].rearrange("p (a t) -> p a t", a=NHT),
                E.xn_sc[:, :, fb * self.TBF:(fb + 1) * self.TBF]
                .transpose([1, 0, 2]))
        elif it[0] == "gu":
            fb, j = it[1], it[2]
            if j >= 1:
                dp = self.sch[("gu", fb, j - 1)]
            elif fb >= 2:
                dp = self.sch[("gu", fb - 1, NPAIR_GU - 1)]
            else:
                dp = None
            for half, gkey, wt, wsrc, strm in (
                    (0, "pe_gh", self.wtg, E.wgu_sc[j], E.dxp2),
                    (1, "pe_g", self.wtg, E.wgu_sc[j], E.dxp2),
                    (0, "pe_uh", self.wtu, E.wgu_sc[j + NPAIR_GU], E.dxp3),
                    (1, "pe_u", self.wtu, E.wgu_sc[j + NPAIR_GU], E.dxp3)):
                if dp is not None:
                    sy.wait_ge(E.s_qpe.h, dp[gkey])
                self.toks[(gkey, fb, j)] = strm.issue(
                    sy, wt[half][:], wsrc[:, half * 1024:(half + 1) * 1024])
        elif it[0] == "dnl":
            fb, oq = it[1], it[2]
            if oq >= 1:
                sy.wait_ge(E.s_qpe.h, self.sch[("dn", fb, oq - 1,
                                                self.nttf - 1)]["pe"])
            elif fb >= 2:
                sy.wait_ge(E.s_qpe.h, self.sch[("dn", fb - 1, 3,
                                                self.nttf - 1)]["pe"])
            if oq == 0:
                DmaStream.wait(sy, self.sch[("gu", fb, NPAIR_GU - 1)]["hst"])
                DmaStream.wait(sy, self.sch[("gu", fb, NPAIR_GU - 2)]["hst"])
            self.toks[("wdn", fb, oq)] = E.dxp.issue(
                sy, self.wdn[:].rearrange("p (a i) -> p a i", a=NIT_DN),
                E.wdn_sc[oq])
        elif it[0] == "dn":
            fb, oq, tt = it[1], it[2], it[3]
            k = oq * self.nttf + tt
            if k >= 1:
                oq2, tt2 = divmod(k - 1, self.nttf)
                dp = self.sch[("dn", fb, oq2, tt2)]
            elif fb >= 2:
                dp = self.sch[("dn", fb - 1, 3, self.nttf - 1)]
            else:
                dp = None
            hsrc = self.h_sc[:, :, tt * 128:(tt + 1) * 128].transpose([1, 0, 2])
            if dp is not None:
                sy.wait_ge(E.s_qpe.h, dp["pe_h"])
            self.toks[("hla", fb, oq, tt)] = E.dxp2.issue(
                sy, self.hslab[0][:].rearrange("p (a i) -> p a i", a=22),
                hsrc[:, 0:22])
            if dp is not None:
                sy.wait_ge(E.s_qpe.h, dp["pe"])
            self.toks[("hlb", fb, oq, tt)] = E.dxp2.issue(
                sy, self.hslab[1][:].rearrange("p (a i) -> p a i", a=22),
                hsrc[:, 22:44])

    def emit_pe(self, t, it):
        E = self.E
        if it[0] == "gu":
            fb, j = it[1], it[2]
            d = self.sch[it]
            DmaStream.wait(t, self.xnf_tok[fb])
            for slot_key, wt, keys in ((d["slot_g"], self.wtg,
                                        ("pe_gh", "pe_g")),
                                       (d["slot_u"], self.wtu,
                                        ("pe_uh", "pe_u"))):
                self.wait_slot(t, slot_key)
                for half in range(2):
                    DmaStream.wait(t, self.toks[(keys[half], fb, j)])
                    for h2 in range(NHT // 2):
                        ht = half * (NHT // 2) + h2
                        ins = t.matmul(
                            self.slot(slot_key),
                            wt[half][:, h2 * 128:(h2 + 1) * 128],
                            self.xnf[:, ht * self.TBF:(ht + 1) * self.TBF],
                            start=(ht == 0), stop=(ht == NHT - 1))
                    ins.then_inc(E.s_qpe.h, 1)
            self.slot_state[d["slot_g"]] = [(E.s_qac, d["silu"])]
            self.slot_state[d["slot_u"]] = [(E.s_qac, d["ucp"])]
        elif it[0] == "dn":
            fb, oq, tt = it[1], it[2], it[3]
            k = oq * self.nttf + tt
            d = self.sch[it]
            DmaStream.wait(t, self.toks[("wdn", fb, oq)])
            self.wait_slot(t, d["slot"])
            DmaStream.wait(t, self.toks[("hla", fb, oq, tt)])
            for i2 in range(22):
                ins = t.matmul(self.slot(d["slot"]),
                               self.hslab[0][:, i2 * 128:(i2 + 1) * 128],
                               self.wdn[:, i2 * 512:(i2 + 1) * 512],
                               start=(i2 == 0), stop=False)
            ins.then_inc(E.s_qpe.h, 1)
            DmaStream.wait(t, self.toks[("hlb", fb, oq, tt)])
            for i2 in range(22, NIT_DN):
                ins = t.matmul(self.slot(d["slot"]),
                               self.hslab[1][:, (i2 - 22) * 128:
                                             (i2 - 21) * 128],
                               self.wdn[:, i2 * 512:(i2 + 1) * 512],
                               start=False, stop=(i2 == NIT_DN - 1))
            ins.then_inc(E.s_qpe.h, 1)
            self.slot_state[d["slot"]] = [(E.s_qac, d["fcp"])]

    def emit_ac(self, s, it):
        E = self.E
        if it[0] == "gu":
            fb, j = it[1], it[2]
            d = self.sch[it]
            s.wait_ge(E.s_qpe.h, d["pe_g"])
            if j >= 2:
                s.wait_ge(E.s_qdv.h, self.sch[("gu", fb, j - 2)]["htt"])
            elif fb >= 2:
                s.wait_ge(E.s_qdv.h,
                          self.sch[("gu", fb - 1, NPAIR_GU - 2 + j)]["htt"])
            s.activation(self.sgst[j % 2][:], self.slot(d["slot_g"]),
                         ACT.Silu).then_inc(E.s_qac.h, 1)
            s.wait_ge(E.s_qpe.h, d["pe_u"])
            if j >= 2:
                DmaStream.wait(s, self.sch[("gu", fb, j - 2)]["hst"])
            elif fb >= 2:
                DmaStream.wait(
                    s, self.sch[("gu", fb - 1, NPAIR_GU - 2 + j)]["hst"])
            s.activation(self.hstg[j % 2][:], self.slot(d["slot_u"]),
                         ACT.Copy).then_inc(E.s_qac.h, 1)
        elif it[0] == "dn":
            fb, oq, tt = it[1], it[2], it[3]
            d = self.sch[it]
            s.wait_ge(E.s_qpe.h, d["pe"])
            k = oq * self.nttf + tt
            if k >= 2:
                oq2, tt2 = divmod(k - 2, self.nttf)
                s.wait_ge(E.s_qdv.h, self.sch[("dn", fb, oq2, tt2)]["dv"])
            elif fb >= 2:
                kp = 4 * self.nttf + (k - 2)
                oq2, tt2 = divmod(kp, self.nttf)
                s.wait_ge(E.s_qdv.h, self.sch[("dn", fb - 1, oq2, tt2)]["dv"])
            s.activation(self.sqb[k % 2][:], self.slot(d["slot"]), ACT.Square) \
                .then_inc(E.s_qac.h, 1)
            if k >= 2:
                oq2, tt2 = divmod(k - 2, self.nttf)
                DmaStream.wait(s, self.sch[("dn", fb, oq2, tt2)]["fst"])
            elif fb >= 2:
                kp = 4 * self.nttf + (k - 2)
                oq2, tt2 = divmod(kp, self.nttf)
                DmaStream.wait(s, self.sch[("dn", fb - 1, oq2, tt2)]["fst"])
            s.activation(self.fst[k % 2][:], self.slot(d["slot"]), ACT.Copy) \
                .then_inc(E.s_qac.h, 1)
        elif it[0] == "inv":
            fb = it[1]
            d = self.sch[it]
            for tt2 in range(self.nttf):
                s.wait_ge(E.s_qdv.h, self.sch[("dn", fb, 3, tt2)]["dv"])
            t0 = fb * self.nttf
            s.activation(E.inv_ff[:, t0:t0 + self.nttf],
                         self.ssq[:, 0:self.nttf], ACT.Ln,
                         bias=E.c_epsff[:], scale=1.0 / H)
            s.drain()
            s.activation(E.inv_ff[:, t0:t0 + self.nttf],
                         E.inv_ff[:, t0:t0 + self.nttf],
                         ACT.Exp, scale=-0.5).then_inc(E.s_qac.h, 1)

    def emit_dv(self, v, it):
        E = self.E
        if it[0] == "gu":
            fb, j = it[1], it[2]
            d = self.sch[it]
            v.wait_ge(E.s_qac.h, d["ucp"])
            v.tensor_tensor(self.hstg[j % 2][:], self.hstg[j % 2][:],
                            self.sgst[j % 2][:], ALU.mult) \
                .then_inc(E.s_qdv.h, 1)
        elif it[0] == "dn":
            fb, oq, tt = it[1], it[2], it[3]
            k = oq * self.nttf + tt
            d = self.sch[it]
            v.wait_ge(E.s_qac.h, d["sq"])
            v.reduce_sum(self.sscol[:], self.sqb[k % 2][:], axis=AX.X) \
                .then_inc(E.s_qdv.h, 1)
            v.drain()
            if oq == 0:
                if fb >= 2 and tt == 0:
                    iv = self.sch[("inv", fb - 1)]
                    v.wait_ge(E.s_qac.h, iv["inv"])
                v.tensor_copy(self.ssq[:, tt:tt + 1], self.sscol[:]) \
                    .then_inc(E.s_qdv.h, 1)
            else:
                v.tensor_tensor(self.ssq[:, tt:tt + 1], self.ssq[:, tt:tt + 1],
                                self.sscol[:], ALU.add).then_inc(E.s_qdv.h, 1)
            v.drain()

    def emit_gp(self, g, it):
        E = self.E
        if it[0] == "gu":
            fb, j = it[1], it[2]
            d = self.sch[it]
            g.wait_ge(E.s_qdv.h, d["htt"])
            if fb >= 2 and j == 0:
                DmaStream.wait(g, self.toks[("hlb", fb - 1, 3, self.nttf - 1)])
            tk = E.dst2.issue(g, self.h_sc[j], self.hstg[j % 2][:])
            assert tk == d["hst"], (tk, d["hst"])
        elif it[0] == "dn":
            fb, oq, tt = it[1], it[2], it[3]
            d = self.sch[it]
            g.wait_ge(E.s_qac.h, d["fcp"])
            ttg = fb * self.nttf + tt
            tk = E.dst2.issue(
                g, E.ffn_sc[ttg, :, oq * 512:(oq + 1) * 512],
                self.fst[(oq * self.nttf + tt) % 2][:])
            assert tk == d["fst"], (tk, d["fst"])


def _p3_yevac(s, E, d, yfm, psB, q, SPE, SAC, dprev):
    hq = q * 8
    s.wait_ge(SPE.h, d[f"t5_{q}"])
    s.activation(
        yfm[0:64, hq * 64:hq * 64 + 512].rearrange("p (a i) -> p a i", a=4),
        psB[0:64, :].rearrange("p (a b i) -> p a b i", a=4, b=2)[:, :, 0, :],
        ACT.Copy)
    s.activation(
        yfm[64:128, hq * 64:hq * 64 + 512].rearrange("p (a i) -> p a i", a=4),
        psB[0:64, :].rearrange("p (a b i) -> p a b i", a=4, b=2)[:, :, 1, :],
        ACT.Copy).then_inc(SAC.h, 2)


def _p4_out(nc, st, E):
    ps_ = ExitStack()
    e = ps_.enter_context
    T, ntt = E.T, E.ntt

    wout = e(nc.sbuf_tensor("p4_w", [128, 32 * 1024], BF16))
    ysb = [e(nc.sbuf_tensor(f"p4_y{i}", [128, 32 * 128], BF16)) for i in range(2)]
    xsb = [e(nc.sbuf_tensor(f"p4_x{i}", [128, 1024], F32)) for i in range(2)]
    fsb = [e(nc.sbuf_tensor(f"p4_f{i}", [128, 1024], BF16)) for i in range(2)]
    st1 = e(nc.sbuf_tensor("p4_s1", [128, 512], F32))
    ost = [e(nc.sbuf_tensor(f"p4_o{i}", [128, 512], F32)) for i in range(4)]
    psO = e(nc.psum_tensor("p4_ps", [128, 2048], F32))

    SPE, SDV, SAC = E.s_pe, E.s_dv, E.s_ac
    otoks = E.dst.plan(2 * ntt * 2)

    for oh in range(2):
        mm_d = []
        ev_d = []
        for tt in range(ntt):
            for ob in range(2):
                mm_d.append(E.s_pe.bump())
                ev_d.append(E.s_dv.bump(2))
        wtok, ytok, xtok, ftok2 = [None], [], [], []

        with nc.Block() as blk:
            @blk.gpsimd
            def _(g):
                wtok[0] = E.dgp.issue(
                    g, wout[:].rearrange("p (a f) -> p a f", a=32),
                    E.woutT[:, oh * 1024:(oh + 1) * 1024].rearrange(
                        "(a p) f -> p a f", p=128))

            @blk.sync
            def _(sy):
                for tt in range(ntt):
                    if tt >= 2:
                        sy.wait_ge(SPE.h, mm_d[2 * (tt - 2) + 1])
                        sy.wait_ge(SDV.h, ev_d[2 * (tt - 2) + 1])
                    ytok.append(E.dx.issue(
                        sy, ysb[tt % 2][:].rearrange("p (a i) -> p a i", a=32),
                        E.ynw_sc[tt].transpose([1, 0, 2])))
                    xtok.append(E.dx2.issue(
                        sy, xsb[tt % 2][:],
                        E.x_in[tt * 128:(tt + 1) * 128,
                               oh * 1024:(oh + 1) * 1024]))
                    ftok2.append(E.dcn.issue(
                        sy, fsb[tt % 2][:],
                        E.ffn_sc[tt, :, oh * 1024:(oh + 1) * 1024]))

            @blk.tensor
            def _(t):
                DmaStream.wait(t, wtok[0])
                for tt in range(ntt):
                    DmaStream.wait(t, ytok[tt])
                    for ob in range(2):
                        k = tt * 2 + ob
                        slot = k % 4
                        if k >= 4:
                            t.wait_ge(SDV.h, ev_d[k - 4])
                        for ft in range(32):
                            ins = t.matmul(
                                psO[:, slot * 512:(slot + 1) * 512],
                                ysb[tt % 2][:, ft * 128:(ft + 1) * 128],
                                wout[:, ft * 1024 + ob * 512:
                                     ft * 1024 + ob * 512 + 512],
                                start=(ft == 0), stop=(ft == 31))
                        ins.then_inc(SPE.h, 1)

            @blk.vector
            def _(v):
                for tt in range(ntt):
                    DmaStream.wait(v, xtok[tt])
                    DmaStream.wait(v, ftok2[tt])
                    for ob in range(2):
                        k = tt * 2 + ob
                        slot = k % 4
                        gidx = oh * 2 * ntt + k
                        v.wait_ge(SPE.h, mm_d[k])
                        if gidx >= 4:
                            DmaStream.wait(v, otoks[gidx - 4])
                        v.scalar_tensor_tensor(
                            st1[:], psO[:, slot * 512:(slot + 1) * 512],
                            E.inv_at[:, tt:tt + 1],
                            xsb[tt % 2][:, ob * 512:(ob + 1) * 512],
                            ALU.mult, ALU.add).then_inc(SDV.h, 1)
                        v.drain()
                        v.scalar_tensor_tensor(
                            ost[k % 4][:], fsb[tt % 2][:, ob * 512:(ob + 1) * 512],
                            E.inv_ff[:, tt:tt + 1], st1[:],
                            ALU.mult, ALU.add).then_inc(SDV.h, 1)
                        v.drain()

            @blk.scalar
            def _(s):
                for tt in range(ntt):
                    for ob in range(2):
                        k = tt * 2 + ob
                        gidx = oh * 2 * ntt + k
                        s.wait_ge(SDV.h, ev_d[k])
                        tk = E.dst.issue(
                            s, E.out[tt * 128:(tt + 1) * 128,
                                     oh * 1024 + ob * 512:
                                     oh * 1024 + (ob + 1) * 512],
                            ost[k % 4][:])
                        assert tk == otoks[gidx]

    with nc.Block() as blk:
        @blk.sync
        def _(sy):
            for tk in otoks[-3:]:
                DmaStream.wait(sy, tk)
    ps_.close()


_prog_cache = {}


def _host_prep(inputs):
    w = {}
    w["winT"] = np.ascontiguousarray(np.asarray(inputs["L0_attn_in_proj_w"]).T)
    w["woutT"] = np.ascontiguousarray(np.asarray(inputs["L0_attn_out_proj_w"]).T)
    cw = np.asarray(inputs["L0_attn_conv_w"]).reshape(34, 128, 4)
    w["convw"] = np.ascontiguousarray(cw.transpose(1, 0, 2).reshape(128, 136))
    w["convb"] = np.ascontiguousarray(
        np.asarray(inputs["L0_attn_conv_b"]).reshape(34, 128).T)
    w["dtb"] = np.asarray(inputs["L0_attn_dt_bias"]).reshape(1, 64)
    w["alog"] = np.asarray(inputs["L0_attn_A_log"]).reshape(1, 64)
    w["dvec"] = np.asarray(inputs["L0_attn_D"]).reshape(1, 64)
    w["nw"] = np.ascontiguousarray(
        np.asarray(inputs["L0_attn_norm_weight"]).reshape(32, 128).T)
    w["gup"] = np.asarray(inputs["L0_ffn_gate_up_packed"])
    w["gus"] = np.ascontiguousarray(np.asarray(inputs["L0_ffn_gate_up_scales"]).T)
    w["dnp"] = np.asarray(inputs["L0_ffn_down_packed"])
    w["dns"] = np.ascontiguousarray(np.asarray(inputs["L0_ffn_down_scales"]).T)
    return {k: np.ascontiguousarray(v) for k, v in w.items()}


def kernel(**inputs):
    x = np.asarray(inputs["L0_attn_hidden_states"])
    B = x.shape[0]
    w = _host_prep(inputs)
    key = ("full", 2048)
    if key not in _prog_cache:
        _prog_cache[key] = build(T=2048)
    nc = _prog_cache[key]
    in_maps = []
    for c in range(B):
        m = {"x_in": np.ascontiguousarray(np.asarray(x[c], dtype=np.float32))}
        m.update(w)
        in_maps.append(m)
    res = run_bass_kernel_spmd(nc, in_maps, core_ids=list(range(B)))
    outs = [res.results[c]["out"] for c in range(B)]
    return np.stack(outs, axis=0).astype(np.float32)


# version 22
# speedup vs baseline: 1.0574x; 1.0574x over previous
import numpy as np
from contextlib import ExitStack

import concourse.bass as bass
import concourse.mybir as mybir
from concourse.bass_utils import run_bass_kernel_spmd

F32 = mybir.dt.float32
BF16 = mybir.dt.bfloat16
F16 = mybir.dt.float16
I32 = mybir.dt.int32
U16 = mybir.dt.uint16
ALU = mybir.AluOpType
ACT = mybir.ActivationFunctionType
AX = mybir.AxisListType

H = 2048
PROJ = 8512
INTER = 4096
CONVD = 4352
NH = 64
HD = 64
NST = 128
FI = 5632
GUD = 11264
L = 128
NHT = H // 128
NFT_G = INTER // 128
NOT_GU = GUD // 128
NPAIR_GU = FI // 128
NOT_DN = H // 128
NIT_DN = FI // 128


class Sem:
    def __init__(self, st, nc, name):
        self.h = st.enter_context(nc.semaphore(name))
        self.n = 0

    def bump(self, k=1):
        self.n += k
        return self.n


class DmaStream:

    def __init__(self, st, nc, name, depth=2):
        self.sems = [Sem(st, nc, f"{name}{i}") for i in range(depth)]
        self.i = 0

    def issue(self, eng, out, in_, **kw):
        s = self.sems[self.i % len(self.sems)]
        self.i += 1
        if s.n > 0:
            eng.wait_ge(s.h, s.n)
        eng.dma_start(out, in_, **kw).then_inc(s.h, 16)
        s.n += 16
        return (s, s.n)

    def plan(self, n):
        sims = [s.n for s in self.sems]
        toks = []
        for k in range(n):
            j = (self.i + k) % len(self.sems)
            sims[j] += 16
            toks.append((self.sems[j], sims[j]))
        return toks

    @staticmethod
    def wait(eng, tok):
        eng.wait_ge(tok[0].h, tok[1])


class Env:
    pass


def _quiesce_stores(nc, E):
    with nc.Block() as blk:
        @blk.sync
        def _(sy):
            for strm in (E.dst, E.dst2, E.dst3):
                for s in strm.sems:
                    if s.n > 0:
                        sy.wait_ge(s.h, s.n)


def build(T=2048, phases=(0, 1, 2, 3, 4)):
    nck = T // L
    ntt = T // 128
    nc = bass.Bass("TRN2", target_bir_lowering=False)
    E = Env()
    E.T, E.nck, E.ntt = T, nck, ntt

    E.x_in = nc.dram_tensor("x_in", [T, H], F32, kind="ExternalInput").ap()
    E.winT = nc.dram_tensor("winT", [H, PROJ], F32, kind="ExternalInput").ap()
    E.woutT = nc.dram_tensor("woutT", [INTER, H], F32, kind="ExternalInput").ap()
    E.convw = nc.dram_tensor("convw", [128, 136], F32, kind="ExternalInput").ap()
    E.convb = nc.dram_tensor("convb", [128, 34], F32, kind="ExternalInput").ap()
    E.dtb_in = nc.dram_tensor("dtb", [1, NH], F32, kind="ExternalInput").ap()
    E.alog_in = nc.dram_tensor("alog", [1, NH], F32, kind="ExternalInput").ap()
    E.dvec_in = nc.dram_tensor("dvec", [1, NH], F32, kind="ExternalInput").ap()
    E.nw_in = nc.dram_tensor("nw", [128, 32], F32, kind="ExternalInput").ap()
    E.gup_in = nc.dram_tensor("gup", [GUD, H // 8], I32, kind="ExternalInput").ap()
    E.gus_in = nc.dram_tensor("gus", [GUD, 16], F16, kind="ExternalInput").ap()
    E.dnp_in = nc.dram_tensor("dnp", [H, FI // 8], I32, kind="ExternalInput").ap()
    E.dns_in = nc.dram_tensor("dns", [H, 44], F16, kind="ExternalInput").ap()
    E.out = nc.dram_tensor("out", [T, H], F32, kind="ExternalOutput").ap()

    E.xn_sc = nc.dram_tensor("xn_sc", [NHT, 128, T], BF16).ap()
    E.sg_sc = nc.dram_tensor("sg_sc", [NFT_G, 128, T], BF16).ap()
    E.hs_sc = nc.dram_tensor("hs_sc", [NFT_G, 128, T], BF16).ap()
    E.b_sc = nc.dram_tensor("b_sc", [128, T], BF16).ap()
    E.c_sc = nc.dram_tensor("c_sc", [128, T], BF16).ap()
    E.ynw_sc = nc.dram_tensor("ynw_sc", [nck, 32, 128, 128], BF16).ap()
    E.ffn_sc = nc.dram_tensor("ffn_sc", [ntt, 128, H], BF16).ap()
    E.wgu_sc = nc.dram_tensor("wgu_sc", [NOT_GU, 128, H], BF16).ap()
    E.wdn_sc = nc.dram_tensor("wdn_sc", [4, 128, NIT_DN, 512], BF16).ap()

    st = ExitStack()
    with st:
        e = st.enter_context

        E.iot_r = e(nc.sbuf_tensor("iot_r", [128, 128], F32))
        E.iot_c = e(nc.sbuf_tensor("iot_c", [128, 1], F32))
        E.ident_b = e(nc.sbuf_tensor("ident_b", [128, 128], BF16))
        E.ident_f = e(nc.sbuf_tensor("ident_f", [128, 128], F32))
        E.i64 = e(nc.sbuf_tensor("i64", [64, 64], F32))
        E.triu_f = e(nc.sbuf_tensor("triu_f", [128, 128], F32))
        E.negm_f = e(nc.sbuf_tensor("negm_f", [128, 128], F32))
        E.ones_b = e(nc.sbuf_tensor("ones_b", [128, 1], BF16))
        E.ones64 = e(nc.sbuf_tensor("ones64", [64, 128], F32))
        E.c_one = e(nc.sbuf_tensor("c_one", [128, 1], F32))
        E.c_epsln = e(nc.sbuf_tensor("c_epsln", [128, 1], F32))
        E.c_epsat = e(nc.sbuf_tensor("c_epsat", [128, 1], F32))
        E.c_epsff = e(nc.sbuf_tensor("c_epsff", [128, 1], F32))
        E.dtb_rep = e(nc.sbuf_tensor("dtb_rep", [128, NH], F32))
        E.nega_rep = e(nc.sbuf_tensor("nega_rep", [128, NH], F32))
        E.d_rep = e(nc.sbuf_tensor("d_rep", [128, NH], F32))
        E.dfull = e(nc.sbuf_tensor("dfull", [128, 32], BF16))
        E.convw_sb = e(nc.sbuf_tensor("convw_sb", [128, 136], F32))
        E.convb_sb = e(nc.sbuf_tensor("convb_sb", [128, 34], F32))
        E.nw_bf = e(nc.sbuf_tensor("nw_bf", [128, 32], BF16))
        E.dt_all = e(nc.sbuf_tensor("dt_all", [128, NH], F32))
        E.dtbf_all = e(nc.sbuf_tensor("dtbf_all", [128, nck * NH], BF16))
        E.sfm_all = e(nc.sbuf_tensor("sfm_all", [64, nck * 128], F32))
        E.inv_at = e(nc.sbuf_tensor("inv_at", [128, nck], F32))
        E.inv_ff = e(nc.sbuf_tensor("inv_ff", [128, ntt], F32))

        for nm in ("pe", "dv", "ac", "g2", "pe2", "dv2", "ac2", "pe3", "dv3",
                   "qpe", "qac", "qdv"):
            setattr(E, "s_" + nm, Sem(st, nc, "s_" + nm))
        E.dcn = DmaStream(st, nc, "dcn", 2)
        E.dgp = DmaStream(st, nc, "dgp", 2)
        E.dx = DmaStream(st, nc, "dx", 2)
        E.dx2 = DmaStream(st, nc, "dx2", 2)
        E.dst = DmaStream(st, nc, "dst", 3)
        E.dst2 = DmaStream(st, nc, "dst2", 2)
        E.dst3 = DmaStream(st, nc, "dst3", 2)
        E.dxp = DmaStream(st, nc, "dxp", 2)
        E.dxp2 = DmaStream(st, nc, "dxp2", 2)
        E.dxp3 = DmaStream(st, nc, "dxp3", 2)

        _consts(nc, E)
        if 0 in phases:
            _p0_ln(nc, st, E)
        if 1 in phases:
            _p1_ffn(nc, st, E, fb0_only=(3 in phases))
        if 2 in phases:
            _p2_inproj(nc, st, E)
        if 3 in phases:
            _p3_scan(nc, st, E, with_p1=(1 in phases))
        if 4 in phases:
            _p4_out(nc, st, E)

    nc.finalize()
    return nc


def _consts(nc, E):
    with nc.Block() as blk:
        @blk.gpsimd
        def _(g):
            g.iota(E.iot_r[:], [[1, 128]], channel_multiplier=0,
                   allow_small_or_imprecise_dtypes=True)
            g.iota(E.iot_c[:], [[0, 1]], channel_multiplier=1,
                   allow_small_or_imprecise_dtypes=True)
            E.tok_nw = E.dgp.issue(g, E.nw_bf[:], E.nw_in[:])

        @blk.sync
        def _(sy):
            E.tok_cw = E.dcn.issue(sy, E.convw_sb[:], E.convw[:])
            E.tok_cb = E.dcn.issue(sy, E.convb_sb[:], E.convb[:])

        @blk.vector
        def _(v):
            v.memset(E.ones64[:], 1.0)
            v.memset(E.c_one[:], 1.0)
            v.memset(E.c_epsln[:], 1e-5)
            v.memset(E.c_epsat[:], 1e-5)
            v.memset(E.c_epsff[:], 1e-6)
            v.memset(E.ones_b[:], 1.0)

    with nc.Block() as blk:
        @blk.vector
        def _(v):
            v.tensor_scalar(E.ident_b[:], E.iot_r[:], E.iot_c[:], None, ALU.is_equal)
            v.tensor_scalar(E.ident_f[:], E.iot_r[:], E.iot_c[:], None, ALU.is_equal)
            v.tensor_scalar(E.i64[:], E.iot_r[0:64, 0:64], E.iot_c[0:64, :], None,
                            ALU.is_equal)
            v.tensor_scalar(E.triu_f[:], E.iot_r[:], E.iot_c[:], None, ALU.is_ge)
            v.tensor_scalar(E.negm_f[:], E.iot_r[:], E.iot_c[:], -3.0e38,
                            ALU.is_lt, ALU.mult)

    with (nc.sbuf_tensor("cn_vrow", [1, 3 * NH], F32) as vrow,
          nc.sbuf_tensor("cn_ones1f", [1, 128], F32) as ones1f,
          nc.psum_tensor("cn_ps", [128, 3 * NH], F32) as ps_rep):
        toks = []
        with nc.Block() as blk:
            @blk.sync
            def _(sy):
                toks.append(E.dcn.issue(sy, vrow[0:1, 0:NH], E.dtb_in[:]))
                toks.append(E.dcn.issue(sy, vrow[0:1, NH:2 * NH], E.alog_in[:]))
                toks.append(E.dcn.issue(sy, vrow[0:1, 2 * NH:3 * NH], E.dvec_in[:]))

            @blk.vector
            def _(v):
                v.memset(ones1f[:], 1.0)

        with nc.Block() as blk:
            @blk.tensor
            def _(t):
                for tk in toks:
                    DmaStream.wait(t, tk)
                t.matmul(ps_rep[:], ones1f[:], vrow[:], start=True, stop=True)

        with nc.Block() as blk:
            @blk.vector
            def _(v):
                v.tensor_copy(E.dtb_rep[:], ps_rep[:, 0:NH])
                v.tensor_copy(E.d_rep[:], ps_rep[:, 2 * NH:3 * NH])

            @blk.scalar
            def _(s):
                s.activation(E.nega_rep[:], ps_rep[:, NH:2 * NH], ACT.Exp,
                             scale=1.0)

        with nc.Block() as blk:
            @blk.vector
            def _(v):
                v.tensor_scalar(E.nega_rep[:], E.nega_rep[:], -1.0, None, ALU.mult)
                v.tensor_copy(E.dfull[0:64, :], E.d_rep[0:64, 0:NH:2])
                v.tensor_copy(E.dfull[64:128, :], E.d_rep[64:128, 1:NH:2])


def _p0_ln(nc, st, E):
    ps_ = ExitStack()
    e = ps_.enter_context
    ntt = E.ntt
    xt = [e(nc.sbuf_tensor(f"p0_x{i}", [128, H], F32)) for i in range(2)]
    sq = [e(nc.sbuf_tensor(f"p0_sq{i}", [128, H], F32)) for i in range(2)]
    mcol = e(nc.sbuf_tensor("p0_m", [128, 2], F32))
    m2col = e(nc.sbuf_tensor("p0_m2", [128, 2], F32))
    vcol = e(nc.sbuf_tensor("p0_v", [128, 2], F32))
    icol = e(nc.sbuf_tensor("p0_i", [128, 2], F32))
    xn_tm = [e(nc.sbuf_tensor(f"p0_xn{i}", [128, H], BF16)) for i in range(2)]
    stg = [e(nc.sbuf_tensor(f"p0_st{i}", [128, H], BF16)) for i in range(2)]
    pst = [e(nc.psum_tensor(f"p0_ps{i}", [128, 2048], BF16)) for i in range(2)]

    sqd = [E.s_ac.bump() for _ in range(ntt)]
    std = [E.s_dv.bump() for _ in range(ntt)]
    inv = [E.s_ac2.bump() for _ in range(ntt)]
    xnd = [E.s_dv2.bump() for _ in range(ntt)]
    tpd = [E.s_pe.bump() for _ in range(ntt)]
    evd = [E.s_g2.bump() for _ in range(ntt)]
    xtok = []
    sttok = []

    with nc.Block() as blk:
        @blk.sync
        def _(sy):
            for i in range(ntt):
                if i >= 2:
                    sy.wait_ge(E.s_dv2.h, xnd[i - 2])
                    sy.wait_ge(E.s_ac.h, sqd[i - 2])
                xtok.append(E.dx.issue(
                    sy, xt[i % 2][:], E.x_in[i * 128:(i + 1) * 128, :]))

        @blk.scalar
        def _(s):
            for i in range(ntt):
                p = i % 2
                DmaStream.wait(s, xtok[i])
                if i >= 2:
                    s.wait_ge(E.s_dv.h, std[i - 2])
                s.activation(sq[p][:], xt[p][:], ACT.Square).then_inc(E.s_ac.h, 1)
                if i >= 1:
                    s.wait_ge(E.s_g2.h, evd[i - 1])
                    sttok.append(E.dst.issue(
                        s,
                        E.xn_sc[:, :, (i - 1) * 128:i * 128].transpose([1, 0, 2]),
                        stg[(i - 1) % 2][:].rearrange("p (a t) -> p a t", a=NHT)))
                s.wait_ge(E.s_dv.h, std[i])
                s.activation(icol[:, p:p + 1], vcol[:, p:p + 1], ACT.Ln,
                             bias=E.c_epsln[:])
                s.drain()
                s.activation(icol[:, p:p + 1], icol[:, p:p + 1], ACT.Exp,
                             scale=-0.5).then_inc(E.s_ac2.h, 1)
            s.wait_ge(E.s_g2.h, evd[ntt - 1])
            sttok.append(E.dst.issue(
                s,
                E.xn_sc[:, :, (ntt - 1) * 128:ntt * 128].transpose([1, 0, 2]),
                stg[(ntt - 1) % 2][:].rearrange("p (a t) -> p a t", a=NHT)))

        @blk.vector
        def _(v):
            for i in range(ntt):
                p = i % 2
                if i >= 1:
                    v.wait_ge(E.s_pe.h, tpd[i - 1])
                    if i >= 3:
                        DmaStream.wait(v, sttok[i - 3])
                    v.tensor_copy(stg[(i - 1) % 2][:], pst[(i - 1) % 2][:]) \
                        .then_inc(E.s_g2.h, 1)
                DmaStream.wait(v, xtok[i])
                v.wait_ge(E.s_ac.h, sqd[i])
                if i >= 2:
                    v.wait_ge(E.s_ac2.h, inv[i - 2])
                    v.drain()
                v.reduce_sum(mcol[:, p:p + 1], xt[p][:], axis=AX.X)
                v.reduce_sum(m2col[:, p:p + 1], sq[p][:], axis=AX.X)
                v.drain()
                v.tensor_scalar(mcol[:, p:p + 1], mcol[:, p:p + 1], -1.0 / H,
                                None, ALU.mult)
                v.tensor_scalar(m2col[:, p:p + 1], m2col[:, p:p + 1], 1.0 / H,
                                None, ALU.mult)
                v.drain()
                v.tensor_scalar(vcol[:, p:p + 1], mcol[:, p:p + 1],
                                mcol[:, p:p + 1], -1.0, ALU.mult, ALU.mult)
                v.drain()
                v.tensor_tensor(vcol[:, p:p + 1], vcol[:, p:p + 1],
                                m2col[:, p:p + 1], ALU.add).then_inc(E.s_dv.h, 1)
                v.wait_ge(E.s_ac2.h, inv[i])
                if i >= 2:
                    v.wait_ge(E.s_pe.h, tpd[i - 2])
                v.tensor_scalar(xn_tm[p][:], xt[p][:], mcol[:, p:p + 1],
                                icol[:, p:p + 1], ALU.add, ALU.mult) \
                    .then_inc(E.s_dv2.h, 1)
            v.wait_ge(E.s_pe.h, tpd[ntt - 1])
            if ntt >= 3:
                DmaStream.wait(v, sttok[ntt - 3])
            v.tensor_copy(stg[(ntt - 1) % 2][:], pst[(ntt - 1) % 2][:]) \
                .then_inc(E.s_g2.h, 1)

        @blk.tensor
        def _(t):
            for i in range(ntt):
                p = i % 2
                t.wait_ge(E.s_dv2.h, xnd[i])
                if i >= 2:
                    t.wait_ge(E.s_g2.h, evd[i - 2])
                for ht in range(NHT):
                    ins = t.transpose(pst[p][:, ht * 128:(ht + 1) * 128],
                                      xn_tm[p][:, ht * 128:(ht + 1) * 128],
                                      E.ident_b[:])
                ins.then_inc(E.s_pe.h, 1)

    _quiesce_stores(nc, E)
    ps_.close()


def _p1_ffn(nc, st, E, fb0_only=False):
    ps_ = ExitStack()
    e = ps_.enter_context
    T = E.T
    TBF = min(512, T)
    nfb = T // TBF
    E.p1_TBF = TBF
    E.p1_nfb = nfb
    ntt_fb = TBF // 128

    h_fm = e(nc.sbuf_tensor("p1_h", [128, NIT_DN * TBF], BF16))
    ssq = e(nc.sbuf_tensor("p1_ssq", [128, ntt_fb], F32))
    ps_g = e(nc.psum_tensor("p1_psg", [128, TBF], F32))
    ps_u = e(nc.psum_tensor("p1_psu", [128, TBF], F32))
    psT = e(nc.psum_tensor("p1_psT", [128, 2048], BF16))
    psD = e(nc.psum_tensor("p1_psD", [128, 1024], F32))

    fbs = range(1) if (fb0_only and nfb > 1) else range(nfb)
    for fb in fbs:
        deq = (fb == 0)
        _p1_gate_up(nc, E, fb, TBF, deq, h_fm, ps_g, ps_u, psT)
        _p1_down(nc, E, fb, TBF, deq, h_fm, psD, psT, ssq)

    _quiesce_stores(nc, E)
    ps_.close()


def _p1_gate_up(nc, E, fb, TBF, deq, h_fm, ps_g, ps_u, psT):
    T = E.T
    with ExitStack() as ps_:
        e = ps_.enter_context
        xnf = e(nc.sbuf_tensor(f"p1_xn{fb}", [128, NHT * TBF], BF16))
        wt = [[e(nc.sbuf_tensor(f"p1_w{fb}_{a}{i}", [128, 2048], BF16))
               for i in range(2)] for a in range(2)]
        sgst = [e(nc.sbuf_tensor(f"p1_sg{fb}_{i}", [128, TBF], BF16)) for i in range(2)]
        if deq:
            pkb = [e(nc.sbuf_tensor(f"p1_pk{fb}_{i}", [128, 256], I32)) for i in range(2)]
            scb = [e(nc.sbuf_tensor(f"p1_sc{fb}_{i}", [128, 16], F16)) for i in range(2)]
            scf = [e(nc.sbuf_tensor(f"p1_scf{fb}_{i}", [128, 16], BF16))
                   for i in range(2)]
            nib = e(nc.sbuf_tensor(f"p1_nib{fb}", [128, 2048], U16))
            tq1 = e(nc.sbuf_tensor(f"p1_tq1{fb}", [128, 2048], U16))
            tq2 = [e(nc.sbuf_tensor(f"p1_tq2{fb}_{i}", [128, 2048], U16))
                   for i in range(2)]
            tq3 = e(nc.sbuf_tensor(f"p1_tq3{fb}", [128, 2048], U16))
            wbf = e(nc.sbuf_tensor(f"p1_wbf{fb}", [128, 2048], BF16))

        SPE, SDV, SAC, SGP = E.s_pe, E.s_dv, E.s_ac, E.s_g2
        ntile = 2 * NPAIR_GU
        ots = []
        for j in range(NPAIR_GU):
            ots.append(j)
            ots.append(j + NPAIR_GU)

        def wslot(i):
            return wt[i % 2][(i // 2) % 2]

        sch = [{} for _ in range(ntile)]
        for i in range(ntile):
            d = sch[i]
            if deq:
                d["gp_nib"] = SDV.bump(8)
                d["dv_pre"] = SDV.bump(1)
                d["g_deq"] = SGP.bump(1)
                if i >= 2 and (i - 2) % 2 == 1:
                    sch[i - 2]["dv_h"] = SDV.bump()
                d["t_tp"] = SPE.bump()
                d["a_ev"] = SAC.bump()
            d["t_mm"] = SPE.bump()
            if i % 2 == 1:
                d["a_silu"] = SAC.bump()
                if not deq:
                    d["dv_h"] = SDV.bump()
        if deq:
            sch[ntile - 1]["dv_h"] = SDV.bump()

        xtok = [None]
        ptok, sctok, wtok, wstok = [], [], [], []

        with nc.Block() as blk:
            @blk.sync
            def _(sy):
                xtok[0] = E.dx.issue(
                    sy, xnf[:].rearrange("p (a t) -> p a t", a=NHT),
                    E.xn_sc[:, :, fb * TBF:(fb + 1) * TBF].transpose([1, 0, 2]))
                for i, ot in enumerate(ots):
                    if deq:
                        if i >= 2:
                            sy.wait_ge(SDV.h, sch[i - 2]["gp_nib"])
                            sy.wait_ge(SDV.h, sch[i - 2]["dv_pre"])
                        ptok.append(E.dx2.issue(sy, pkb[i % 2][:],
                                                E.gup_in[ot * 128:(ot + 1) * 128, :]))
                        sctok.append(E.dcn.issue(sy, scb[i % 2][:],
                                                 E.gus_in[ot * 128:(ot + 1) * 128, :]))
                    else:
                        if fb == 1 and i == 0:
                            for tk in E.p1_gu_wstok[-3:]:
                                DmaStream.wait(sy, tk)
                        if i >= 4:
                            sy.wait_ge(SPE.h, sch[i - 4]["t_mm"])
                        wtok.append(E.dx2.issue(sy, wslot(i)[:], E.wgu_sc[ot]))

            @blk.gpsimd
            def _(g):
                if not deq:
                    return
                for i, ot in enumerate(ots):
                    g.wait_ge(SDV.h, sch[i]["dv_pre"])
                    if i >= 1:
                        g.wait_ge(SPE.h, sch[i - 1]["t_tp"])
                    scv = scf[i % 2][:].unsqueeze(2) \
                        .to_broadcast([128, 16, 128])
                    g.tensor_tensor(
                        wbf[:].rearrange("p (a i) -> p a i", a=16),
                        tq2[i % 2][:].bitcast(BF16).rearrange(
                            "p (a i) -> p a i", a=16),
                        scv, ALU.mult).then_inc(SGP.h, 1)
                    g.drain()

            @blk.vector
            def _(v):
                for i, ot in enumerate(ots):
                    if deq:
                        DmaStream.wait(v, ptok[i])
                        pku = pkb[i % 2][:].bitcast(U16)
                        for k in range(8):
                            v.tensor_scalar(nib[:, k::8],
                                            pku[:, (1 if k >= 4 else 0)::2],
                                            4 * (k % 4), 15,
                                            ALU.logical_shift_right,
                                            ALU.bitwise_and).then_inc(SDV.h, 1)
                        v.drain()
                        if i >= 2:
                            v.wait_ge(SGP.h, sch[i - 2]["g_deq"])
                        v.tensor_scalar(tq3[:], nib[:], 12, 0x8000,
                                        ALU.logical_shift_left, ALU.bitwise_and)
                        v.tensor_scalar(tq1[:], nib[:], 7, None,
                                        ALU.bitwise_and)
                        v.drain()
                        v.tensor_scalar(tq2[i % 2][:], tq1[:], 252, None,
                                        ALU.add)
                        v.drain()
                        v.scalar_tensor_tensor(tq1[:], tq1[:], 252.0,
                                               tq2[i % 2][:],
                                               ALU.mult, ALU.min)
                        v.drain()
                        v.scalar_tensor_tensor(tq2[i % 2][:], tq1[:], 64.0,
                                               tq3[:], ALU.mult, ALU.add)
                        v.drain()
                        DmaStream.wait(v, sctok[i])
                        v.tensor_copy(scf[i % 2][:], scb[i % 2][:]) \
                            .then_inc(SDV.h, 1)
                    ih = i - 2 if deq else i
                    if ih >= 0 and ih % 2 == 1:
                        j = ih // 2
                        v.wait_ge(SAC.h, sch[ih]["a_silu"])
                        v.wait_ge(SPE.h, sch[ih]["t_mm"])
                        v.tensor_tensor(h_fm[:, j * TBF:(j + 1) * TBF],
                                        sgst[j % 2][:], ps_u[:], ALU.mult) \
                            .then_inc(SDV.h, 1)
                    if deq and i == len(ots) - 1:
                        j = i // 2
                        v.wait_ge(SAC.h, sch[i]["a_silu"])
                        v.wait_ge(SPE.h, sch[i]["t_mm"])
                        v.tensor_tensor(h_fm[:, j * TBF:(j + 1) * TBF],
                                        sgst[j % 2][:], ps_u[:], ALU.mult) \
                            .then_inc(SDV.h, 1)

            @blk.scalar
            def _(s):
                for i, ot in enumerate(ots):
                    if deq:
                        s.wait_ge(SPE.h, sch[i]["t_tp"])
                        if i >= 4:
                            s.wait_ge(SPE.h, sch[i - 4]["t_mm"])
                        s.activation(wslot(i)[:], psT[:], ACT.Copy) \
                            .then_inc(SAC.h, 1)
                        s.drain()
                        wstok.append(E.dst.issue(s, E.wgu_sc[ot], wslot(i)[:]))
                    if i % 2 == 1:
                        j = i // 2
                        s.wait_ge(SPE.h, sch[i - 1]["t_mm"])
                        if j >= 2:
                            s.wait_ge(SDV.h, sch[2 * (j - 2) + 1]["dv_h"])
                        s.activation(sgst[j % 2][:], ps_g[:], ACT.Silu) \
                            .then_inc(SAC.h, 1)

            @blk.tensor
            def _(t):
                DmaStream.wait(t, xtok[0])
                for i, ot in enumerate(ots):
                    if deq:
                        t.wait_ge(SGP.h, sch[i]["g_deq"])
                        if i >= 1:
                            t.wait_ge(SAC.h, sch[i - 1]["a_ev"])
                        for ht in range(NHT):
                            ins = t.transpose(psT[:, ht * 128:(ht + 1) * 128],
                                              wbf[:, ht * 128:(ht + 1) * 128],
                                              E.ident_b[:])
                        ins.then_inc(SPE.h, 1)
                        t.wait_ge(SAC.h, sch[i]["a_ev"])
                    else:
                        DmaStream.wait(t, wtok[i])
                    if i % 2 == 0 and i >= 2:
                        t.wait_ge(SAC.h, sch[i - 1]["a_silu"])
                    if i % 2 == 1 and i >= 3:
                        t.wait_ge(SDV.h, sch[i - 2]["dv_h"])
                    psu = ps_g if (i % 2 == 0) else ps_u
                    for ht in range(NHT):
                        ins = t.matmul(psu[:],
                                       wslot(i)[:, ht * 128:(ht + 1) * 128],
                                       xnf[:, ht * TBF:(ht + 1) * TBF],
                                       start=(ht == 0), stop=(ht == NHT - 1))
                    ins.then_inc(SPE.h, 1)

        if deq:
            E.p1_gu_wstok = wstok
        _quiesce_stores(nc, E)


def _p1_down(nc, E, fb, TBF, deq, h_fm, psD, psT, ssq):
    T = E.T
    ntt_fb = TBF // 128
    with ExitStack() as ps_:
        e = ps_.enter_context
        wdn = e(nc.sbuf_tensor(f"p1_wdn{fb}", [128, NIT_DN * 512], BF16))
        sqb = e(nc.sbuf_tensor(f"p1_sqb{fb}", [128, 512], F32))
        sscol = e(nc.sbuf_tensor(f"p1_sscol{fb}", [128, 1], F32))
        fst = [e(nc.sbuf_tensor(f"p1_fs{fb}_{i}", [128, 512], BF16)) for i in range(2)]
        if deq:
            pkb = [e(nc.sbuf_tensor(f"p1_dpk{fb}_{i}", [128, 704], I32)) for i in range(2)]
            scb = [e(nc.sbuf_tensor(f"p1_dsc{fb}_{i}", [128, 44], F16)) for i in range(2)]
            scf = e(nc.sbuf_tensor(f"p1_dscf{fb}", [128, 44], F32))
            nib = e(nc.sbuf_tensor(f"p1_dnib{fb}", [128, 5632], U16))
            tq1 = e(nc.sbuf_tensor(f"p1_dtq1{fb}", [128, 5632], U16))
            tq2 = e(nc.sbuf_tensor(f"p1_dtq2{fb}", [128, 5632], U16))
            tq3 = e(nc.sbuf_tensor(f"p1_dtq3{fb}", [128, 5632], U16))
            wbf = e(nc.sbuf_tensor(f"p1_dwbf{fb}", [128, 5632], BF16))

        SPE, SDV, SAC, SGP = E.s_pe, E.s_dv, E.s_ac, E.s_g2
        if deq:
            E.p1_dn_wstok = {}
        ftoks_all = E.dst2.plan(4 * ntt_fb)

        for oq in range(4):
            sch_d = []
            if deq:
                for i in range(4):
                    d = {}
                    d["gp_nib"] = SDV.bump(8)
                    d["dv_deq"] = SDV.bump(6)
                    d["t_tp"] = SPE.bump(3)
                    d["a_ev"] = SAC.bump(3)
                    sch_d.append(d)
            mm_d = [E.s_pe.bump() for _ in range(ntt_fb)]
            sq_d = [E.s_ac.bump() for _ in range(ntt_fb)]
            ss_d, ev_d = [], []
            for _tt in range(ntt_fb):
                ss_d.append(E.s_dv.bump(2))
                ev_d.append(E.s_dv.bump())
            if oq == 3:
                inv_d = E.s_ac.bump()
            ptok, sctok, wltok, wstok2, ftok = [], [], [], [], []

            with nc.Block() as blk:
                @blk.sync
                def _(sy):
                    if deq:
                        for i in range(4):
                            if i >= 2:
                                sy.wait_ge(SDV.h, sch_d[i - 2]["gp_nib"])
                                sy.wait_ge(SDV.h, sch_d[i - 2]["dv_deq"] - 5)
                            ot = oq * 4 + i
                            ptok.append(E.dx2.issue(
                                sy, pkb[i % 2][:],
                                E.dnp_in[ot * 128:(ot + 1) * 128, :]))
                            sctok.append(E.dcn.issue(
                                sy, scb[i % 2][:],
                                E.dns_in[ot * 128:(ot + 1) * 128, :]))
                    else:
                        if fb == 1:
                            for _tk in E.p1_dn_wstok[oq]:
                                DmaStream.wait(sy, _tk)
                        wltok.append(E.dx2.issue(
                            sy, wdn[:].rearrange("p (a i) -> p a i", a=NIT_DN),
                            E.wdn_sc[oq]))

                @blk.gpsimd
                def _(g):
                    for tt in range(ntt_fb):
                        g.wait_ge(SDV.h, ev_d[tt])
                        ttg = fb * ntt_fb + tt
                        tk = E.dst2.issue(
                            g, E.ffn_sc[ttg, :, oq * 512:(oq + 1) * 512],
                            fst[tt % 2][:])
                        assert tk == ftoks_all[oq * ntt_fb + tt]
                        ftok.append(tk)

                @blk.vector
                def _(v):
                    if deq:
                        for i in range(4):
                            DmaStream.wait(v, ptok[i])
                            if i >= 1:
                                v.wait_ge(SDV.h, sch_d[i - 1]["dv_deq"])
                            pku = pkb[i % 2][:].bitcast(U16)
                            for k in range(8):
                                v.tensor_scalar(nib[:, k::8],
                                                pku[:, (1 if k >= 4 else 0)::2],
                                                4 * (k % 4), 15,
                                                ALU.logical_shift_right,
                                                ALU.bitwise_and) \
                                    .then_inc(SDV.h, 1)
                            v.drain()
                            DmaStream.wait(v, sctok[i])
                            if i >= 1:
                                v.wait_ge(SPE.h, sch_d[i - 1]["t_tp"])
                            v.tensor_copy(scf[:], scb[i % 2][:]).then_inc(SDV.h, 1)
                            v.tensor_scalar(tq3[:], nib[:], 12, 0x8000,
                                            ALU.logical_shift_left,
                                            ALU.bitwise_and)
                            v.tensor_scalar(tq1[:], nib[:], 7, None,
                                            ALU.bitwise_and)
                            v.drain()
                            v.tensor_scalar(tq2[:], tq1[:], 252, None, ALU.add)
                            v.drain()
                            v.tensor_scalar(tq1[:], tq1[:], 252, None, ALU.mult)
                            v.drain()
                            v.tensor_tensor(tq1[:], tq1[:], tq2[:], ALU.min)
                            v.drain()
                            v.scalar_tensor_tensor(tq2[:], tq1[:], 64.0, tq3[:],
                                                   ALU.mult, ALU.add)
                            v.drain()
                            scv = scf[:].unsqueeze(2).to_broadcast([128, 44, 128])
                            v.tensor_tensor(
                                wbf[:].rearrange("p (a i) -> p a i", a=44),
                                tq2[:].bitcast(BF16).rearrange(
                                    "p (a i) -> p a i", a=44),
                                scv, ALU.mult).then_inc(SDV.h, 5)
                    for tt in range(ntt_fb):
                        v.wait_ge(SAC.h, sq_d[tt])
                        v.reduce_sum(sscol[:], sqb[:], axis=AX.X) \
                            .then_inc(SDV.h, 1)
                        v.drain()
                        if oq == 0:
                            v.tensor_copy(ssq[:, tt:tt + 1], sscol[:]) \
                                .then_inc(SDV.h, 1)
                        else:
                            v.tensor_tensor(ssq[:, tt:tt + 1], ssq[:, tt:tt + 1],
                                            sscol[:], ALU.add).then_inc(SDV.h, 1)
                        v.drain()
                        v.wait_ge(SPE.h, mm_d[tt])
                        gidx = oq * ntt_fb + tt
                        if gidx >= 2:
                            DmaStream.wait(v, ftoks_all[gidx - 2])
                        v.tensor_copy(fst[tt % 2][:],
                                      psD[:, (tt % 2) * 512:(tt % 2) * 512 + 512]) \
                            .then_inc(SDV.h, 1)

                @blk.scalar
                def _(s):
                    if deq:
                        wstore = []
                        for i in range(4):
                            base = sch_d[i]["t_tp"] - 3
                            for w in range(3):
                                nt = 16 if w < 2 else 12
                                if oq >= 1 and i == 0:
                                    DmaStream.wait(
                                        s, E.p1_dn_wstok[oq - 1][w])
                                s.wait_ge(SPE.h, base + w + 1)
                                s.activation(
                                    wdn[:].rearrange(
                                        "p (a i) -> p a i", a=NIT_DN)
                                    [:, w * 16:w * 16 + nt, i * 128:(i + 1) * 128],
                                    psT[:].rearrange("p (a i) -> p a i", a=16)
                                    [:, 0:nt, :],
                                    ACT.Copy).then_inc(SAC.h, 1)
                                if i == 3:
                                    s.drain()
                                    wstore.append(E.dst.issue(
                                        s,
                                        E.wdn_sc[oq][:, w * 16:w * 16 + nt],
                                        wdn[:].rearrange(
                                            "p (a i) -> p a i", a=NIT_DN)
                                        [:, w * 16:w * 16 + nt, :]))
                        E.p1_dn_wstok[oq] = wstore
                    for tt in range(ntt_fb):
                        s.wait_ge(SPE.h, mm_d[tt])
                        if tt >= 1:
                            s.wait_ge(SDV.h, ss_d[tt - 1])
                        s.activation(sqb[:],
                                     psD[:, (tt % 2) * 512:(tt % 2) * 512 + 512],
                                     ACT.Square).then_inc(SAC.h, 1)
                    if oq == 3:
                        for tt in range(ntt_fb):
                            s.wait_ge(SDV.h, ss_d[tt])
                        ttg0 = fb * ntt_fb
                        s.activation(E.inv_ff[:, ttg0:ttg0 + ntt_fb],
                                     ssq[:, 0:ntt_fb], ACT.Ln,
                                     bias=E.c_epsff[:], scale=1.0 / H)
                        s.drain()
                        s.activation(E.inv_ff[:, ttg0:ttg0 + ntt_fb],
                                     E.inv_ff[:, ttg0:ttg0 + ntt_fb],
                                     ACT.Exp, scale=-0.5).then_inc(SAC.h, 1)

                @blk.tensor
                def _(t):
                    if deq:
                        for i in range(4):
                            t.wait_ge(SDV.h, sch_d[i]["dv_deq"])
                            base_pe = sch_d[i]["t_tp"] - 3
                            base_ac = sch_d[i]["a_ev"] - 3
                            for w in range(3):
                                nt = 16 if w < 2 else 12
                                if (i, w) != (0, 0):
                                    t.wait_ge(SAC.h, base_ac + w)
                                for k in range(nt):
                                    it = w * 16 + k
                                    ins = t.transpose(
                                        psT[:, k * 128:(k + 1) * 128],
                                        wbf[:, it * 128:(it + 1) * 128],
                                        E.ident_b[:])
                                ins.then_inc(SPE.h, 1)
                        t.wait_ge(SAC.h, sch_d[3]["a_ev"])
                    else:
                        DmaStream.wait(t, wltok[0])
                    for tt in range(ntt_fb):
                        if tt >= 2:
                            t.wait_ge(SDV.h, ev_d[tt - 2])
                            t.wait_ge(SAC.h, sq_d[tt - 2])
                        for it in range(NIT_DN):
                            ins = t.matmul(
                                psD[:, (tt % 2) * 512:(tt % 2) * 512 + 512],
                                h_fm[:, it * TBF + tt * 128:
                                     it * TBF + (tt + 1) * 128],
                                wdn[:, it * 512:(it + 1) * 512],
                                start=(it == 0), stop=(it == NIT_DN - 1))
                        ins.then_inc(SPE.h, 1)


            E.p1_last_toks = ftok
        _quiesce_stores(nc, E)


def _p2_inproj(nc, st, E):
    ps_ = ExitStack()
    e = ps_.enter_context
    T, nck = E.T, E.nck
    TB = min(512, T)
    ntb = T // TB

    xn_sb = e(nc.sbuf_tensor("p2_xn", [128, NHT * T], BF16))
    wdt_sb = e(nc.sbuf_tensor("p2_wdt", [128, NHT * 64], BF16))
    loop_st = ExitStack()
    e2 = loop_st.enter_context
    wtb = [e2(nc.sbuf_tensor(f"p2_wt{i}", [128, NHT * 128], BF16)) for i in range(2)]
    gst = [e2(nc.sbuf_tensor(f"p2_gs{i}", [128, T], BF16)) for i in range(2)]
    cvs = [e2(nc.sbuf_tensor(f"p2_cv{i}", [128, T + 3], BF16)) for i in range(2)]
    hst = [e2(nc.sbuf_tensor(f"p2_hs{i}", [128, T], BF16)) for i in range(2)]
    diag = [e2(nc.sbuf_tensor(f"p2_dg{i}", [128, 4 * 128], BF16)) for i in range(2)]
    ps_ip = e2(nc.psum_tensor("p2_psip", [128, T if T <= 2048 else 2048], F32))
    ps_cv = e2(nc.psum_tensor("p2_pscv", [128, T if T <= 2048 else 2048], F32))

    NFT = 66

    mmd = [E.s_pe.bump() for _ in range(NFT)]
    evg = {}
    for ft in range(NFT):
        if ft < 32:
            evg[ft] = ("ac", E.s_ac.bump())
        else:
            evg[ft] = ("dv", E.s_dv.bump())
    cvd = {ft: E.s_pe2.bump() for ft in range(32, NFT)}
    sld = {ft: E.s_ac2.bump() for ft in range(32, NFT)}
    dgd = {ft: E.s_dv2.bump(4) for ft in range(32, NFT)}
    wtok = []
    stok = []

    with nc.Block() as blk:
        @blk.sync
        def _(sy):
            E.p2_xntok = E.dx.issue(
                sy, xn_sb[:].rearrange("p (a t) -> p a t", a=NHT), E.xn_sc[:, :, :].transpose([1, 0, 2]))

        @blk.gpsimd
        def _(g):
            E.p2_wdtok = E.dgp.issue(
                g, wdt_sb[:].rearrange("p (a f) -> p a f", a=NHT),
                E.winT[:, PROJ - 64:PROJ].rearrange("(a p) f -> p a f", p=128))
            for ft in range(NFT):
                if ft >= 2:
                    g.wait_ge(E.s_pe.h, mmd[ft - 2])
                wtok.append(E.dgp.issue(
                    g, wtb[ft % 2][:].rearrange("p (a f) -> p a f", a=NHT),
                    E.winT[:, ft * 128:(ft + 1) * 128].rearrange(
                        "(a p) f -> p a f", p=128)))

        @blk.vector
        def _(v):
            v.memset(cvs[0][:, 0:3], 0.0)
            v.memset(cvs[1][:, 0:3], 0.0)
            for ft in range(32, NFT):
                p = ft % 2
                if ft >= 34:
                    v.wait_ge(E.s_pe2.h, cvd[ft - 2])
                for k in range(4):
                    v.tensor_scalar(diag[p][:, k * 128:(k + 1) * 128],
                                    E.ident_b[:],
                                    E.convw_sb[:, (ft - 32) * 4 + k:(ft - 32) * 4 + k + 1],
                                    None, ALU.mult).then_inc(E.s_dv2.h, 1)
                v.wait_ge(E.s_pe.h, mmd[ft])
                if ft >= 34:
                    v.wait_ge(E.s_ac2.h, sld[ft - 2])
                v.tensor_copy(cvs[p][:, 3:T + 3], ps_ip[:]).then_inc(E.s_dv.h, 1)

        @blk.tensor
        def _(t):
            DmaStream.wait(t, E.p2_xntok)
            for ft in range(NFT):
                p = ft % 2
                DmaStream.wait(t, wtok[ft])
                if ft >= 1:
                    kind, val = evg[ft - 1]
                    t.wait_ge(getattr(E, "s_" + kind).h, val)
                for tb in range(ntb):
                    for ht in range(NHT):
                        ins = t.matmul(
                            ps_ip[:, tb * TB:(tb + 1) * TB],
                            wtb[p][:, ht * 128:(ht + 1) * 128],
                            xn_sb[:, ht * T + tb * TB: ht * T + (tb + 1) * TB],
                            start=(ht == 0), stop=(ht == NHT - 1))
                ins.then_inc(E.s_pe.h, 1)
                if ft >= 32:
                    t.wait_ge(E.s_dv.h, evg[ft][1])
                    t.wait_ge(E.s_dv2.h, dgd[ft])
                    if ft >= 33:
                        t.wait_ge(E.s_ac2.h, sld[ft - 1])
                    for tb in range(ntb):
                        for k in range(4):
                            ins = t.matmul(
                                ps_cv[:, tb * TB:(tb + 1) * TB],
                                diag[p][:, k * 128:(k + 1) * 128],
                                cvs[p][:, tb * TB + k: tb * TB + k + TB],
                                start=(k == 0), stop=(k == 3))
                    ins.then_inc(E.s_pe2.h, 1)

        @blk.scalar
        def _(s):
            for ft in range(NFT):
                p = ft % 2
                if ft < 32:
                    s.wait_ge(E.s_pe.h, mmd[ft])
                    if ft >= 2:
                        DmaStream.wait(s, stok[ft - 2])
                    s.activation(gst[p][:], ps_ip[:], ACT.Silu) \
                        .then_inc(E.s_ac.h, 1)
                    s.drain()
                    stok.append(E.dst.issue(s, E.sg_sc[ft, :, :], gst[p][:]))
                else:
                    s.wait_ge(E.s_pe2.h, cvd[ft])
                    if ft >= 34:
                        DmaStream.wait(s, stok[ft - 2])
                    s.activation(hst[p][:], ps_cv[:], ACT.Silu,
                                 bias=E.convb_sb[:, ft - 32:ft - 31]) \
                        .then_inc(E.s_ac2.h, 1)
                    s.drain()
                    if ft < 64:
                        dst_ap = E.hs_sc[ft - 32, :, :]
                    elif ft == 64:
                        dst_ap = E.b_sc[:, :]
                    else:
                        dst_ap = E.c_sc[:, :]
                    stok.append(E.dst.issue(s, dst_ap, hst[p][:]))

    _quiesce_stores(nc, E)
    loop_st.close()
    dtw = e(nc.sbuf_tensor("p2_dtw", [128, 64], F32))
    ab3 = e(nc.sbuf_tensor("p2_ab", [128, 64], F32))
    rl = e(nc.sbuf_tensor("p2_rl", [128, 64], F32))
    atw = e(nc.sbuf_tensor("p2_at", [128, 64], F32))
    stw = e(nc.sbuf_tensor("p2_stw", [128, 64], F32))
    ps_dt = e(nc.psum_tensor("p2_psdt", [128, 64], F32))
    ps_sf = e(nc.psum_tensor("p2_pssf", [64, 128], F32))

    with nc.Block() as blk:
        dt_mm = [E.s_pe.bump() for _ in range(nck)]
        dt_z = [E.s_dv.bump() for _ in range(nck)]
        dt_sp = [E.s_ac.bump() for _ in range(nck)]
        dt_fin = [E.s_dv2.bump() for _ in range(nck)]
        s_mm = [E.s_pe2.bump() for _ in range(nck)]
        s_ev = [E.s_dv3.bump() for _ in range(nck)]
        sf_tp = [E.s_pe3.bump() for _ in range(nck)]
        sf_ev = [E.s_g2.bump() for _ in range(nck)]

        @blk.tensor
        def _(t):
            DmaStream.wait(t, E.p2_wdtok)
            for ck in range(nck):
                if ck >= 1:
                    t.wait_ge(E.s_dv3.h, s_ev[ck - 1])
                for ht in range(NHT):
                    ins = t.matmul(ps_dt[:],
                                   xn_sb[:, ht * T + ck * 128: ht * T + (ck + 1) * 128],
                                   wdt_sb[:, ht * 64:(ht + 1) * 64],
                                   start=(ht == 0), stop=(ht == NHT - 1))
                ins.then_inc(E.s_pe.h, 1)
                t.wait_ge(E.s_dv2.h, dt_fin[ck])
                ins = t.matmul(ps_dt[:], E.triu_f[:], atw[:], start=True,
                               stop=True)
                ins.then_inc(E.s_pe2.h, 1)
                t.wait_ge(E.s_dv3.h, s_ev[ck])
                if ck >= 1:
                    t.wait_ge(E.s_g2.h, sf_ev[ck - 1])
                t.transpose(ps_sf[:], stw[:], E.ident_f[:]).then_inc(E.s_pe3.h, 1)

        @blk.vector
        def _(v):
            for ck in range(nck):
                v.wait_ge(E.s_pe.h, dt_mm[ck])
                if ck >= 1:
                    v.wait_ge(E.s_ac.h, dt_sp[ck - 1])
                v.tensor_tensor(dtw[:], ps_dt[:], E.dtb_rep[:], ALU.add) \
                    .then_inc(E.s_dv.h, 1)
                v.wait_ge(E.s_ac.h, dt_sp[ck])
                v.drain()
                v.tensor_relu(rl[:], dtw[:])
                v.drain()
                v.tensor_tensor(E.dt_all[:], rl[:], ab3[:],
                                ALU.add)
                v.drain()
                v.tensor_copy(E.dtbf_all[:, ck * 64:(ck + 1) * 64],
                              E.dt_all[:])
                v.tensor_tensor(atw[:], E.dt_all[:],
                                E.nega_rep[:], ALU.mult).then_inc(E.s_dv2.h, 1)
                v.wait_ge(E.s_pe2.h, s_mm[ck])
                v.tensor_copy(stw[:], ps_dt[:]).then_inc(E.s_dv3.h, 1)
                v.wait_ge(E.s_pe3.h, sf_tp[ck])
                v.tensor_copy(E.sfm_all[:, ck * 128:(ck + 1) * 128], ps_sf[:]) \
                    .then_inc(E.s_g2.h, 1)

        @blk.scalar
        def _(s):
            for ck in range(nck):
                s.wait_ge(E.s_dv.h, dt_z[ck])
                if ck >= 1:
                    s.wait_ge(E.s_dv2.h, dt_fin[ck - 1])
                s.activation(ab3[:], dtw[:], ACT.Abs)
                s.drain()
                s.activation(ab3[:], ab3[:], ACT.Exp, scale=-1.0)
                s.drain()
                s.activation(ab3[:], ab3[:], ACT.Ln, bias=E.c_one[:]) \
                    .then_inc(E.s_ac.h, 1)

    _quiesce_stores(nc, E)
    ps_.close()


def _p3_scan(nc, st, E, with_p1=False):
    ps_ = ExitStack()
    e = ps_.enter_context
    T, nck = E.T, E.nck
    NB = 8

    hsb = [e(nc.sbuf_tensor(f"p3_hs{i}", [128, 4096], BF16)) for i in range(2)]
    sgb = [e(nc.sbuf_tensor(f"p3_sg{i}", [128, 4096], BF16)) for i in range(2)]
    bcb = [e(nc.sbuf_tensor(f"p3_bc{i}", [128, 256], BF16)) for i in range(2)]
    hs_tm = e(nc.sbuf_tensor("p3_hstm", [128, 4096], BF16))
    b_tm = e(nc.sbuf_tensor("p3_btm", [128, 128], BF16))
    cbtm = e(nc.sbuf_tensor("p3_cbtm", [128, 128], BF16))
    sneg = e(nc.sbuf_tensor("p3_sneg", [64, 128], F32))
    esr = e(nc.sbuf_tensor("p3_esr", [128, 1024], BF16))
    esl = e(nc.sbuf_tensor("p3_esl", [128, NH], F32))
    dexp = e(nc.sbuf_tensor("p3_dexp", [128, 1024], BF16))
    cbdt = e(nc.sbuf_tensor("p3_cbdt", [128, 1024], BF16))
    scq = [e(nc.sbuf_tensor(f"p3_scq{i}", [128, 1024], BF16)) for i in range(2)]
    cexq = [e(nc.sbuf_tensor(f"p3_cexq{i}", [128, 1024], BF16)) for i in range(2)]
    bdq = [e(nc.sbuf_tensor(f"p3_bdq{i}", [128, 1024], BF16)) for i in range(2)]
    wex = e(nc.sbuf_tensor("p3_wex", [128, 8], BF16))
    ssel = [e(nc.sbuf_tensor(f"p3_ssel{i}", [64, 1024], F32)) for i in range(2)]
    wcol = e(nc.sbuf_tensor("p3_wcol", [128, 8], BF16))
    s_all = e(nc.sbuf_tensor("p3_sall", [128, 4096], F32))
    s_bfc = e(nc.sbuf_tensor("p3_sbf", [128, 4096], BF16))
    yfm = e(nc.sbuf_tensor("p3_yfm", [128, 4096], BF16))
    psA = e(nc.psum_tensor("p3_psA", [128, 1024], F32))
    psB = e(nc.psum_tensor("p3_psB", [64, 1024], F32))
    psC = e(nc.psum_tensor("p3_psC", [128, 512], F32))

    htok, sgtok, bctok = [], [], []
    sttok = E.dst3.plan(E.nck)

    SPE, SDV, SAC, SGP = E.s_pe, E.s_dv, E.s_ac, E.s_g2
    sch = []
    for ck in range(nck):
        d = {}
        d["t_tp0"] = SPE.bump()
        d["v_etp0"] = SDV.bump()
        d["t_tp1"] = SPE.bump()
        d["v_sbf"] = SDV.bump()
        d["v_etp1"] = SDV.bump()
        d["t_cbt"] = SPE.bump()
        d["v_cbt"] = SDV.bump()
        d["v_sneg"] = SDV.bump()
        for s in range(NB + 1):
            if s < NB:
                d[f"t1_{s}"] = SPE.bump()
                d[f"t2_{s}"] = SPE.bump()
            if s >= 1:
                d[f"t5_{s - 1}"] = SPE.bump()
        for q in range(NB):
            d[f"a1_{q}"] = SAC.bump(2)
            d[f"a2_{q}"] = SAC.bump(2)
            if q >= 1:
                d[f"a3_{q - 1}"] = SAC.bump(2)
        d[f"a3_{NB - 1}"] = SAC.bump(2)
        for s in range(NB + 1):
            if s < NB:
                d[f"g0_{s}"] = SGP.bump()
            if s >= 1:
                d[f"g1_{s - 1}"] = SGP.bump()
        for q in range(NB):
            if q >= 1:
                d[f"v2_{q - 1}"] = SDV.bump(2)
            d[f"v1_{q}"] = SDV.bump(4)
        d[f"v2_{NB - 1}"] = SDV.bump(2)
        d["v_y2h0"] = SDV.bump(4)
        d["v_y2h1"] = SDV.bump(4)
        d["v_y2"] = SDV.bump(1)
        d["t_rms"] = SPE.bump()
        d["a_inv"] = SAC.bump()
        sch.append(d)

    def pget(dd, key, q, ck):
        if q >= 0:
            return sch[ck][f"{key}_{q}"]
        if ck >= 1:
            return sch[ck - 1][f"{key}_{q + NB}"]
        return None

    p1w = _P1Work(nc, e, E, nck) if (with_p1 and E.p1_nfb > 1) else None

    with nc.Block() as blk:
        @blk.sync
        def _(sy):
            for ck in range(nck):
                if ck >= 2:
                    sy.wait_ge(SPE.h, sch[ck - 2]["t_rms"])
                    sy.wait_ge(SDV.h, sch[ck - 2]["v_y2"])
                    sy.wait_ge(SAC.h, sch[ck - 2]["a_inv"])
                htok.append(E.dx.issue(
                    sy, hsb[ck % 2][:].rearrange("p (a i) -> p a i", a=32),
                    E.hs_sc[:, :, ck * 128:(ck + 1) * 128].transpose([1, 0, 2])))
                sgtok.append(E.dx2.issue(
                    sy, sgb[ck % 2][:].rearrange("p (a i) -> p a i", a=32),
                    E.sg_sc[:, :, ck * 128:(ck + 1) * 128].transpose([1, 0, 2])))
                bctok.append((
                    E.dcn.issue(sy, bcb[ck % 2][:, 0:128],
                                E.b_sc[:, ck * 128:(ck + 1) * 128]),
                    E.dcn.issue(sy, bcb[ck % 2][:, 128:256],
                                E.c_sc[:, ck * 128:(ck + 1) * 128])))
                if p1w:
                    for it in p1w.items_for(ck):
                        p1w.emit_sync(sy, it)

        @blk.gpsimd
        def _(g):
            for ck in range(nck):
                d = sch[ck]
                p = ck % 2
                p1it = p1w.items_for(ck) if p1w else []
                np1 = len(p1it)
                for s in range(NB + 1):
                    if s < NB:
                        q = s
                        hq = q * 8
                        pq = q % 2
                        t2p = pget(d, "t2", q - 2, ck)
                        if t2p is not None:
                            g.wait_ge(SPE.h, t2p)
                        i_v = E.i64[:, hq:hq + 8].unsqueeze(2) \
                            .to_broadcast([64, 8, 128])
                        s_v = E.sfm_all[0:64, ck * 128:(ck + 1) * 128] \
                            .unsqueeze(1).to_broadcast([64, 8, 128])
                        g.tensor_tensor(
                            ssel[pq][:].rearrange("p (a i) -> p a i", a=8),
                            i_v, s_v, ALU.mult).then_inc(SGP.h, 1)
                        g.drain()
                    if s >= 1:
                        q = s - 1
                        hq = q * 8
                        pq = q % 2
                        g.wait_ge(SAC.h, d[f"a1_{q}"])
                        t5p = pget(d, "t5", q - 2, ck)
                        if t5p is not None:
                            g.wait_ge(SPE.h, t5p)
                        c_v = bcb[p][:, 128:256].unsqueeze(1) \
                            .to_broadcast([128, 8, 128])
                        g.tensor_tensor(
                            cexq[pq][:].rearrange("p (a i) -> p a i", a=8),
                            c_v, esr[:].rearrange("p (a i) -> p a i", a=8),
                            ALU.mult).then_inc(SGP.h, 1)
                        g.drain()
                    if p1w and s < NB:
                        for it in p1w.gslice(ck * NB + s - 2):
                            p1w.emit_gp(g, it)
                if p1w and ck == nck - 1:
                    for gg in range(p1w.G - 2, p1w.G):
                        for it in p1w.gslice(gg):
                            p1w.emit_gp(g, it)

        @blk.tensor
        def _(t):
            for ck in range(nck):
                d = sch[ck]
                p = ck % 2
                p1it = p1w.items_for(ck) if p1w else []
                np1 = len(p1it)
                DmaStream.wait(t, htok[ck])
                DmaStream.wait(t, bctok[ck][0])
                DmaStream.wait(t, bctok[ck][1])
                if ck >= 1:
                    t.wait_ge(SAC.h, sch[ck - 1]["a_inv"])
                    t.wait_ge(SDV.h, sch[ck - 1]["v_y2"])
                    t.wait_ge(SAC.h, sch[ck - 1][f"a2_{NB - 1}"])
                    t.wait_ge(SDV.h, sch[ck - 1][f"v1_{NB - 1}"])
                for ft in range(16):
                    ins = t.transpose(psA[:].bitcast(BF16)
                                      [:, ft * 128:(ft + 1) * 128],
                                      hsb[p][:, ft * 128:(ft + 1) * 128],
                                      E.ident_b[:])
                ins.then_inc(SPE.h, 1)
                t.wait_ge(SDV.h, d["v_etp0"])
                for ft in range(16):
                    ins = t.transpose(psA[:].bitcast(BF16)
                                      [:, ft * 128:(ft + 1) * 128],
                                      hsb[p][:, 2048 + ft * 128:
                                             2048 + (ft + 1) * 128],
                                      E.ident_b[:])
                ins.then_inc(SPE.h, 1)
                t.wait_ge(SDV.h, d["v_etp1"])
                ins = t.transpose(psA[:].bitcast(BF16)[:, 0:128],
                                  bcb[p][:, 0:128], E.ident_b[:])
                t.matmul(psA[:, 64:192], bcb[p][:, 0:128],
                         bcb[p][:, 128:256], start=True, stop=True) \
                    .then_inc(SPE.h, 1)
                for s in range(NB + 1):
                    if s < NB:
                        q = s
                        pq = q % 2
                        a2p = pget(d, "a2", q - 1, ck)
                        if a2p is not None:
                            t.wait_ge(SAC.h, a2p)
                        if q == 0:
                            t.wait_ge(SDV.h, d["v_cbt"])
                        t.wait_ge(SGP.h, d[f"g0_{q}"])
                        for bb in range(2):
                            ins = t.matmul(psA[:, bb * 512:(bb + 1) * 512],
                                           E.ones64[:],
                                           ssel[pq][:, bb * 512:(bb + 1) * 512],
                                           start=True, stop=True)
                        ins.then_inc(SPE.h, 1)
                    if s < NB:
                        q = s
                        hq = q * 8
                        pq = q % 2
                        t.wait_ge(SAC.h, d[f"a1_{q}"])
                        if q == 0:
                            t.wait_ge(SDV.h, d["v_sneg"])
                        for bb in range(2):
                            t.matmul(psA[:, bb * 512:(bb + 1) * 512],
                                     E.ones64[:],
                                     ssel[pq][:, bb * 512:(bb + 1) * 512],
                                     start=True, stop=False)
                        for bb in range(2):
                            iv = E.i64[:, hq + bb * 4:hq + bb * 4 + 4] \
                                .unsqueeze(2).to_broadcast([64, 4, 128])
                            t.matmul(psA[:, bb * 512:(bb + 1) * 512],
                                     sneg[:], iv, start=False, stop=False)
                        for bb in range(2):
                            rhm = E.negm_f[:].unsqueeze(1) \
                                .to_broadcast([128, 4, 128])
                            ins = t.matmul(psA[:, bb * 512:(bb + 1) * 512],
                                           E.ident_f[:], rhm,
                                           start=False, stop=True)
                        ins.then_inc(SPE.h, 1)
                    if s >= 1:
                        q = s - 1
                        hq = q * 8
                        pq = q % 2
                        t.wait_ge(SDV.h, d[f"v1_{q}"])
                        t.wait_ge(SGP.h, d[f"g1_{q}"])
                        if q == 0:
                            t.wait_ge(SDV.h, d["v_sbf"])
                        a3p = pget(d, "a3", q - 1, ck)
                        if a3p is not None:
                            t.wait_ge(SAC.h, a3p)
                        v2p = pget(d, "v2", q - 1, ck)
                        if v2p is not None:
                            t.wait_ge(SDV.h, v2p)
                        for hh in range(8):
                            h = hq + hh
                            ysl = psB[0:64, hh * 128:(hh + 1) * 128]
                            t.matmul(ysl, s_bfc[:, h * 64:(h + 1) * 64],
                                     cexq[pq][:, hh * 128:(hh + 1) * 128],
                                     start=(hh in (0, 4)), stop=False)
                            t.matmul(ysl, hs_tm[:, h * 64:(h + 1) * 64],
                                     scq[pq][:, hh * 128:(hh + 1) * 128],
                                     start=False, stop=(hh in (3, 7)))
                            dsl = psC[:, hh * 64:(hh + 1) * 64]
                            ins = t.matmul(dsl,
                                           bdq[pq][:, hh * 128:(hh + 1) * 128],
                                           hs_tm[:, h * 64:(h + 1) * 64],
                                           start=(hh == 0), stop=(hh == 7))
                        ins.then_inc(SPE.h, 1)
                    if p1w and s < NB - 1:
                        for it in p1w.gslice(ck * NB + s):
                            p1w.emit_pe(t, it)
                last_sl = p1w.gslice(ck * NB + NB - 1) if p1w else []
                for it in last_sl[:1]:
                    p1w.emit_pe(t, it)
                for hf in range(2):
                    t.wait_ge(SDV.h, d[f"v_y2h{hf}"])
                    for ft in range(hf * 16, hf * 16 + 16):
                        ins = t.matmul(psA[:, 0:1],
                                       sgb[p][:, ft * 128:(ft + 1) * 128],
                                       E.ones_b[:],
                                       start=(ft == 0), stop=(ft == 31))
                ins.then_inc(SPE.h, 1)
                for it in last_sl[1:]:
                    p1w.emit_pe(t, it)

        @blk.vector
        def _(v):
            for ck in range(nck):
                d = sch[ck]
                p = ck % 2
                p1it = p1w.items_for(ck) if p1w else []
                np1 = len(p1it)
                if ck == 0:
                    v.memset(s_all[:], 0.0)
                    v.drain()
                if p1w:
                    for qq in (0, 1):
                        for it in p1w.gslice(ck * NB + qq - 2):
                            p1w.emit_dv(v, it)
                v.wait_ge(SPE.h, d["t_tp0"])
                v.tensor_copy(hs_tm[:, 0:2048], psA[:].bitcast(BF16)) \
                    .then_inc(SDV.h, 1)
                v.wait_ge(SPE.h, d["t_tp1"])
                v.tensor_copy(s_bfc[:], s_all[:]).then_inc(SDV.h, 1)
                v.tensor_copy(hs_tm[:, 2048:4096], psA[:].bitcast(BF16)) \
                    .then_inc(SDV.h, 1)
                v.wait_ge(SPE.h, d["t_cbt"])
                v.tensor_copy(b_tm[:], psA[:].bitcast(BF16)[:, 0:128])
                v.tensor_tensor(cbtm[:], psA[:, 64:192], E.triu_f[:],
                                ALU.mult).then_inc(SDV.h, 1)
                v.tensor_scalar(sneg[:], E.sfm_all[0:64, ck * 128:(ck + 1) * 128],
                                -1.0, None, ALU.mult).then_inc(SDV.h, 1)
                def emit_ssl(qq):
                    hq2 = qq * 8
                    v.wait_ge(SPE.h, d[f"t5_{qq}"])
                    esl_v = esl[:, hq2:hq2 + 8].unsqueeze(2) \
                        .to_broadcast([128, 8, 64])
                    ssl = s_all[:, hq2 * 64:hq2 * 64 + 512]
                    v.tensor_tensor(ssl.rearrange("p (a i) -> p a i", a=8),
                                    ssl.rearrange("p (a i) -> p a i", a=8),
                                    esl_v, ALU.mult).then_inc(SDV.h, 1)
                    v.drain()
                    v.tensor_tensor(ssl, ssl, psC[:], ALU.add) \
                        .then_inc(SDV.h, 1)
                    v.drain()

                for q in range(NB):
                    if p1w and q >= 2:
                        for it in p1w.gslice(ck * NB + q - 2):
                            p1w.emit_dv(v, it)
                    if q >= 1:
                        emit_ssl(q - 1)
                    hq = q * 8
                    pq = q % 2
                    v.wait_ge(SAC.h, d[f"a2_{q}"])
                    t5p = pget(d, "t5", q - 2, ck)
                    if t5p is not None:
                        v.wait_ge(SPE.h, t5p)
                    dtv_b = E.dtbf_all[:, ck * NH + hq:ck * NH + hq + 8] \
                        .unsqueeze(2).to_broadcast([128, 8, 128])
                    cb_v = cbtm[:].unsqueeze(1).to_broadcast([128, 8, 128])
                    v.tensor_tensor(cbdt[:].rearrange("p (a i) -> p a i", a=8),
                                    cb_v, dtv_b, ALU.mult).then_inc(SDV.h, 1)
                    v.drain()
                    v.tensor_tensor(scq[pq][:], dexp[:], cbdt[:], ALU.mult) \
                        .then_inc(SDV.h, 1)
                    v.tensor_tensor(wcol[:], wex[:],
                                    E.dtbf_all[:, ck * NH + hq:ck * NH + hq + 8],
                                    ALU.mult).then_inc(SDV.h, 1)
                    v.drain()
                    wc_v = wcol[:].unsqueeze(2).to_broadcast([128, 8, 128])
                    bt_v = b_tm[:].unsqueeze(1).to_broadcast([128, 8, 128])
                    v.tensor_tensor(bdq[pq][:].rearrange("p (a i) -> p a i", a=8),
                                    bt_v, wc_v, ALU.mult).then_inc(SDV.h, 1)
                emit_ssl(NB - 1)
                DmaStream.wait(v, sgtok[ck])
                v.wait_ge(SAC.h, d[f"a3_{NB - 1}"])
                for hf in range(2):
                    h0, h1 = hf * 2048, (hf + 1) * 2048
                    df_v = E.dfull[:, hf * 16:(hf + 1) * 16].unsqueeze(2) \
                        .to_broadcast([128, 16, 128])
                    v.tensor_tensor(
                        hsb[p][:, h0:h1].rearrange("p (a i) -> p a i", a=16),
                        hsb[p][:, h0:h1].rearrange("p (a i) -> p a i", a=16),
                        df_v, ALU.mult).then_inc(SDV.h, 1)
                    v.drain()
                    v.tensor_tensor(yfm[:, h0:h1], yfm[:, h0:h1],
                                    hsb[p][:, h0:h1], ALU.add) \
                        .then_inc(SDV.h, 1)
                    v.drain()
                    v.tensor_tensor(yfm[:, h0:h1], yfm[:, h0:h1],
                                    sgb[p][:, h0:h1], ALU.mult) \
                        .then_inc(SDV.h, 1)
                    v.drain()
                    v.tensor_tensor(sgb[p][:, h0:h1], yfm[:, h0:h1],
                                    yfm[:, h0:h1], ALU.mult) \
                        .then_inc(SDV.h, 1)
                    v.drain()
                nw_v = E.nw_bf[:].unsqueeze(2).to_broadcast([128, 32, 128])
                v.tensor_tensor(yfm[:].rearrange("p (a i) -> p a i", a=32),
                                yfm[:].rearrange("p (a i) -> p a i", a=32),
                                nw_v, ALU.mult).then_inc(SDV.h, 1)
                if p1w and ck == nck - 1:
                    for g in range(p1w.G - 2, p1w.G):
                        for it in p1w.gslice(g):
                            p1w.emit_dv(v, it)

        @blk.scalar
        def _(s):
            for ck in range(nck):
                d = sch[ck]
                p1it = p1w.items_for(ck) if p1w else []
                np1 = len(p1it)
                for q in range(NB):
                    if p1w:
                        for it in p1w.gslice(ck * NB + q - 1):
                            p1w.emit_ac(s, it)
                    hq = q * 8
                    pq = q % 2
                    s.wait_ge(SPE.h, d[f"t1_{q}"])
                    g1p = pget(d, "g1", q - 1, ck)
                    if g1p is not None:
                        s.wait_ge(SGP.h, g1p)
                    v1p = pget(d, "v1", q - 1, ck)
                    if v1p is not None:
                        s.wait_ge(SDV.h, v1p)
                    if ck >= 1:
                        s.wait_ge(SDV.h, sch[ck - 1][f"v2_{q}"])
                    s.activation(esr[:], psA[:], ACT.Exp)
                    s.activation(esl[:, hq:hq + 8], psA[:, 127::128],
                                 ACT.Exp).then_inc(SAC.h, 2)
                    s.wait_ge(SPE.h, d[f"t2_{q}"])
                    s.activation(dexp[:], psA[:], ACT.Exp)
                    s.activation(wex[:], psA[:, 127::128], ACT.Exp) \
                        .then_inc(SAC.h, 2)
                    qq = q - 1
                    if qq >= 0:
                        if qq == 0 and ck >= 1:
                            DmaStream.wait(s, sttok[ck - 1])
                        _p3_yevac(s, E, d, yfm, psB, qq, SPE, SAC, None)
                _p3_yevac(s, E, d, yfm, psB, NB - 1, SPE, SAC, None)
                if p1w and ck == nck - 1:
                    for g in range(p1w.G - 1, p1w.G):
                        for it in p1w.gslice(g):
                            p1w.emit_ac(s, it)
                s.wait_ge(SPE.h, d["t_rms"])
                s.activation(E.inv_at[:, ck:ck + 1], psA[:, 0:1], ACT.Ln,
                             bias=E.c_epsat[:], scale=1.0 / INTER)
                s.drain()
                s.activation(E.inv_at[:, ck:ck + 1], E.inv_at[:, ck:ck + 1],
                             ACT.Exp, scale=-0.5).then_inc(SAC.h, 1)
                s.wait_ge(SDV.h, d["v_y2"])
                tk = E.dst3.issue(
                    s, E.ynw_sc[ck].transpose([1, 0, 2]),
                    yfm[:].rearrange("p (a i) -> p a i", a=32))
                assert tk == sttok[ck], (tk, sttok[ck])

    _quiesce_stores(nc, E)
    ps_.close()


class _P1Work:

    def __init__(self, nc, e, E, nck):
        self.nc, self.E = nc, E
        TBF = E.p1_TBF
        self.TBF = TBF
        self.nttf = TBF // 128
        self.h_sc = nc.dram_tensor("p13_h", [NIT_DN, 128, TBF], BF16).ap()
        self.xnf = e(nc.sbuf_tensor("q1_xn", [128, NHT * TBF], BF16))
        self.wtg = [e(nc.sbuf_tensor(f"q1_wg{i}", [128, 1024], BF16))
                    for i in range(2)]
        self.wtu = [e(nc.sbuf_tensor(f"q1_wu{i}", [128, 1024], BF16))
                    for i in range(2)]
        self.sgst = [e(nc.sbuf_tensor(f"q1_sg{i}", [128, TBF], BF16))
                     for i in range(2)]
        self.hstg = [e(nc.sbuf_tensor(f"q1_hs{i}", [128, TBF], BF16))
                     for i in range(2)]
        self.wdn = e(nc.sbuf_tensor("q1_wdn", [128, NIT_DN * 512], BF16))
        self.hslab = [e(nc.sbuf_tensor(f"q1_hl{i}", [128, 22 * 128], BF16))
                      for i in range(2)]
        self.sqb = [e(nc.sbuf_tensor(f"q1_sqb{i}", [128, 512], F32))
                    for i in range(2)]
        self.sscol = e(nc.sbuf_tensor("q1_ssc", [128, 1], F32))
        self.ssq = e(nc.sbuf_tensor("q1_ssq", [128, 4], F32))
        self.fst = [e(nc.sbuf_tensor(f"q1_fs{i}", [128, 512], BF16))
                    for i in range(2)]
        self.pool = e(nc.psum_tensor("q1_pool", [128, 1536], F32))
        self.slot_state = [None, None, None]
        self.slot_i = 0

        items = []
        pending_inv = None
        for fb in range(1, E.p1_nfb):
            items.append(("xnf", fb))
            if pending_inv is not None:
                items.append(pending_inv)
                pending_inv = None
            for j in range(NPAIR_GU):
                items.append(("gu", fb, j))
            for oq in range(4):
                items.append(("dnl", fb, oq))
                for tt in range(self.nttf):
                    items.append(("dn", fb, oq, tt))
            pending_inv = ("inv", fb)
        if pending_inv is not None:
            items.append(("pad", 0))
            items.append(pending_inv)
        self.items = items
        self.G = nck * 8
        self.slices = []
        n = len(items)
        for ck in range(nck):
            lo = n * ck // nck
            hi = n * (ck + 1) // nck
            self.slices.append(items[lo:hi])

        QPE, QAC, QDV = E.s_qpe, E.s_qac, E.s_qdv
        self.sch = {}
        ndst2 = sum(1 for it in items if it[0] in ("gu", "dn"))
        self.d2tok = E.dst2.plan(ndst2)
        d2i = 0
        si = 0
        for it in items:
            d = {}
            if it[0] == "gu":
                d["slot_g"] = si % 3
                d["slot_u"] = (si + 1) % 3
                si += 2
                d["pe_gh"] = QPE.bump()
                d["pe_g"] = QPE.bump()
                d["pe_uh"] = QPE.bump()
                d["pe_u"] = QPE.bump()
                d["silu"] = QAC.bump()
                d["ucp"] = QAC.bump()
                d["htt"] = QDV.bump()
                d["hst"] = self.d2tok[d2i]
                d2i += 1
            elif it[0] == "dn":
                d["slot"] = si % 3
                si += 1
                d["pe_h"] = QPE.bump()
                d["pe"] = QPE.bump()
                d["sq"] = QAC.bump()
                d["fcp"] = QAC.bump()
                d["dv"] = QDV.bump(2)
                d["fst"] = self.d2tok[d2i]
                d2i += 1
            elif it[0] == "inv":
                d["inv"] = QAC.bump()
            self.sch[it] = d
        self.toks = {}
        self.xnf_tok = {}

    def items_for(self, ck):
        return self.slices[ck]

    def slot(self, i):
        return self.pool[:, 512 * i:512 * (i + 1)]

    def wait_slot(self, eng, i):
        st_ = self.slot_state[i]
        if st_ is not None:
            for sem, val in st_:
                eng.wait_ge(sem.h, val)

    def gslice(self, g):
        if g < 0 or g >= self.G:
            return []
        n = len(self.items)
        return self.items[n * g // self.G:n * (g + 1) // self.G]

    def emit_sync(self, sy, it):
        E = self.E
        if it[0] == "xnf":
            fb = it[1]
            if fb >= 2:
                pg = self.sch[("gu", fb - 1, NPAIR_GU - 1)]
                sy.wait_ge(E.s_qpe.h, pg["pe_u"])
            self.xnf_tok[fb] = E.dxp.issue(
                sy, self.xnf[:].rearrange("p (a t) -> p a t", a=NHT),
                E.xn_sc[:, :, fb * self.TBF:(fb + 1) * self.TBF]
                .transpose([1, 0, 2]))
        elif it[0] == "gu":
            fb, j = it[1], it[2]
            if j >= 1:
                dp = self.sch[("gu", fb, j - 1)]
            elif fb >= 2:
                dp = self.sch[("gu", fb - 1, NPAIR_GU - 1)]
            else:
                dp = None
            for half, gkey, wt, wsrc, strm in (
                    (0, "pe_gh", self.wtg, E.wgu_sc[j], E.dxp2),
                    (1, "pe_g", self.wtg, E.wgu_sc[j], E.dxp2),
                    (0, "pe_uh", self.wtu, E.wgu_sc[j + NPAIR_GU], E.dxp3),
                    (1, "pe_u", self.wtu, E.wgu_sc[j + NPAIR_GU], E.dxp3)):
                if dp is not None:
                    sy.wait_ge(E.s_qpe.h, dp[gkey])
                self.toks[(gkey, fb, j)] = strm.issue(
                    sy, wt[half][:], wsrc[:, half * 1024:(half + 1) * 1024])
        elif it[0] == "dnl":
            fb, oq = it[1], it[2]
            if oq >= 1:
                sy.wait_ge(E.s_qpe.h, self.sch[("dn", fb, oq - 1,
                                                self.nttf - 1)]["pe"])
            elif fb >= 2:
                sy.wait_ge(E.s_qpe.h, self.sch[("dn", fb - 1, 3,
                                                self.nttf - 1)]["pe"])
            if oq == 0:
                DmaStream.wait(sy, self.sch[("gu", fb, NPAIR_GU - 1)]["hst"])
                DmaStream.wait(sy, self.sch[("gu", fb, NPAIR_GU - 2)]["hst"])
            self.toks[("wdn", fb, oq)] = E.dxp.issue(
                sy, self.wdn[:].rearrange("p (a i) -> p a i", a=NIT_DN),
                E.wdn_sc[oq])
        elif it[0] == "dn":
            fb, oq, tt = it[1], it[2], it[3]
            k = oq * self.nttf + tt
            if k >= 1:
                oq2, tt2 = divmod(k - 1, self.nttf)
                dp = self.sch[("dn", fb, oq2, tt2)]
            elif fb >= 2:
                dp = self.sch[("dn", fb - 1, 3, self.nttf - 1)]
            else:
                dp = None
            hsrc = self.h_sc[:, :, tt * 128:(tt + 1) * 128].transpose([1, 0, 2])
            if dp is not None:
                sy.wait_ge(E.s_qpe.h, dp["pe_h"])
            self.toks[("hla", fb, oq, tt)] = E.dxp2.issue(
                sy, self.hslab[0][:].rearrange("p (a i) -> p a i", a=22),
                hsrc[:, 0:22])
            if dp is not None:
                sy.wait_ge(E.s_qpe.h, dp["pe"])
            self.toks[("hlb", fb, oq, tt)] = E.dxp2.issue(
                sy, self.hslab[1][:].rearrange("p (a i) -> p a i", a=22),
                hsrc[:, 22:44])

    def emit_pe(self, t, it):
        E = self.E
        if it[0] == "gu":
            fb, j = it[1], it[2]
            d = self.sch[it]
            DmaStream.wait(t, self.xnf_tok[fb])
            for slot_key, wt, keys in ((d["slot_g"], self.wtg,
                                        ("pe_gh", "pe_g")),
                                       (d["slot_u"], self.wtu,
                                        ("pe_uh", "pe_u"))):
                self.wait_slot(t, slot_key)
                for half in range(2):
                    DmaStream.wait(t, self.toks[(keys[half], fb, j)])
                    for h2 in range(NHT // 2):
                        ht = half * (NHT // 2) + h2
                        ins = t.matmul(
                            self.slot(slot_key),
                            wt[half][:, h2 * 128:(h2 + 1) * 128],
                            self.xnf[:, ht * self.TBF:(ht + 1) * self.TBF],
                            start=(ht == 0), stop=(ht == NHT - 1))
                    ins.then_inc(E.s_qpe.h, 1)
            self.slot_state[d["slot_g"]] = [(E.s_qac, d["silu"])]
            self.slot_state[d["slot_u"]] = [(E.s_qac, d["ucp"])]
        elif it[0] == "dn":
            fb, oq, tt = it[1], it[2], it[3]
            k = oq * self.nttf + tt
            d = self.sch[it]
            DmaStream.wait(t, self.toks[("wdn", fb, oq)])
            self.wait_slot(t, d["slot"])
            DmaStream.wait(t, self.toks[("hla", fb, oq, tt)])
            for i2 in range(22):
                ins = t.matmul(self.slot(d["slot"]),
                               self.hslab[0][:, i2 * 128:(i2 + 1) * 128],
                               self.wdn[:, i2 * 512:(i2 + 1) * 512],
                               start=(i2 == 0), stop=False)
            ins.then_inc(E.s_qpe.h, 1)
            DmaStream.wait(t, self.toks[("hlb", fb, oq, tt)])
            for i2 in range(22, NIT_DN):
                ins = t.matmul(self.slot(d["slot"]),
                               self.hslab[1][:, (i2 - 22) * 128:
                                             (i2 - 21) * 128],
                               self.wdn[:, i2 * 512:(i2 + 1) * 512],
                               start=False, stop=(i2 == NIT_DN - 1))
            ins.then_inc(E.s_qpe.h, 1)
            self.slot_state[d["slot"]] = [(E.s_qac, d["fcp"])]

    def emit_ac(self, s, it):
        E = self.E
        if it[0] == "gu":
            fb, j = it[1], it[2]
            d = self.sch[it]
            s.wait_ge(E.s_qpe.h, d["pe_g"])
            if j >= 2:
                s.wait_ge(E.s_qdv.h, self.sch[("gu", fb, j - 2)]["htt"])
            elif fb >= 2:
                s.wait_ge(E.s_qdv.h,
                          self.sch[("gu", fb - 1, NPAIR_GU - 2 + j)]["htt"])
            s.activation(self.sgst[j % 2][:], self.slot(d["slot_g"]),
                         ACT.Silu).then_inc(E.s_qac.h, 1)
            s.wait_ge(E.s_qpe.h, d["pe_u"])
            if j >= 2:
                DmaStream.wait(s, self.sch[("gu", fb, j - 2)]["hst"])
            elif fb >= 2:
                DmaStream.wait(
                    s, self.sch[("gu", fb - 1, NPAIR_GU - 2 + j)]["hst"])
            s.activation(self.hstg[j % 2][:], self.slot(d["slot_u"]),
                         ACT.Copy).then_inc(E.s_qac.h, 1)
        elif it[0] == "dn":
            fb, oq, tt = it[1], it[2], it[3]
            d = self.sch[it]
            s.wait_ge(E.s_qpe.h, d["pe"])
            k = oq * self.nttf + tt
            if k >= 2:
                oq2, tt2 = divmod(k - 2, self.nttf)
                s.wait_ge(E.s_qdv.h, self.sch[("dn", fb, oq2, tt2)]["dv"])
            elif fb >= 2:
                kp = 4 * self.nttf + (k - 2)
                oq2, tt2 = divmod(kp, self.nttf)
                s.wait_ge(E.s_qdv.h, self.sch[("dn", fb - 1, oq2, tt2)]["dv"])
            s.activation(self.sqb[k % 2][:], self.slot(d["slot"]), ACT.Square) \
                .then_inc(E.s_qac.h, 1)
            if k >= 2:
                oq2, tt2 = divmod(k - 2, self.nttf)
                DmaStream.wait(s, self.sch[("dn", fb, oq2, tt2)]["fst"])
            elif fb >= 2:
                kp = 4 * self.nttf + (k - 2)
                oq2, tt2 = divmod(kp, self.nttf)
                DmaStream.wait(s, self.sch[("dn", fb - 1, oq2, tt2)]["fst"])
            s.activation(self.fst[k % 2][:], self.slot(d["slot"]), ACT.Copy) \
                .then_inc(E.s_qac.h, 1)
        elif it[0] == "inv":
            fb = it[1]
            d = self.sch[it]
            for tt2 in range(self.nttf):
                s.wait_ge(E.s_qdv.h, self.sch[("dn", fb, 3, tt2)]["dv"])
            t0 = fb * self.nttf
            s.activation(E.inv_ff[:, t0:t0 + self.nttf],
                         self.ssq[:, 0:self.nttf], ACT.Ln,
                         bias=E.c_epsff[:], scale=1.0 / H)
            s.drain()
            s.activation(E.inv_ff[:, t0:t0 + self.nttf],
                         E.inv_ff[:, t0:t0 + self.nttf],
                         ACT.Exp, scale=-0.5).then_inc(E.s_qac.h, 1)

    def emit_dv(self, v, it):
        E = self.E
        if it[0] == "gu":
            fb, j = it[1], it[2]
            d = self.sch[it]
            v.wait_ge(E.s_qac.h, d["ucp"])
            v.tensor_tensor(self.hstg[j % 2][:], self.hstg[j % 2][:],
                            self.sgst[j % 2][:], ALU.mult) \
                .then_inc(E.s_qdv.h, 1)
        elif it[0] == "dn":
            fb, oq, tt = it[1], it[2], it[3]
            k = oq * self.nttf + tt
            d = self.sch[it]
            v.wait_ge(E.s_qac.h, d["sq"])
            v.reduce_sum(self.sscol[:], self.sqb[k % 2][:], axis=AX.X) \
                .then_inc(E.s_qdv.h, 1)
            v.drain()
            if oq == 0:
                if fb >= 2 and tt == 0:
                    iv = self.sch[("inv", fb - 1)]
                    v.wait_ge(E.s_qac.h, iv["inv"])
                v.tensor_copy(self.ssq[:, tt:tt + 1], self.sscol[:]) \
                    .then_inc(E.s_qdv.h, 1)
            else:
                v.tensor_tensor(self.ssq[:, tt:tt + 1], self.ssq[:, tt:tt + 1],
                                self.sscol[:], ALU.add).then_inc(E.s_qdv.h, 1)
            v.drain()

    def emit_gp(self, g, it):
        E = self.E
        if it[0] == "gu":
            fb, j = it[1], it[2]
            d = self.sch[it]
            g.wait_ge(E.s_qdv.h, d["htt"])
            if fb >= 2 and j == 0:
                DmaStream.wait(g, self.toks[("hlb", fb - 1, 3, self.nttf - 1)])
            tk = E.dst2.issue(g, self.h_sc[j], self.hstg[j % 2][:])
            assert tk == d["hst"], (tk, d["hst"])
        elif it[0] == "dn":
            fb, oq, tt = it[1], it[2], it[3]
            d = self.sch[it]
            g.wait_ge(E.s_qac.h, d["fcp"])
            ttg = fb * self.nttf + tt
            tk = E.dst2.issue(
                g, E.ffn_sc[ttg, :, oq * 512:(oq + 1) * 512],
                self.fst[(oq * self.nttf + tt) % 2][:])
            assert tk == d["fst"], (tk, d["fst"])


def _p3_yevac(s, E, d, yfm, psB, q, SPE, SAC, dprev):
    hq = q * 8
    s.wait_ge(SPE.h, d[f"t5_{q}"])
    s.activation(
        yfm[0:64, hq * 64:hq * 64 + 512].rearrange("p (a i) -> p a i", a=4),
        psB[0:64, :].rearrange("p (a b i) -> p a b i", a=4, b=2)[:, :, 0, :],
        ACT.Copy)
    s.activation(
        yfm[64:128, hq * 64:hq * 64 + 512].rearrange("p (a i) -> p a i", a=4),
        psB[0:64, :].rearrange("p (a b i) -> p a b i", a=4, b=2)[:, :, 1, :],
        ACT.Copy).then_inc(SAC.h, 2)


def _p4_out(nc, st, E):
    ps_ = ExitStack()
    e = ps_.enter_context
    T, ntt = E.T, E.ntt

    wout = e(nc.sbuf_tensor("p4_w", [128, 32 * 1024], BF16))
    ysb = [e(nc.sbuf_tensor(f"p4_y{i}", [128, 32 * 128], BF16)) for i in range(2)]
    xsb = [e(nc.sbuf_tensor(f"p4_x{i}", [128, 1024], F32)) for i in range(2)]
    fsb = [e(nc.sbuf_tensor(f"p4_f{i}", [128, 1024], BF16)) for i in range(2)]
    st1 = e(nc.sbuf_tensor("p4_s1", [128, 512], F32))
    ost = [e(nc.sbuf_tensor(f"p4_o{i}", [128, 512], F32)) for i in range(4)]
    psO = e(nc.psum_tensor("p4_ps", [128, 2048], F32))

    SPE, SDV, SAC = E.s_pe, E.s_dv, E.s_ac
    otoks = E.dst.plan(2 * ntt * 2)

    for oh in range(2):
        mm_d = []
        ev_d = []
        for tt in range(ntt):
            for ob in range(2):
                mm_d.append(E.s_pe.bump())
                ev_d.append(E.s_dv.bump(2))
        wtok, ytok, xtok, ftok2 = [None], [], [], []

        with nc.Block() as blk:
            @blk.gpsimd
            def _(g):
                wtok[0] = E.dgp.issue(
                    g, wout[:].rearrange("p (a f) -> p a f", a=32),
                    E.woutT[:, oh * 1024:(oh + 1) * 1024].rearrange(
                        "(a p) f -> p a f", p=128))

            @blk.sync
            def _(sy):
                for tt in range(ntt):
                    if tt >= 2:
                        sy.wait_ge(SPE.h, mm_d[2 * (tt - 2) + 1])
                        sy.wait_ge(SDV.h, ev_d[2 * (tt - 2) + 1])
                    ytok.append(E.dx.issue(
                        sy, ysb[tt % 2][:].rearrange("p (a i) -> p a i", a=32),
                        E.ynw_sc[tt].transpose([1, 0, 2])))
                    xtok.append(E.dx2.issue(
                        sy, xsb[tt % 2][:],
                        E.x_in[tt * 128:(tt + 1) * 128,
                               oh * 1024:(oh + 1) * 1024]))
                    ftok2.append(E.dcn.issue(
                        sy, fsb[tt % 2][:],
                        E.ffn_sc[tt, :, oh * 1024:(oh + 1) * 1024]))

            @blk.tensor
            def _(t):
                DmaStream.wait(t, wtok[0])
                for tt in range(ntt):
                    DmaStream.wait(t, ytok[tt])
                    for ob in range(2):
                        k = tt * 2 + ob
                        slot = k % 4
                        if k >= 4:
                            t.wait_ge(SDV.h, ev_d[k - 4])
                        for ft in range(32):
                            ins = t.matmul(
                                psO[:, slot * 512:(slot + 1) * 512],
                                ysb[tt % 2][:, ft * 128:(ft + 1) * 128],
                                wout[:, ft * 1024 + ob * 512:
                                     ft * 1024 + ob * 512 + 512],
                                start=(ft == 0), stop=(ft == 31))
                        ins.then_inc(SPE.h, 1)

            @blk.vector
            def _(v):
                for tt in range(ntt):
                    DmaStream.wait(v, xtok[tt])
                    DmaStream.wait(v, ftok2[tt])
                    for ob in range(2):
                        k = tt * 2 + ob
                        slot = k % 4
                        gidx = oh * 2 * ntt + k
                        v.wait_ge(SPE.h, mm_d[k])
                        if gidx >= 4:
                            DmaStream.wait(v, otoks[gidx - 4])
                        v.scalar_tensor_tensor(
                            st1[:], psO[:, slot * 512:(slot + 1) * 512],
                            E.inv_at[:, tt:tt + 1],
                            xsb[tt % 2][:, ob * 512:(ob + 1) * 512],
                            ALU.mult, ALU.add).then_inc(SDV.h, 1)
                        v.drain()
                        v.scalar_tensor_tensor(
                            ost[k % 4][:], fsb[tt % 2][:, ob * 512:(ob + 1) * 512],
                            E.inv_ff[:, tt:tt + 1], st1[:],
                            ALU.mult, ALU.add).then_inc(SDV.h, 1)
                        v.drain()

            @blk.scalar
            def _(s):
                for tt in range(ntt):
                    for ob in range(2):
                        k = tt * 2 + ob
                        gidx = oh * 2 * ntt + k
                        s.wait_ge(SDV.h, ev_d[k])
                        tk = E.dst.issue(
                            s, E.out[tt * 128:(tt + 1) * 128,
                                     oh * 1024 + ob * 512:
                                     oh * 1024 + (ob + 1) * 512],
                            ost[k % 4][:])
                        assert tk == otoks[gidx]

    with nc.Block() as blk:
        @blk.sync
        def _(sy):
            for tk in otoks[-3:]:
                DmaStream.wait(sy, tk)
    ps_.close()


_prog_cache = {}


def _host_prep(inputs):
    w = {}
    w["winT"] = np.ascontiguousarray(np.asarray(inputs["L0_attn_in_proj_w"]).T)
    w["woutT"] = np.ascontiguousarray(np.asarray(inputs["L0_attn_out_proj_w"]).T)
    cw = np.asarray(inputs["L0_attn_conv_w"]).reshape(34, 128, 4)
    w["convw"] = np.ascontiguousarray(cw.transpose(1, 0, 2).reshape(128, 136))
    w["convb"] = np.ascontiguousarray(
        np.asarray(inputs["L0_attn_conv_b"]).reshape(34, 128).T)
    w["dtb"] = np.asarray(inputs["L0_attn_dt_bias"]).reshape(1, 64)
    w["alog"] = np.asarray(inputs["L0_attn_A_log"]).reshape(1, 64)
    w["dvec"] = np.asarray(inputs["L0_attn_D"]).reshape(1, 64)
    w["nw"] = np.ascontiguousarray(
        np.asarray(inputs["L0_attn_norm_weight"]).reshape(32, 128).T)
    w["gup"] = np.asarray(inputs["L0_ffn_gate_up_packed"])
    w["gus"] = np.ascontiguousarray(np.asarray(inputs["L0_ffn_gate_up_scales"]).T)
    w["dnp"] = np.asarray(inputs["L0_ffn_down_packed"])
    w["dns"] = np.ascontiguousarray(np.asarray(inputs["L0_ffn_down_scales"]).T)
    return {k: np.ascontiguousarray(v) for k, v in w.items()}


def kernel(**inputs):
    x = np.asarray(inputs["L0_attn_hidden_states"])
    B = x.shape[0]
    w = _host_prep(inputs)
    key = ("full", 2048)
    if key not in _prog_cache:
        _prog_cache[key] = build(T=2048)
    nc = _prog_cache[key]
    in_maps = []
    for c in range(B):
        m = {"x_in": np.ascontiguousarray(np.asarray(x[c], dtype=np.float32))}
        m.update(w)
        in_maps.append(m)
    res = run_bass_kernel_spmd(nc, in_maps, core_ids=list(range(B)))
    outs = [res.results[c]["out"] for c in range(B)]
    return np.stack(outs, axis=0).astype(np.float32)


# version 23
# speedup vs baseline: 1.1250x; 1.0639x over previous
import numpy as np
from contextlib import ExitStack

import concourse.bass as bass
import concourse.mybir as mybir
from concourse.bass_utils import run_bass_kernel_spmd

F32 = mybir.dt.float32
BF16 = mybir.dt.bfloat16
F16 = mybir.dt.float16
I32 = mybir.dt.int32
U16 = mybir.dt.uint16
ALU = mybir.AluOpType
ACT = mybir.ActivationFunctionType
AX = mybir.AxisListType

H = 2048
PROJ = 8512
INTER = 4096
CONVD = 4352
NH = 64
HD = 64
NST = 128
FI = 5632
GUD = 11264
L = 128
NHT = H // 128
NFT_G = INTER // 128
NOT_GU = GUD // 128
NPAIR_GU = FI // 128
NOT_DN = H // 128
NIT_DN = FI // 128


class Sem:
    def __init__(self, st, nc, name):
        self.h = st.enter_context(nc.semaphore(name))
        self.n = 0

    def bump(self, k=1):
        self.n += k
        return self.n


class DmaStream:

    def __init__(self, st, nc, name, depth=2):
        self.sems = [Sem(st, nc, f"{name}{i}") for i in range(depth)]
        self.i = 0

    def issue(self, eng, out, in_, **kw):
        s = self.sems[self.i % len(self.sems)]
        self.i += 1
        if s.n > 0:
            eng.wait_ge(s.h, s.n)
        eng.dma_start(out, in_, **kw).then_inc(s.h, 16)
        s.n += 16
        return (s, s.n)

    def plan(self, n):
        sims = [s.n for s in self.sems]
        toks = []
        for k in range(n):
            j = (self.i + k) % len(self.sems)
            sims[j] += 16
            toks.append((self.sems[j], sims[j]))
        return toks

    @staticmethod
    def wait(eng, tok):
        eng.wait_ge(tok[0].h, tok[1])


class Env:
    pass


def _quiesce_stores(nc, E):
    with nc.Block() as blk:
        @blk.sync
        def _(sy):
            for strm in (E.dst, E.dst2, E.dst3):
                for s in strm.sems:
                    if s.n > 0:
                        sy.wait_ge(s.h, s.n)


def build(T=2048, phases=(0, 1, 2, 3, 4)):
    nck = T // L
    ntt = T // 128
    nc = bass.Bass("TRN2", target_bir_lowering=False)
    E = Env()
    E.T, E.nck, E.ntt = T, nck, ntt

    E.x_in = nc.dram_tensor("x_in", [T, H], F32, kind="ExternalInput").ap()
    E.winT = nc.dram_tensor("winT", [H, PROJ], F32, kind="ExternalInput").ap()
    E.woutT = nc.dram_tensor("woutT", [INTER, H], F32, kind="ExternalInput").ap()
    E.convw = nc.dram_tensor("convw", [128, 136], F32, kind="ExternalInput").ap()
    E.convb = nc.dram_tensor("convb", [128, 34], F32, kind="ExternalInput").ap()
    E.dtb_in = nc.dram_tensor("dtb", [1, NH], F32, kind="ExternalInput").ap()
    E.alog_in = nc.dram_tensor("alog", [1, NH], F32, kind="ExternalInput").ap()
    E.dvec_in = nc.dram_tensor("dvec", [1, NH], F32, kind="ExternalInput").ap()
    E.nw_in = nc.dram_tensor("nw", [128, 32], F32, kind="ExternalInput").ap()
    E.gup_in = nc.dram_tensor("gup", [GUD, H // 8], I32, kind="ExternalInput").ap()
    E.gus_in = nc.dram_tensor("gus", [GUD, 16], F16, kind="ExternalInput").ap()
    E.dnp_in = nc.dram_tensor("dnp", [H, FI // 8], I32, kind="ExternalInput").ap()
    E.dns_in = nc.dram_tensor("dns", [H, 44], F16, kind="ExternalInput").ap()
    E.out = nc.dram_tensor("out", [T, H], F32, kind="ExternalOutput").ap()

    E.xn_sc = nc.dram_tensor("xn_sc", [NHT, 128, T], BF16).ap()
    E.sg_sc = nc.dram_tensor("sg_sc", [NFT_G, 128, T], BF16).ap()
    E.hs_sc = nc.dram_tensor("hs_sc", [NFT_G, 128, T], BF16).ap()
    E.b_sc = nc.dram_tensor("b_sc", [128, T], BF16).ap()
    E.c_sc = nc.dram_tensor("c_sc", [128, T], BF16).ap()
    E.ynw_sc = nc.dram_tensor("ynw_sc", [nck, 32, 128, 128], BF16).ap()
    E.ffn_sc = nc.dram_tensor("ffn_sc", [ntt, 128, H], BF16).ap()
    E.wgu_sc = nc.dram_tensor("wgu_sc", [NOT_GU, 128, H], BF16).ap()
    E.wdn_sc = nc.dram_tensor("wdn_sc", [4, 128, NIT_DN, 512], BF16).ap()

    st = ExitStack()
    with st:
        e = st.enter_context

        E.iot_r = e(nc.sbuf_tensor("iot_r", [128, 128], F32))
        E.iot_c = e(nc.sbuf_tensor("iot_c", [128, 1], F32))
        E.ident_b = e(nc.sbuf_tensor("ident_b", [128, 128], BF16))
        E.ident_f = e(nc.sbuf_tensor("ident_f", [128, 128], F32))
        E.i64 = e(nc.sbuf_tensor("i64", [64, 64], F32))
        E.triu_f = e(nc.sbuf_tensor("triu_f", [128, 128], F32))
        E.negm_f = e(nc.sbuf_tensor("negm_f", [128, 128], F32))
        E.ones_b = e(nc.sbuf_tensor("ones_b", [128, 1], BF16))
        E.ones64 = e(nc.sbuf_tensor("ones64", [64, 128], F32))
        E.c_one = e(nc.sbuf_tensor("c_one", [128, 1], F32))
        E.c_epsln = e(nc.sbuf_tensor("c_epsln", [128, 1], F32))
        E.c_epsat = e(nc.sbuf_tensor("c_epsat", [128, 1], F32))
        E.c_epsff = e(nc.sbuf_tensor("c_epsff", [128, 1], F32))
        E.dtb_rep = e(nc.sbuf_tensor("dtb_rep", [128, NH], F32))
        E.nega_rep = e(nc.sbuf_tensor("nega_rep", [128, NH], F32))
        E.d_rep = e(nc.sbuf_tensor("d_rep", [128, NH], F32))
        E.dfull = e(nc.sbuf_tensor("dfull", [128, 32], BF16))
        E.convw_sb = e(nc.sbuf_tensor("convw_sb", [128, 136], F32))
        E.convb_sb = e(nc.sbuf_tensor("convb_sb", [128, 34], F32))
        E.nw_bf = e(nc.sbuf_tensor("nw_bf", [128, 32], BF16))
        E.dt_all = e(nc.sbuf_tensor("dt_all", [128, NH], F32))
        E.dtbf_all = e(nc.sbuf_tensor("dtbf_all", [128, nck * NH], BF16))
        E.sfm_all = e(nc.sbuf_tensor("sfm_all", [64, nck * 128], F32))
        E.inv_at = e(nc.sbuf_tensor("inv_at", [128, nck], F32))
        E.inv_ff = e(nc.sbuf_tensor("inv_ff", [128, ntt], F32))

        for nm in ("pe", "dv", "ac", "g2", "pe2", "dv2", "ac2", "pe3", "dv3",
                   "qpe", "qac", "qdv"):
            setattr(E, "s_" + nm, Sem(st, nc, "s_" + nm))
        E.dcn = DmaStream(st, nc, "dcn", 2)
        E.dgp = DmaStream(st, nc, "dgp", 2)
        E.dx = DmaStream(st, nc, "dx", 2)
        E.dx2 = DmaStream(st, nc, "dx2", 2)
        E.dst = DmaStream(st, nc, "dst", 3)
        E.dst2 = DmaStream(st, nc, "dst2", 2)
        E.dst3 = DmaStream(st, nc, "dst3", 2)
        E.dxp = DmaStream(st, nc, "dxp", 2)
        E.dxp2 = DmaStream(st, nc, "dxp2", 2)
        E.dxp3 = DmaStream(st, nc, "dxp3", 2)

        _consts(nc, E)
        if 0 in phases:
            _p0_ln(nc, st, E)
        if 1 in phases:
            _p1_ffn(nc, st, E, fb0_only=(3 in phases))
        if 2 in phases:
            _p2_inproj(nc, st, E)
        if 3 in phases:
            _p3_scan(nc, st, E, with_p1=(1 in phases))
        if 4 in phases:
            _p4_out(nc, st, E)

    nc.finalize()
    return nc


def _consts(nc, E):
    with nc.Block() as blk:
        @blk.gpsimd
        def _(g):
            g.iota(E.iot_r[:], [[1, 128]], channel_multiplier=0,
                   allow_small_or_imprecise_dtypes=True)
            g.iota(E.iot_c[:], [[0, 1]], channel_multiplier=1,
                   allow_small_or_imprecise_dtypes=True)
            E.tok_nw = E.dgp.issue(g, E.nw_bf[:], E.nw_in[:])

        @blk.sync
        def _(sy):
            E.tok_cw = E.dcn.issue(sy, E.convw_sb[:], E.convw[:])
            E.tok_cb = E.dcn.issue(sy, E.convb_sb[:], E.convb[:])

        @blk.vector
        def _(v):
            v.memset(E.ones64[:], 1.0)
            v.memset(E.c_one[:], 1.0)
            v.memset(E.c_epsln[:], 1e-5)
            v.memset(E.c_epsat[:], 1e-5)
            v.memset(E.c_epsff[:], 1e-6)
            v.memset(E.ones_b[:], 1.0)

    with nc.Block() as blk:
        @blk.vector
        def _(v):
            v.tensor_scalar(E.ident_b[:], E.iot_r[:], E.iot_c[:], None, ALU.is_equal)
            v.tensor_scalar(E.ident_f[:], E.iot_r[:], E.iot_c[:], None, ALU.is_equal)
            v.tensor_scalar(E.i64[:], E.iot_r[0:64, 0:64], E.iot_c[0:64, :], None,
                            ALU.is_equal)
            v.tensor_scalar(E.triu_f[:], E.iot_r[:], E.iot_c[:], None, ALU.is_ge)
            v.tensor_scalar(E.negm_f[:], E.iot_r[:], E.iot_c[:], -3.0e38,
                            ALU.is_lt, ALU.mult)

    with (nc.sbuf_tensor("cn_vrow", [1, 3 * NH], F32) as vrow,
          nc.sbuf_tensor("cn_ones1f", [1, 128], F32) as ones1f,
          nc.psum_tensor("cn_ps", [128, 3 * NH], F32) as ps_rep):
        toks = []
        with nc.Block() as blk:
            @blk.sync
            def _(sy):
                toks.append(E.dcn.issue(sy, vrow[0:1, 0:NH], E.dtb_in[:]))
                toks.append(E.dcn.issue(sy, vrow[0:1, NH:2 * NH], E.alog_in[:]))
                toks.append(E.dcn.issue(sy, vrow[0:1, 2 * NH:3 * NH], E.dvec_in[:]))

            @blk.vector
            def _(v):
                v.memset(ones1f[:], 1.0)

        with nc.Block() as blk:
            @blk.tensor
            def _(t):
                for tk in toks:
                    DmaStream.wait(t, tk)
                t.matmul(ps_rep[:], ones1f[:], vrow[:], start=True, stop=True)

        with nc.Block() as blk:
            @blk.vector
            def _(v):
                v.tensor_copy(E.dtb_rep[:], ps_rep[:, 0:NH])
                v.tensor_copy(E.d_rep[:], ps_rep[:, 2 * NH:3 * NH])

            @blk.scalar
            def _(s):
                s.activation(E.nega_rep[:], ps_rep[:, NH:2 * NH], ACT.Exp,
                             scale=1.0)

        with nc.Block() as blk:
            @blk.vector
            def _(v):
                v.tensor_scalar(E.nega_rep[:], E.nega_rep[:], -1.0, None, ALU.mult)
                v.tensor_copy(E.dfull[0:64, :], E.d_rep[0:64, 0:NH:2])
                v.tensor_copy(E.dfull[64:128, :], E.d_rep[64:128, 1:NH:2])


def _p0_ln(nc, st, E):
    ps_ = ExitStack()
    e = ps_.enter_context
    ntt = E.ntt
    xt = [e(nc.sbuf_tensor(f"p0_x{i}", [128, H], F32)) for i in range(2)]
    sq = [e(nc.sbuf_tensor(f"p0_sq{i}", [128, H], F32)) for i in range(2)]
    mcol = e(nc.sbuf_tensor("p0_m", [128, 2], F32))
    m2col = e(nc.sbuf_tensor("p0_m2", [128, 2], F32))
    vcol = e(nc.sbuf_tensor("p0_v", [128, 2], F32))
    icol = e(nc.sbuf_tensor("p0_i", [128, 2], F32))
    xn_tm = [e(nc.sbuf_tensor(f"p0_xn{i}", [128, H], BF16)) for i in range(2)]
    stg = [e(nc.sbuf_tensor(f"p0_st{i}", [128, H], BF16)) for i in range(2)]
    pst = [e(nc.psum_tensor(f"p0_ps{i}", [128, 2048], BF16)) for i in range(2)]

    sqd = [E.s_ac.bump() for _ in range(ntt)]
    std = [E.s_dv.bump() for _ in range(ntt)]
    inv = [E.s_ac2.bump() for _ in range(ntt)]
    xnd = [E.s_dv2.bump() for _ in range(ntt)]
    tpd = [E.s_pe.bump() for _ in range(ntt)]
    evd = [E.s_g2.bump() for _ in range(ntt)]
    xtok = []
    sttok = []

    with nc.Block() as blk:
        @blk.sync
        def _(sy):
            for i in range(ntt):
                if i >= 2:
                    sy.wait_ge(E.s_dv2.h, xnd[i - 2])
                    sy.wait_ge(E.s_ac.h, sqd[i - 2])
                xtok.append(E.dx.issue(
                    sy, xt[i % 2][:], E.x_in[i * 128:(i + 1) * 128, :]))

        @blk.scalar
        def _(s):
            for i in range(ntt):
                p = i % 2
                DmaStream.wait(s, xtok[i])
                if i >= 2:
                    s.wait_ge(E.s_dv.h, std[i - 2])
                s.activation(sq[p][:], xt[p][:], ACT.Square).then_inc(E.s_ac.h, 1)
                if i >= 1:
                    s.wait_ge(E.s_g2.h, evd[i - 1])
                    sttok.append(E.dst.issue(
                        s,
                        E.xn_sc[:, :, (i - 1) * 128:i * 128].transpose([1, 0, 2]),
                        stg[(i - 1) % 2][:].rearrange("p (a t) -> p a t", a=NHT)))
                s.wait_ge(E.s_dv.h, std[i])
                s.activation(icol[:, p:p + 1], vcol[:, p:p + 1], ACT.Ln,
                             bias=E.c_epsln[:])
                s.drain()
                s.activation(icol[:, p:p + 1], icol[:, p:p + 1], ACT.Exp,
                             scale=-0.5).then_inc(E.s_ac2.h, 1)
            s.wait_ge(E.s_g2.h, evd[ntt - 1])
            sttok.append(E.dst.issue(
                s,
                E.xn_sc[:, :, (ntt - 1) * 128:ntt * 128].transpose([1, 0, 2]),
                stg[(ntt - 1) % 2][:].rearrange("p (a t) -> p a t", a=NHT)))

        @blk.vector
        def _(v):
            for i in range(ntt):
                p = i % 2
                if i >= 1:
                    v.wait_ge(E.s_pe.h, tpd[i - 1])
                    if i >= 3:
                        DmaStream.wait(v, sttok[i - 3])
                    v.tensor_copy(stg[(i - 1) % 2][:], pst[(i - 1) % 2][:]) \
                        .then_inc(E.s_g2.h, 1)
                DmaStream.wait(v, xtok[i])
                v.wait_ge(E.s_ac.h, sqd[i])
                if i >= 2:
                    v.wait_ge(E.s_ac2.h, inv[i - 2])
                    v.drain()
                v.reduce_sum(mcol[:, p:p + 1], xt[p][:], axis=AX.X)
                v.reduce_sum(m2col[:, p:p + 1], sq[p][:], axis=AX.X)
                v.drain()
                v.tensor_scalar(mcol[:, p:p + 1], mcol[:, p:p + 1], -1.0 / H,
                                None, ALU.mult)
                v.tensor_scalar(m2col[:, p:p + 1], m2col[:, p:p + 1], 1.0 / H,
                                None, ALU.mult)
                v.drain()
                v.tensor_scalar(vcol[:, p:p + 1], mcol[:, p:p + 1],
                                mcol[:, p:p + 1], -1.0, ALU.mult, ALU.mult)
                v.drain()
                v.tensor_tensor(vcol[:, p:p + 1], vcol[:, p:p + 1],
                                m2col[:, p:p + 1], ALU.add).then_inc(E.s_dv.h, 1)
                v.wait_ge(E.s_ac2.h, inv[i])
                if i >= 2:
                    v.wait_ge(E.s_pe.h, tpd[i - 2])
                v.tensor_scalar(xn_tm[p][:], xt[p][:], mcol[:, p:p + 1],
                                icol[:, p:p + 1], ALU.add, ALU.mult) \
                    .then_inc(E.s_dv2.h, 1)
            v.wait_ge(E.s_pe.h, tpd[ntt - 1])
            if ntt >= 3:
                DmaStream.wait(v, sttok[ntt - 3])
            v.tensor_copy(stg[(ntt - 1) % 2][:], pst[(ntt - 1) % 2][:]) \
                .then_inc(E.s_g2.h, 1)

        @blk.tensor
        def _(t):
            for i in range(ntt):
                p = i % 2
                t.wait_ge(E.s_dv2.h, xnd[i])
                if i >= 2:
                    t.wait_ge(E.s_g2.h, evd[i - 2])
                for ht in range(NHT):
                    ins = t.transpose(pst[p][:, ht * 128:(ht + 1) * 128],
                                      xn_tm[p][:, ht * 128:(ht + 1) * 128],
                                      E.ident_b[:])
                ins.then_inc(E.s_pe.h, 1)

    _quiesce_stores(nc, E)
    ps_.close()


def _p1_ffn(nc, st, E, fb0_only=False):
    ps_ = ExitStack()
    e = ps_.enter_context
    T = E.T
    TBF = min(512, T)
    nfb = T // TBF
    E.p1_TBF = TBF
    E.p1_nfb = nfb
    ntt_fb = TBF // 128

    h_fm = e(nc.sbuf_tensor("p1_h", [128, NIT_DN * TBF], BF16))
    ssq = e(nc.sbuf_tensor("p1_ssq", [128, ntt_fb], F32))
    ps_g = e(nc.psum_tensor("p1_psg", [128, TBF], F32))
    ps_u = e(nc.psum_tensor("p1_psu", [128, TBF], F32))
    psT = e(nc.psum_tensor("p1_psT", [128, 2048], BF16))
    psD = e(nc.psum_tensor("p1_psD", [128, 1024], F32))

    fbs = range(1) if (fb0_only and nfb > 1) else range(nfb)
    for fb in fbs:
        deq = (fb == 0)
        _p1_gate_up(nc, E, fb, TBF, deq, h_fm, ps_g, ps_u, psT)
        _p1_down(nc, E, fb, TBF, deq, h_fm, psD, psT, ssq)

    _quiesce_stores(nc, E)
    ps_.close()


def _p1_gate_up(nc, E, fb, TBF, deq, h_fm, ps_g, ps_u, psT):
    T = E.T
    with ExitStack() as ps_:
        e = ps_.enter_context
        xnf = e(nc.sbuf_tensor(f"p1_xn{fb}", [128, NHT * TBF], BF16))
        wt = [[e(nc.sbuf_tensor(f"p1_w{fb}_{a}{i}", [128, 2048], BF16))
               for i in range(2)] for a in range(2)]
        sgst = [e(nc.sbuf_tensor(f"p1_sg{fb}_{i}", [128, TBF], BF16)) for i in range(2)]
        if deq:
            pkb = [e(nc.sbuf_tensor(f"p1_pk{fb}_{i}", [128, 256], I32)) for i in range(2)]
            scb = [e(nc.sbuf_tensor(f"p1_sc{fb}_{i}", [128, 16], F16)) for i in range(2)]
            scf = [e(nc.sbuf_tensor(f"p1_scf{fb}_{i}", [128, 16], BF16))
                   for i in range(2)]
            nib = e(nc.sbuf_tensor(f"p1_nib{fb}", [128, 2048], U16))
            tq1 = e(nc.sbuf_tensor(f"p1_tq1{fb}", [128, 2048], U16))
            tq2 = [e(nc.sbuf_tensor(f"p1_tq2{fb}_{i}", [128, 2048], U16))
                   for i in range(2)]
            tq3 = e(nc.sbuf_tensor(f"p1_tq3{fb}", [128, 2048], U16))
            wbf = e(nc.sbuf_tensor(f"p1_wbf{fb}", [128, 2048], BF16))

        SPE, SDV, SAC, SGP = E.s_pe, E.s_dv, E.s_ac, E.s_g2
        ntile = 2 * NPAIR_GU
        ots = []
        for j in range(NPAIR_GU):
            ots.append(j)
            ots.append(j + NPAIR_GU)

        def wslot(i):
            return wt[i % 2][(i // 2) % 2]

        sch = [{} for _ in range(ntile)]
        for i in range(ntile):
            d = sch[i]
            if deq:
                d["gp_nib"] = SDV.bump(8)
                d["dv_pre"] = SDV.bump(1)
                d["g_deq"] = SGP.bump(1)
                if i >= 2 and (i - 2) % 2 == 1:
                    sch[i - 2]["dv_h"] = SDV.bump()
                d["t_tp"] = SPE.bump()
                d["a_ev"] = SAC.bump()
            d["t_mm"] = SPE.bump()
            if i % 2 == 1:
                d["a_silu"] = SAC.bump()
                if not deq:
                    d["dv_h"] = SDV.bump()
        if deq:
            sch[ntile - 1]["dv_h"] = SDV.bump()

        xtok = [None]
        ptok, sctok, wtok, wstok = [], [], [], []

        with nc.Block() as blk:
            @blk.sync
            def _(sy):
                xtok[0] = E.dx.issue(
                    sy, xnf[:].rearrange("p (a t) -> p a t", a=NHT),
                    E.xn_sc[:, :, fb * TBF:(fb + 1) * TBF].transpose([1, 0, 2]))
                for i, ot in enumerate(ots):
                    if deq:
                        if i >= 2:
                            sy.wait_ge(SDV.h, sch[i - 2]["gp_nib"])
                            sy.wait_ge(SDV.h, sch[i - 2]["dv_pre"])
                        ptok.append(E.dx2.issue(sy, pkb[i % 2][:],
                                                E.gup_in[ot * 128:(ot + 1) * 128, :]))
                        sctok.append(E.dcn.issue(sy, scb[i % 2][:],
                                                 E.gus_in[ot * 128:(ot + 1) * 128, :]))
                    else:
                        if fb == 1 and i == 0:
                            for tk in E.p1_gu_wstok[-3:]:
                                DmaStream.wait(sy, tk)
                        if i >= 4:
                            sy.wait_ge(SPE.h, sch[i - 4]["t_mm"])
                        wtok.append(E.dx2.issue(sy, wslot(i)[:], E.wgu_sc[ot]))

            @blk.gpsimd
            def _(g):
                if not deq:
                    return
                for i, ot in enumerate(ots):
                    g.wait_ge(SDV.h, sch[i]["dv_pre"])
                    if i >= 1:
                        g.wait_ge(SPE.h, sch[i - 1]["t_tp"])
                    scv = scf[i % 2][:].unsqueeze(2) \
                        .to_broadcast([128, 16, 128])
                    g.tensor_tensor(
                        wbf[:].rearrange("p (a i) -> p a i", a=16),
                        tq2[i % 2][:].bitcast(BF16).rearrange(
                            "p (a i) -> p a i", a=16),
                        scv, ALU.mult).then_inc(SGP.h, 1)
                    g.drain()

            @blk.vector
            def _(v):
                for i, ot in enumerate(ots):
                    if deq:
                        DmaStream.wait(v, ptok[i])
                        pku = pkb[i % 2][:].bitcast(U16)
                        for k in range(8):
                            v.tensor_scalar(nib[:, k::8],
                                            pku[:, (1 if k >= 4 else 0)::2],
                                            4 * (k % 4), 15,
                                            ALU.logical_shift_right,
                                            ALU.bitwise_and).then_inc(SDV.h, 1)
                        v.drain()
                        if i >= 2:
                            v.wait_ge(SGP.h, sch[i - 2]["g_deq"])
                        v.tensor_scalar(tq3[:], nib[:], 12, 0x8000,
                                        ALU.logical_shift_left, ALU.bitwise_and)
                        v.tensor_scalar(tq1[:], nib[:], 7, None,
                                        ALU.bitwise_and)
                        v.drain()
                        v.tensor_scalar(tq2[i % 2][:], tq1[:], 252, None,
                                        ALU.add)
                        v.drain()
                        v.scalar_tensor_tensor(tq1[:], tq1[:], 252.0,
                                               tq2[i % 2][:],
                                               ALU.mult, ALU.min)
                        v.drain()
                        v.scalar_tensor_tensor(tq2[i % 2][:], tq1[:], 64.0,
                                               tq3[:], ALU.mult, ALU.add)
                        v.drain()
                        DmaStream.wait(v, sctok[i])
                        v.tensor_copy(scf[i % 2][:], scb[i % 2][:]) \
                            .then_inc(SDV.h, 1)
                    ih = i - 2 if deq else i
                    if ih >= 0 and ih % 2 == 1:
                        j = ih // 2
                        v.wait_ge(SAC.h, sch[ih]["a_silu"])
                        v.wait_ge(SPE.h, sch[ih]["t_mm"])
                        v.tensor_tensor(h_fm[:, j * TBF:(j + 1) * TBF],
                                        sgst[j % 2][:], ps_u[:], ALU.mult) \
                            .then_inc(SDV.h, 1)
                    if deq and i == len(ots) - 1:
                        j = i // 2
                        v.wait_ge(SAC.h, sch[i]["a_silu"])
                        v.wait_ge(SPE.h, sch[i]["t_mm"])
                        v.tensor_tensor(h_fm[:, j * TBF:(j + 1) * TBF],
                                        sgst[j % 2][:], ps_u[:], ALU.mult) \
                            .then_inc(SDV.h, 1)

            @blk.scalar
            def _(s):
                for i, ot in enumerate(ots):
                    if deq:
                        s.wait_ge(SPE.h, sch[i]["t_tp"])
                        if i >= 4:
                            s.wait_ge(SPE.h, sch[i - 4]["t_mm"])
                        s.activation(wslot(i)[:], psT[:], ACT.Copy) \
                            .then_inc(SAC.h, 1)
                        s.drain()
                        wstok.append(E.dst.issue(s, E.wgu_sc[ot], wslot(i)[:]))
                    if i % 2 == 1:
                        j = i // 2
                        s.wait_ge(SPE.h, sch[i - 1]["t_mm"])
                        if j >= 2:
                            s.wait_ge(SDV.h, sch[2 * (j - 2) + 1]["dv_h"])
                        s.activation(sgst[j % 2][:], ps_g[:], ACT.Silu) \
                            .then_inc(SAC.h, 1)

            @blk.tensor
            def _(t):
                DmaStream.wait(t, xtok[0])
                for i, ot in enumerate(ots):
                    if deq:
                        t.wait_ge(SGP.h, sch[i]["g_deq"])
                        if i >= 1:
                            t.wait_ge(SAC.h, sch[i - 1]["a_ev"])
                        for ht in range(NHT):
                            ins = t.transpose(psT[:, ht * 128:(ht + 1) * 128],
                                              wbf[:, ht * 128:(ht + 1) * 128],
                                              E.ident_b[:])
                        ins.then_inc(SPE.h, 1)
                        t.wait_ge(SAC.h, sch[i]["a_ev"])
                    else:
                        DmaStream.wait(t, wtok[i])
                    if i % 2 == 0 and i >= 2:
                        t.wait_ge(SAC.h, sch[i - 1]["a_silu"])
                    if i % 2 == 1 and i >= 3:
                        t.wait_ge(SDV.h, sch[i - 2]["dv_h"])
                    psu = ps_g if (i % 2 == 0) else ps_u
                    for ht in range(NHT):
                        ins = t.matmul(psu[:],
                                       wslot(i)[:, ht * 128:(ht + 1) * 128],
                                       xnf[:, ht * TBF:(ht + 1) * TBF],
                                       start=(ht == 0), stop=(ht == NHT - 1))
                    ins.then_inc(SPE.h, 1)

        if deq:
            E.p1_gu_wstok = wstok
        _quiesce_stores(nc, E)


def _p1_down(nc, E, fb, TBF, deq, h_fm, psD, psT, ssq):
    T = E.T
    ntt_fb = TBF // 128
    with ExitStack() as ps_:
        e = ps_.enter_context
        wdn = e(nc.sbuf_tensor(f"p1_wdn{fb}", [128, NIT_DN * 512], BF16))
        sqb = e(nc.sbuf_tensor(f"p1_sqb{fb}", [128, 512], F32))
        sscol = e(nc.sbuf_tensor(f"p1_sscol{fb}", [128, 1], F32))
        fst = [e(nc.sbuf_tensor(f"p1_fs{fb}_{i}", [128, 512], BF16)) for i in range(2)]
        if deq:
            pkb = [e(nc.sbuf_tensor(f"p1_dpk{fb}_{i}", [128, 704], I32)) for i in range(2)]
            scb = [e(nc.sbuf_tensor(f"p1_dsc{fb}_{i}", [128, 44], F16)) for i in range(2)]
            scf = e(nc.sbuf_tensor(f"p1_dscf{fb}", [128, 44], F32))
            nib = e(nc.sbuf_tensor(f"p1_dnib{fb}", [128, 5632], U16))
            tq1 = e(nc.sbuf_tensor(f"p1_dtq1{fb}", [128, 5632], U16))
            tq2 = e(nc.sbuf_tensor(f"p1_dtq2{fb}", [128, 5632], U16))
            tq3 = e(nc.sbuf_tensor(f"p1_dtq3{fb}", [128, 5632], U16))
            wbf = e(nc.sbuf_tensor(f"p1_dwbf{fb}", [128, 5632], BF16))

        SPE, SDV, SAC, SGP = E.s_pe, E.s_dv, E.s_ac, E.s_g2
        if deq:
            E.p1_dn_wstok = {}
        ftoks_all = E.dst2.plan(4 * ntt_fb)

        for oq in range(4):
            sch_d = []
            if deq:
                for i in range(4):
                    d = {}
                    d["gp_nib"] = SDV.bump(8)
                    d["dv_deq"] = SDV.bump(6)
                    d["t_tp"] = SPE.bump(3)
                    d["a_ev"] = SAC.bump(3)
                    sch_d.append(d)
            mm_d = [E.s_pe.bump() for _ in range(ntt_fb)]
            sq_d = [E.s_ac.bump() for _ in range(ntt_fb)]
            ss_d, ev_d = [], []
            for _tt in range(ntt_fb):
                ss_d.append(E.s_dv.bump(2))
                ev_d.append(E.s_dv.bump())
            if oq == 3:
                inv_d = E.s_ac.bump()
            ptok, sctok, wltok, wstok2, ftok = [], [], [], [], []

            with nc.Block() as blk:
                @blk.sync
                def _(sy):
                    if deq:
                        for i in range(4):
                            if i >= 2:
                                sy.wait_ge(SDV.h, sch_d[i - 2]["gp_nib"])
                                sy.wait_ge(SDV.h, sch_d[i - 2]["dv_deq"] - 5)
                            ot = oq * 4 + i
                            ptok.append(E.dx2.issue(
                                sy, pkb[i % 2][:],
                                E.dnp_in[ot * 128:(ot + 1) * 128, :]))
                            sctok.append(E.dcn.issue(
                                sy, scb[i % 2][:],
                                E.dns_in[ot * 128:(ot + 1) * 128, :]))
                    else:
                        if fb == 1:
                            for _tk in E.p1_dn_wstok[oq]:
                                DmaStream.wait(sy, _tk)
                        wltok.append(E.dx2.issue(
                            sy, wdn[:].rearrange("p (a i) -> p a i", a=NIT_DN),
                            E.wdn_sc[oq]))

                @blk.gpsimd
                def _(g):
                    for tt in range(ntt_fb):
                        g.wait_ge(SDV.h, ev_d[tt])
                        ttg = fb * ntt_fb + tt
                        tk = E.dst2.issue(
                            g, E.ffn_sc[ttg, :, oq * 512:(oq + 1) * 512],
                            fst[tt % 2][:])
                        assert tk == ftoks_all[oq * ntt_fb + tt]
                        ftok.append(tk)

                @blk.vector
                def _(v):
                    if deq:
                        for i in range(4):
                            DmaStream.wait(v, ptok[i])
                            if i >= 1:
                                v.wait_ge(SDV.h, sch_d[i - 1]["dv_deq"])
                            pku = pkb[i % 2][:].bitcast(U16)
                            for k in range(8):
                                v.tensor_scalar(nib[:, k::8],
                                                pku[:, (1 if k >= 4 else 0)::2],
                                                4 * (k % 4), 15,
                                                ALU.logical_shift_right,
                                                ALU.bitwise_and) \
                                    .then_inc(SDV.h, 1)
                            v.drain()
                            DmaStream.wait(v, sctok[i])
                            if i >= 1:
                                v.wait_ge(SPE.h, sch_d[i - 1]["t_tp"])
                            v.tensor_copy(scf[:], scb[i % 2][:]).then_inc(SDV.h, 1)
                            v.tensor_scalar(tq3[:], nib[:], 12, 0x8000,
                                            ALU.logical_shift_left,
                                            ALU.bitwise_and)
                            v.tensor_scalar(tq1[:], nib[:], 7, None,
                                            ALU.bitwise_and)
                            v.drain()
                            v.tensor_scalar(tq2[:], tq1[:], 252, None, ALU.add)
                            v.drain()
                            v.tensor_scalar(tq1[:], tq1[:], 252, None, ALU.mult)
                            v.drain()
                            v.tensor_tensor(tq1[:], tq1[:], tq2[:], ALU.min)
                            v.drain()
                            v.scalar_tensor_tensor(tq2[:], tq1[:], 64.0, tq3[:],
                                                   ALU.mult, ALU.add)
                            v.drain()
                            scv = scf[:].unsqueeze(2).to_broadcast([128, 44, 128])
                            v.tensor_tensor(
                                wbf[:].rearrange("p (a i) -> p a i", a=44),
                                tq2[:].bitcast(BF16).rearrange(
                                    "p (a i) -> p a i", a=44),
                                scv, ALU.mult).then_inc(SDV.h, 5)
                    for tt in range(ntt_fb):
                        v.wait_ge(SAC.h, sq_d[tt])
                        v.reduce_sum(sscol[:], sqb[:], axis=AX.X) \
                            .then_inc(SDV.h, 1)
                        v.drain()
                        if oq == 0:
                            v.tensor_copy(ssq[:, tt:tt + 1], sscol[:]) \
                                .then_inc(SDV.h, 1)
                        else:
                            v.tensor_tensor(ssq[:, tt:tt + 1], ssq[:, tt:tt + 1],
                                            sscol[:], ALU.add).then_inc(SDV.h, 1)
                        v.drain()
                        v.wait_ge(SPE.h, mm_d[tt])
                        gidx = oq * ntt_fb + tt
                        if gidx >= 2:
                            DmaStream.wait(v, ftoks_all[gidx - 2])
                        v.tensor_copy(fst[tt % 2][:],
                                      psD[:, (tt % 2) * 512:(tt % 2) * 512 + 512]) \
                            .then_inc(SDV.h, 1)

                @blk.scalar
                def _(s):
                    if deq:
                        wstore = []
                        for i in range(4):
                            base = sch_d[i]["t_tp"] - 3
                            for w in range(3):
                                nt = 16 if w < 2 else 12
                                if oq >= 1 and i == 0:
                                    DmaStream.wait(
                                        s, E.p1_dn_wstok[oq - 1][w])
                                s.wait_ge(SPE.h, base + w + 1)
                                s.activation(
                                    wdn[:].rearrange(
                                        "p (a i) -> p a i", a=NIT_DN)
                                    [:, w * 16:w * 16 + nt, i * 128:(i + 1) * 128],
                                    psT[:].rearrange("p (a i) -> p a i", a=16)
                                    [:, 0:nt, :],
                                    ACT.Copy).then_inc(SAC.h, 1)
                                if i == 3:
                                    s.drain()
                                    wstore.append(E.dst.issue(
                                        s,
                                        E.wdn_sc[oq][:, w * 16:w * 16 + nt],
                                        wdn[:].rearrange(
                                            "p (a i) -> p a i", a=NIT_DN)
                                        [:, w * 16:w * 16 + nt, :]))
                        E.p1_dn_wstok[oq] = wstore
                    for tt in range(ntt_fb):
                        s.wait_ge(SPE.h, mm_d[tt])
                        if tt >= 1:
                            s.wait_ge(SDV.h, ss_d[tt - 1])
                        s.activation(sqb[:],
                                     psD[:, (tt % 2) * 512:(tt % 2) * 512 + 512],
                                     ACT.Square).then_inc(SAC.h, 1)
                    if oq == 3:
                        for tt in range(ntt_fb):
                            s.wait_ge(SDV.h, ss_d[tt])
                        ttg0 = fb * ntt_fb
                        s.activation(E.inv_ff[:, ttg0:ttg0 + ntt_fb],
                                     ssq[:, 0:ntt_fb], ACT.Ln,
                                     bias=E.c_epsff[:], scale=1.0 / H)
                        s.drain()
                        s.activation(E.inv_ff[:, ttg0:ttg0 + ntt_fb],
                                     E.inv_ff[:, ttg0:ttg0 + ntt_fb],
                                     ACT.Exp, scale=-0.5).then_inc(SAC.h, 1)

                @blk.tensor
                def _(t):
                    if deq:
                        for i in range(4):
                            t.wait_ge(SDV.h, sch_d[i]["dv_deq"])
                            base_pe = sch_d[i]["t_tp"] - 3
                            base_ac = sch_d[i]["a_ev"] - 3
                            for w in range(3):
                                nt = 16 if w < 2 else 12
                                if (i, w) != (0, 0):
                                    t.wait_ge(SAC.h, base_ac + w)
                                for k in range(nt):
                                    it = w * 16 + k
                                    ins = t.transpose(
                                        psT[:, k * 128:(k + 1) * 128],
                                        wbf[:, it * 128:(it + 1) * 128],
                                        E.ident_b[:])
                                ins.then_inc(SPE.h, 1)
                        t.wait_ge(SAC.h, sch_d[3]["a_ev"])
                    else:
                        DmaStream.wait(t, wltok[0])
                    for tt in range(ntt_fb):
                        if tt >= 2:
                            t.wait_ge(SDV.h, ev_d[tt - 2])
                            t.wait_ge(SAC.h, sq_d[tt - 2])
                        for it in range(NIT_DN):
                            ins = t.matmul(
                                psD[:, (tt % 2) * 512:(tt % 2) * 512 + 512],
                                h_fm[:, it * TBF + tt * 128:
                                     it * TBF + (tt + 1) * 128],
                                wdn[:, it * 512:(it + 1) * 512],
                                start=(it == 0), stop=(it == NIT_DN - 1))
                        ins.then_inc(SPE.h, 1)


            E.p1_last_toks = ftok
        _quiesce_stores(nc, E)


def _p2_inproj(nc, st, E):
    ps_ = ExitStack()
    e = ps_.enter_context
    T, nck = E.T, E.nck
    TB = min(512, T)
    ntb = T // TB

    xn_sb = e(nc.sbuf_tensor("p2_xn", [128, NHT * T], BF16))
    wdt_sb = e(nc.sbuf_tensor("p2_wdt", [128, NHT * 64], BF16))
    loop_st = ExitStack()
    e2 = loop_st.enter_context
    wtb = [e2(nc.sbuf_tensor(f"p2_wt{i}", [128, NHT * 128], BF16)) for i in range(2)]
    gst = [e2(nc.sbuf_tensor(f"p2_gs{i}", [128, T], BF16)) for i in range(2)]
    cvs = [e2(nc.sbuf_tensor(f"p2_cv{i}", [128, T + 3], BF16)) for i in range(2)]
    hst = [e2(nc.sbuf_tensor(f"p2_hs{i}", [128, T], BF16)) for i in range(2)]
    diag = [e2(nc.sbuf_tensor(f"p2_dg{i}", [128, 4 * 128], BF16)) for i in range(2)]
    ps_ip = e2(nc.psum_tensor("p2_psip", [128, T if T <= 2048 else 2048], F32))
    ps_cv = e2(nc.psum_tensor("p2_pscv", [128, T if T <= 2048 else 2048], F32))

    NFT = 66

    mmd = [E.s_pe.bump() for _ in range(NFT)]
    evg = {}
    for ft in range(NFT):
        if ft < 32:
            evg[ft] = ("ac", E.s_ac.bump())
        else:
            evg[ft] = ("dv", E.s_dv.bump())
    cvd = {ft: E.s_pe2.bump() for ft in range(32, NFT)}
    sld = {ft: E.s_ac2.bump() for ft in range(32, NFT)}
    dgd = {ft: E.s_dv2.bump(4) for ft in range(32, NFT)}
    wtok = []
    stok = []

    with nc.Block() as blk:
        @blk.sync
        def _(sy):
            E.p2_xntok = E.dx.issue(
                sy, xn_sb[:].rearrange("p (a t) -> p a t", a=NHT), E.xn_sc[:, :, :].transpose([1, 0, 2]))

        @blk.gpsimd
        def _(g):
            E.p2_wdtok = E.dgp.issue(
                g, wdt_sb[:].rearrange("p (a f) -> p a f", a=NHT),
                E.winT[:, PROJ - 64:PROJ].rearrange("(a p) f -> p a f", p=128))
            for ft in range(NFT):
                if ft >= 2:
                    g.wait_ge(E.s_pe.h, mmd[ft - 2])
                wtok.append(E.dgp.issue(
                    g, wtb[ft % 2][:].rearrange("p (a f) -> p a f", a=NHT),
                    E.winT[:, ft * 128:(ft + 1) * 128].rearrange(
                        "(a p) f -> p a f", p=128)))

        @blk.vector
        def _(v):
            v.memset(cvs[0][:, 0:3], 0.0)
            v.memset(cvs[1][:, 0:3], 0.0)
            for ft in range(32, NFT):
                p = ft % 2
                if ft >= 34:
                    v.wait_ge(E.s_pe2.h, cvd[ft - 2])
                for k in range(4):
                    v.tensor_scalar(diag[p][:, k * 128:(k + 1) * 128],
                                    E.ident_b[:],
                                    E.convw_sb[:, (ft - 32) * 4 + k:(ft - 32) * 4 + k + 1],
                                    None, ALU.mult).then_inc(E.s_dv2.h, 1)
                v.wait_ge(E.s_pe.h, mmd[ft])
                if ft >= 34:
                    v.wait_ge(E.s_ac2.h, sld[ft - 2])
                v.tensor_copy(cvs[p][:, 3:T + 3], ps_ip[:]).then_inc(E.s_dv.h, 1)

        @blk.tensor
        def _(t):
            DmaStream.wait(t, E.p2_xntok)
            for ft in range(NFT):
                p = ft % 2
                DmaStream.wait(t, wtok[ft])
                if ft >= 1:
                    kind, val = evg[ft - 1]
                    t.wait_ge(getattr(E, "s_" + kind).h, val)
                for tb in range(ntb):
                    for ht in range(NHT):
                        ins = t.matmul(
                            ps_ip[:, tb * TB:(tb + 1) * TB],
                            wtb[p][:, ht * 128:(ht + 1) * 128],
                            xn_sb[:, ht * T + tb * TB: ht * T + (tb + 1) * TB],
                            start=(ht == 0), stop=(ht == NHT - 1))
                ins.then_inc(E.s_pe.h, 1)
                if ft >= 32:
                    t.wait_ge(E.s_dv.h, evg[ft][1])
                    t.wait_ge(E.s_dv2.h, dgd[ft])
                    if ft >= 33:
                        t.wait_ge(E.s_ac2.h, sld[ft - 1])
                    for tb in range(ntb):
                        for k in range(4):
                            ins = t.matmul(
                                ps_cv[:, tb * TB:(tb + 1) * TB],
                                diag[p][:, k * 128:(k + 1) * 128],
                                cvs[p][:, tb * TB + k: tb * TB + k + TB],
                                start=(k == 0), stop=(k == 3))
                    ins.then_inc(E.s_pe2.h, 1)

        @blk.scalar
        def _(s):
            for ft in range(NFT):
                p = ft % 2
                if ft < 32:
                    s.wait_ge(E.s_pe.h, mmd[ft])
                    if ft >= 2:
                        DmaStream.wait(s, stok[ft - 2])
                    s.activation(gst[p][:], ps_ip[:], ACT.Silu) \
                        .then_inc(E.s_ac.h, 1)
                    s.drain()
                    stok.append(E.dst.issue(s, E.sg_sc[ft, :, :], gst[p][:]))
                else:
                    s.wait_ge(E.s_pe2.h, cvd[ft])
                    if ft >= 34:
                        DmaStream.wait(s, stok[ft - 2])
                    s.activation(hst[p][:], ps_cv[:], ACT.Silu,
                                 bias=E.convb_sb[:, ft - 32:ft - 31]) \
                        .then_inc(E.s_ac2.h, 1)
                    s.drain()
                    if ft < 64:
                        dst_ap = E.hs_sc[ft - 32, :, :]
                    elif ft == 64:
                        dst_ap = E.b_sc[:, :]
                    else:
                        dst_ap = E.c_sc[:, :]
                    stok.append(E.dst.issue(s, dst_ap, hst[p][:]))

    _quiesce_stores(nc, E)
    loop_st.close()
    dtw = e(nc.sbuf_tensor("p2_dtw", [128, 64], F32))
    ab3 = e(nc.sbuf_tensor("p2_ab", [128, 64], F32))
    rl = e(nc.sbuf_tensor("p2_rl", [128, 64], F32))
    atw = e(nc.sbuf_tensor("p2_at", [128, 64], F32))
    stw = e(nc.sbuf_tensor("p2_stw", [128, 64], F32))
    ps_dt = e(nc.psum_tensor("p2_psdt", [128, 64], F32))
    ps_sf = e(nc.psum_tensor("p2_pssf", [64, 128], F32))

    with nc.Block() as blk:
        dt_mm = [E.s_pe.bump() for _ in range(nck)]
        dt_z = [E.s_dv.bump() for _ in range(nck)]
        dt_sp = [E.s_ac.bump() for _ in range(nck)]
        dt_fin = [E.s_dv2.bump() for _ in range(nck)]
        s_mm = [E.s_pe2.bump() for _ in range(nck)]
        s_ev = [E.s_dv3.bump() for _ in range(nck)]
        sf_tp = [E.s_pe3.bump() for _ in range(nck)]
        sf_ev = [E.s_g2.bump() for _ in range(nck)]

        @blk.tensor
        def _(t):
            DmaStream.wait(t, E.p2_wdtok)
            for ck in range(nck):
                if ck >= 1:
                    t.wait_ge(E.s_dv3.h, s_ev[ck - 1])
                for ht in range(NHT):
                    ins = t.matmul(ps_dt[:],
                                   xn_sb[:, ht * T + ck * 128: ht * T + (ck + 1) * 128],
                                   wdt_sb[:, ht * 64:(ht + 1) * 64],
                                   start=(ht == 0), stop=(ht == NHT - 1))
                ins.then_inc(E.s_pe.h, 1)
                t.wait_ge(E.s_dv2.h, dt_fin[ck])
                ins = t.matmul(ps_dt[:], E.triu_f[:], atw[:], start=True,
                               stop=True)
                ins.then_inc(E.s_pe2.h, 1)
                t.wait_ge(E.s_dv3.h, s_ev[ck])
                if ck >= 1:
                    t.wait_ge(E.s_g2.h, sf_ev[ck - 1])
                t.transpose(ps_sf[:], stw[:], E.ident_f[:]).then_inc(E.s_pe3.h, 1)

        @blk.vector
        def _(v):
            for ck in range(nck):
                v.wait_ge(E.s_pe.h, dt_mm[ck])
                if ck >= 1:
                    v.wait_ge(E.s_ac.h, dt_sp[ck - 1])
                v.tensor_tensor(dtw[:], ps_dt[:], E.dtb_rep[:], ALU.add) \
                    .then_inc(E.s_dv.h, 1)
                v.wait_ge(E.s_ac.h, dt_sp[ck])
                v.drain()
                v.tensor_relu(rl[:], dtw[:])
                v.drain()
                v.tensor_tensor(E.dt_all[:], rl[:], ab3[:],
                                ALU.add)
                v.drain()
                v.tensor_copy(E.dtbf_all[:, ck * 64:(ck + 1) * 64],
                              E.dt_all[:])
                v.tensor_tensor(atw[:], E.dt_all[:],
                                E.nega_rep[:], ALU.mult).then_inc(E.s_dv2.h, 1)
                v.wait_ge(E.s_pe2.h, s_mm[ck])
                v.tensor_copy(stw[:], ps_dt[:]).then_inc(E.s_dv3.h, 1)
                v.wait_ge(E.s_pe3.h, sf_tp[ck])
                v.tensor_copy(E.sfm_all[:, ck * 128:(ck + 1) * 128], ps_sf[:]) \
                    .then_inc(E.s_g2.h, 1)

        @blk.scalar
        def _(s):
            for ck in range(nck):
                s.wait_ge(E.s_dv.h, dt_z[ck])
                if ck >= 1:
                    s.wait_ge(E.s_dv2.h, dt_fin[ck - 1])
                s.activation(ab3[:], dtw[:], ACT.Abs)
                s.drain()
                s.activation(ab3[:], ab3[:], ACT.Exp, scale=-1.0)
                s.drain()
                s.activation(ab3[:], ab3[:], ACT.Ln, bias=E.c_one[:]) \
                    .then_inc(E.s_ac.h, 1)

    _quiesce_stores(nc, E)
    ps_.close()


def _p3_scan(nc, st, E, with_p1=False):
    ps_ = ExitStack()
    e = ps_.enter_context
    T, nck = E.T, E.nck
    NB = 8

    hsb = [e(nc.sbuf_tensor(f"p3_hs{i}", [128, 4096], BF16)) for i in range(2)]
    sgb = [e(nc.sbuf_tensor(f"p3_sg{i}", [128, 4096], BF16)) for i in range(2)]
    bcb = [e(nc.sbuf_tensor(f"p3_bc{i}", [128, 256], BF16)) for i in range(2)]
    hs_tm = e(nc.sbuf_tensor("p3_hstm", [128, 4096], BF16))
    b_tm = e(nc.sbuf_tensor("p3_btm", [128, 128], BF16))
    cbtm = e(nc.sbuf_tensor("p3_cbtm", [128, 128], BF16))
    sneg = e(nc.sbuf_tensor("p3_sneg", [64, 128], F32))
    esr = e(nc.sbuf_tensor("p3_esr", [128, 1024], BF16))
    esl = e(nc.sbuf_tensor("p3_esl", [128, NH], F32))
    dexp = e(nc.sbuf_tensor("p3_dexp", [128, 1024], BF16))
    cbdt = e(nc.sbuf_tensor("p3_cbdt", [128, 1024], BF16))
    scq = [e(nc.sbuf_tensor(f"p3_scq{i}", [128, 1024], BF16)) for i in range(2)]
    cexq = [e(nc.sbuf_tensor(f"p3_cexq{i}", [128, 1024], BF16)) for i in range(2)]
    bdq = [e(nc.sbuf_tensor(f"p3_bdq{i}", [128, 1024], BF16)) for i in range(2)]
    wex = e(nc.sbuf_tensor("p3_wex", [128, 8], BF16))
    ssel = [e(nc.sbuf_tensor(f"p3_ssel{i}", [64, 1024], F32)) for i in range(2)]
    wcol = e(nc.sbuf_tensor("p3_wcol", [128, 8], BF16))
    s_all = e(nc.sbuf_tensor("p3_sall", [128, 4096], F32))
    s_bfc = e(nc.sbuf_tensor("p3_sbf", [128, 4096], BF16))
    yfm = e(nc.sbuf_tensor("p3_yfm", [128, 4096], BF16))
    psA = e(nc.psum_tensor("p3_psA", [128, 1024], F32))
    psB = e(nc.psum_tensor("p3_psB", [64, 1024], F32))
    psC = e(nc.psum_tensor("p3_psC", [128, 512], F32))

    htok, sgtok, bctok = [], [], []
    sttok = E.dst3.plan(E.nck)

    SPE, SDV, SAC, SGP = E.s_pe, E.s_dv, E.s_ac, E.s_g2
    sch = []
    for ck in range(nck):
        d = {}
        d["t_tp0"] = SPE.bump()
        d["v_etp0"] = SDV.bump()
        d["t_tp1"] = SPE.bump()
        d["v_sbf"] = SDV.bump()
        d["v_etp1"] = SDV.bump()
        d["t_cbt"] = SPE.bump()
        d["v_cbt"] = SDV.bump()
        d["v_sneg"] = SDV.bump()
        for s in range(NB + 1):
            if s < NB:
                d[f"t1_{s}"] = SPE.bump()
            if s >= 1:
                d[f"t5_{s - 1}"] = SPE.bump()
            if s < NB:
                d[f"t2_{s}"] = SPE.bump()
        for q in range(NB):
            d[f"a1_{q}"] = SAC.bump(2)
            d[f"a2_{q}"] = SAC.bump(2)
            if q >= 1:
                d[f"a3_{q - 1}"] = SAC.bump(2)
        d[f"a3_{NB - 1}"] = SAC.bump(2)
        for s in range(NB + 1):
            if s < NB:
                d[f"g0_{s}"] = SGP.bump()
            if s >= 1:
                d[f"g1_{s - 1}"] = SGP.bump()
        for q in range(NB):
            if q >= 1:
                d[f"v2_{q - 1}"] = SDV.bump(2)
            d[f"v1_{q}"] = SDV.bump(4)
        d[f"v2_{NB - 1}"] = SDV.bump(2)
        d["v_y2h0"] = SDV.bump(4)
        d["v_y2h1"] = SDV.bump(4)
        d["v_y2"] = SDV.bump(1)
        d["t_rms"] = SPE.bump()
        d["a_inv"] = SAC.bump()
        sch.append(d)

    def pget(dd, key, q, ck):
        if q >= 0:
            return sch[ck][f"{key}_{q}"]
        if ck >= 1:
            return sch[ck - 1][f"{key}_{q + NB}"]
        return None

    p1w = _P1Work(nc, e, E, nck) if (with_p1 and E.p1_nfb > 1) else None

    with nc.Block() as blk:
        @blk.sync
        def _(sy):
            for ck in range(nck):
                if ck >= 2:
                    sy.wait_ge(SPE.h, sch[ck - 2]["t_rms"])
                    sy.wait_ge(SDV.h, sch[ck - 2]["v_y2"])
                    sy.wait_ge(SAC.h, sch[ck - 2]["a_inv"])
                htok.append(E.dx.issue(
                    sy, hsb[ck % 2][:].rearrange("p (a i) -> p a i", a=32),
                    E.hs_sc[:, :, ck * 128:(ck + 1) * 128].transpose([1, 0, 2])))
                sgtok.append(E.dx2.issue(
                    sy, sgb[ck % 2][:].rearrange("p (a i) -> p a i", a=32),
                    E.sg_sc[:, :, ck * 128:(ck + 1) * 128].transpose([1, 0, 2])))
                bctok.append((
                    E.dcn.issue(sy, bcb[ck % 2][:, 0:128],
                                E.b_sc[:, ck * 128:(ck + 1) * 128]),
                    E.dcn.issue(sy, bcb[ck % 2][:, 128:256],
                                E.c_sc[:, ck * 128:(ck + 1) * 128])))
                if p1w:
                    for it in p1w.items_for(ck):
                        p1w.emit_sync(sy, it)

        @blk.gpsimd
        def _(g):
            for ck in range(nck):
                d = sch[ck]
                p = ck % 2
                p1it = p1w.items_for(ck) if p1w else []
                np1 = len(p1it)
                for s in range(NB + 1):
                    if s < NB:
                        q = s
                        hq = q * 8
                        pq = q % 2
                        t2p = pget(d, "t2", q - 2, ck)
                        if t2p is not None:
                            g.wait_ge(SPE.h, t2p)
                        i_v = E.i64[:, hq:hq + 8].unsqueeze(2) \
                            .to_broadcast([64, 8, 128])
                        s_v = E.sfm_all[0:64, ck * 128:(ck + 1) * 128] \
                            .unsqueeze(1).to_broadcast([64, 8, 128])
                        g.tensor_tensor(
                            ssel[pq][:].rearrange("p (a i) -> p a i", a=8),
                            i_v, s_v, ALU.mult).then_inc(SGP.h, 1)
                        g.drain()
                    if s >= 1:
                        q = s - 1
                        hq = q * 8
                        pq = q % 2
                        g.wait_ge(SAC.h, d[f"a1_{q}"])
                        t5p = pget(d, "t5", q - 2, ck)
                        if t5p is not None:
                            g.wait_ge(SPE.h, t5p)
                        c_v = bcb[p][:, 128:256].unsqueeze(1) \
                            .to_broadcast([128, 8, 128])
                        g.tensor_tensor(
                            cexq[pq][:].rearrange("p (a i) -> p a i", a=8),
                            c_v, esr[:].rearrange("p (a i) -> p a i", a=8),
                            ALU.mult).then_inc(SGP.h, 1)
                        g.drain()
                    if p1w and s < NB:
                        for it in p1w.gslice(ck * NB + s - 2):
                            p1w.emit_gp(g, it)
                if p1w and ck == nck - 1:
                    for gg in range(p1w.G - 2, p1w.G):
                        for it in p1w.gslice(gg):
                            p1w.emit_gp(g, it)

        @blk.tensor
        def _(t):
            for ck in range(nck):
                d = sch[ck]
                p = ck % 2
                p1it = p1w.items_for(ck) if p1w else []
                np1 = len(p1it)
                DmaStream.wait(t, htok[ck])
                DmaStream.wait(t, bctok[ck][0])
                DmaStream.wait(t, bctok[ck][1])
                if ck >= 1:
                    t.wait_ge(SAC.h, sch[ck - 1]["a_inv"])
                    t.wait_ge(SDV.h, sch[ck - 1]["v_y2"])
                    t.wait_ge(SAC.h, sch[ck - 1][f"a2_{NB - 1}"])
                    t.wait_ge(SDV.h, sch[ck - 1][f"v1_{NB - 1}"])
                for ft in range(16):
                    ins = t.transpose(psA[:].bitcast(BF16)
                                      [:, ft * 128:(ft + 1) * 128],
                                      hsb[p][:, ft * 128:(ft + 1) * 128],
                                      E.ident_b[:])
                ins.then_inc(SPE.h, 1)
                t.wait_ge(SDV.h, d["v_etp0"])
                for ft in range(16):
                    ins = t.transpose(psA[:].bitcast(BF16)
                                      [:, ft * 128:(ft + 1) * 128],
                                      hsb[p][:, 2048 + ft * 128:
                                             2048 + (ft + 1) * 128],
                                      E.ident_b[:])
                ins.then_inc(SPE.h, 1)
                t.wait_ge(SDV.h, d["v_etp1"])
                ins = t.transpose(psA[:].bitcast(BF16)[:, 0:128],
                                  bcb[p][:, 0:128], E.ident_b[:])
                t.matmul(psA[:, 64:192], bcb[p][:, 0:128],
                         bcb[p][:, 128:256], start=True, stop=True) \
                    .then_inc(SPE.h, 1)
                for s in range(NB + 1):
                    if s < NB:
                        q = s
                        pq = q % 2
                        a2p = pget(d, "a2", q - 1, ck)
                        if a2p is not None:
                            t.wait_ge(SAC.h, a2p)
                        if q == 0:
                            t.wait_ge(SDV.h, d["v_cbt"])
                        t.wait_ge(SGP.h, d[f"g0_{q}"])
                        for bb in range(2):
                            ins = t.matmul(psA[:, bb * 512:(bb + 1) * 512],
                                           E.ones64[:],
                                           ssel[pq][:, bb * 512:(bb + 1) * 512],
                                           start=True, stop=True)
                        ins.then_inc(SPE.h, 1)
                    if s >= 1:
                        q = s - 1
                        hq = q * 8
                        pq = q % 2
                        t.wait_ge(SDV.h, d[f"v1_{q}"])
                        t.wait_ge(SGP.h, d[f"g1_{q}"])
                        if q == 0:
                            t.wait_ge(SDV.h, d["v_sbf"])
                        a3p = pget(d, "a3", q - 1, ck)
                        if a3p is not None:
                            t.wait_ge(SAC.h, a3p)
                        v2p = pget(d, "v2", q - 1, ck)
                        if v2p is not None:
                            t.wait_ge(SDV.h, v2p)
                        for hh in range(8):
                            h = hq + hh
                            ysl = psB[0:64, hh * 128:(hh + 1) * 128]
                            t.matmul(ysl, s_bfc[:, h * 64:(h + 1) * 64],
                                     cexq[pq][:, hh * 128:(hh + 1) * 128],
                                     start=(hh in (0, 4)), stop=False)
                            t.matmul(ysl, hs_tm[:, h * 64:(h + 1) * 64],
                                     scq[pq][:, hh * 128:(hh + 1) * 128],
                                     start=False, stop=(hh in (3, 7)))
                            dsl = psC[:, hh * 64:(hh + 1) * 64]
                            ins = t.matmul(dsl,
                                           bdq[pq][:, hh * 128:(hh + 1) * 128],
                                           hs_tm[:, h * 64:(h + 1) * 64],
                                           start=(hh == 0), stop=(hh == 7))
                        ins.then_inc(SPE.h, 1)
                    if s < NB:
                        q = s
                        hq = q * 8
                        pq = q % 2
                        t.wait_ge(SAC.h, d[f"a1_{q}"])
                        if q == 0:
                            t.wait_ge(SDV.h, d["v_sneg"])
                        for bb in range(2):
                            t.matmul(psA[:, bb * 512:(bb + 1) * 512],
                                     E.ones64[:],
                                     ssel[pq][:, bb * 512:(bb + 1) * 512],
                                     start=True, stop=False)
                        for bb in range(2):
                            iv = E.i64[:, hq + bb * 4:hq + bb * 4 + 4] \
                                .unsqueeze(2).to_broadcast([64, 4, 128])
                            t.matmul(psA[:, bb * 512:(bb + 1) * 512],
                                     sneg[:], iv, start=False, stop=False)
                        for bb in range(2):
                            rhm = E.negm_f[:].unsqueeze(1) \
                                .to_broadcast([128, 4, 128])
                            ins = t.matmul(psA[:, bb * 512:(bb + 1) * 512],
                                           E.ident_f[:], rhm,
                                           start=False, stop=True)
                        ins.then_inc(SPE.h, 1)
                    if p1w and s < NB - 1:
                        for it in p1w.gslice(ck * NB + s):
                            p1w.emit_pe(t, it)
                last_sl = p1w.gslice(ck * NB + NB - 1) if p1w else []
                for it in last_sl[:1]:
                    p1w.emit_pe(t, it)
                for hf in range(2):
                    t.wait_ge(SDV.h, d[f"v_y2h{hf}"])
                    for ft in range(hf * 16, hf * 16 + 16):
                        ins = t.matmul(psA[:, 0:1],
                                       sgb[p][:, ft * 128:(ft + 1) * 128],
                                       E.ones_b[:],
                                       start=(ft == 0), stop=(ft == 31))
                ins.then_inc(SPE.h, 1)
                for it in last_sl[1:]:
                    p1w.emit_pe(t, it)

        @blk.vector
        def _(v):
            for ck in range(nck):
                d = sch[ck]
                p = ck % 2
                p1it = p1w.items_for(ck) if p1w else []
                np1 = len(p1it)
                if ck == 0:
                    v.memset(s_all[:], 0.0)
                    v.drain()
                if p1w:
                    for qq in (0, 1):
                        for it in p1w.gslice(ck * NB + qq - 2):
                            p1w.emit_dv(v, it)
                v.wait_ge(SPE.h, d["t_tp0"])
                v.tensor_copy(hs_tm[:, 0:2048], psA[:].bitcast(BF16)) \
                    .then_inc(SDV.h, 1)
                v.wait_ge(SPE.h, d["t_tp1"])
                v.tensor_copy(s_bfc[:], s_all[:]).then_inc(SDV.h, 1)
                v.tensor_copy(hs_tm[:, 2048:4096], psA[:].bitcast(BF16)) \
                    .then_inc(SDV.h, 1)
                v.wait_ge(SPE.h, d["t_cbt"])
                v.tensor_copy(b_tm[:], psA[:].bitcast(BF16)[:, 0:128])
                v.tensor_tensor(cbtm[:], psA[:, 64:192], E.triu_f[:],
                                ALU.mult).then_inc(SDV.h, 1)
                v.tensor_scalar(sneg[:], E.sfm_all[0:64, ck * 128:(ck + 1) * 128],
                                -1.0, None, ALU.mult).then_inc(SDV.h, 1)
                def emit_ssl(qq):
                    hq2 = qq * 8
                    v.wait_ge(SPE.h, d[f"t5_{qq}"])
                    esl_v = esl[:, hq2:hq2 + 8].unsqueeze(2) \
                        .to_broadcast([128, 8, 64])
                    ssl = s_all[:, hq2 * 64:hq2 * 64 + 512]
                    v.tensor_tensor(ssl.rearrange("p (a i) -> p a i", a=8),
                                    ssl.rearrange("p (a i) -> p a i", a=8),
                                    esl_v, ALU.mult).then_inc(SDV.h, 1)
                    v.drain()
                    v.tensor_tensor(ssl, ssl, psC[:], ALU.add) \
                        .then_inc(SDV.h, 1)
                    v.drain()

                for q in range(NB):
                    if p1w and q >= 2:
                        for it in p1w.gslice(ck * NB + q - 2):
                            p1w.emit_dv(v, it)
                    if q >= 1:
                        emit_ssl(q - 1)
                    hq = q * 8
                    pq = q % 2
                    v.wait_ge(SAC.h, d[f"a2_{q}"])
                    t5p = pget(d, "t5", q - 2, ck)
                    if t5p is not None:
                        v.wait_ge(SPE.h, t5p)
                    dtv_b = E.dtbf_all[:, ck * NH + hq:ck * NH + hq + 8] \
                        .unsqueeze(2).to_broadcast([128, 8, 128])
                    cb_v = cbtm[:].unsqueeze(1).to_broadcast([128, 8, 128])
                    v.tensor_tensor(cbdt[:].rearrange("p (a i) -> p a i", a=8),
                                    cb_v, dtv_b, ALU.mult).then_inc(SDV.h, 1)
                    v.drain()
                    v.tensor_tensor(scq[pq][:], dexp[:], cbdt[:], ALU.mult) \
                        .then_inc(SDV.h, 1)
                    v.tensor_tensor(wcol[:], wex[:],
                                    E.dtbf_all[:, ck * NH + hq:ck * NH + hq + 8],
                                    ALU.mult).then_inc(SDV.h, 1)
                    v.drain()
                    wc_v = wcol[:].unsqueeze(2).to_broadcast([128, 8, 128])
                    bt_v = b_tm[:].unsqueeze(1).to_broadcast([128, 8, 128])
                    v.tensor_tensor(bdq[pq][:].rearrange("p (a i) -> p a i", a=8),
                                    bt_v, wc_v, ALU.mult).then_inc(SDV.h, 1)
                emit_ssl(NB - 1)
                DmaStream.wait(v, sgtok[ck])
                v.wait_ge(SAC.h, d[f"a3_{NB - 1}"])
                for hf in range(2):
                    h0, h1 = hf * 2048, (hf + 1) * 2048
                    df_v = E.dfull[:, hf * 16:(hf + 1) * 16].unsqueeze(2) \
                        .to_broadcast([128, 16, 128])
                    v.tensor_tensor(
                        hsb[p][:, h0:h1].rearrange("p (a i) -> p a i", a=16),
                        hsb[p][:, h0:h1].rearrange("p (a i) -> p a i", a=16),
                        df_v, ALU.mult).then_inc(SDV.h, 1)
                    v.drain()
                    v.tensor_tensor(yfm[:, h0:h1], yfm[:, h0:h1],
                                    hsb[p][:, h0:h1], ALU.add) \
                        .then_inc(SDV.h, 1)
                    v.drain()
                    v.tensor_tensor(yfm[:, h0:h1], yfm[:, h0:h1],
                                    sgb[p][:, h0:h1], ALU.mult) \
                        .then_inc(SDV.h, 1)
                    v.drain()
                    v.tensor_tensor(sgb[p][:, h0:h1], yfm[:, h0:h1],
                                    yfm[:, h0:h1], ALU.mult) \
                        .then_inc(SDV.h, 1)
                    v.drain()
                nw_v = E.nw_bf[:].unsqueeze(2).to_broadcast([128, 32, 128])
                v.tensor_tensor(yfm[:].rearrange("p (a i) -> p a i", a=32),
                                yfm[:].rearrange("p (a i) -> p a i", a=32),
                                nw_v, ALU.mult).then_inc(SDV.h, 1)
                if p1w and ck == nck - 1:
                    for g in range(p1w.G - 2, p1w.G):
                        for it in p1w.gslice(g):
                            p1w.emit_dv(v, it)

        @blk.scalar
        def _(s):
            for ck in range(nck):
                d = sch[ck]
                p1it = p1w.items_for(ck) if p1w else []
                np1 = len(p1it)
                for q in range(NB):
                    if p1w:
                        for it in p1w.gslice(ck * NB + q - 1):
                            p1w.emit_ac(s, it)
                    hq = q * 8
                    pq = q % 2
                    s.wait_ge(SPE.h, d[f"t1_{q}"])
                    g1p = pget(d, "g1", q - 1, ck)
                    if g1p is not None:
                        s.wait_ge(SGP.h, g1p)
                    v1p = pget(d, "v1", q - 1, ck)
                    if v1p is not None:
                        s.wait_ge(SDV.h, v1p)
                    if ck >= 1:
                        s.wait_ge(SDV.h, sch[ck - 1][f"v2_{q}"])
                    s.activation(esr[:], psA[:], ACT.Exp)
                    s.activation(esl[:, hq:hq + 8], psA[:, 127::128],
                                 ACT.Exp).then_inc(SAC.h, 2)
                    s.wait_ge(SPE.h, d[f"t2_{q}"])
                    s.activation(dexp[:], psA[:], ACT.Exp)
                    s.activation(wex[:], psA[:, 127::128], ACT.Exp) \
                        .then_inc(SAC.h, 2)
                    qq = q - 1
                    if qq >= 0:
                        if qq == 0 and ck >= 1:
                            DmaStream.wait(s, sttok[ck - 1])
                        _p3_yevac(s, E, d, yfm, psB, qq, SPE, SAC, None)
                _p3_yevac(s, E, d, yfm, psB, NB - 1, SPE, SAC, None)
                if p1w and ck == nck - 1:
                    for g in range(p1w.G - 1, p1w.G):
                        for it in p1w.gslice(g):
                            p1w.emit_ac(s, it)
                s.wait_ge(SPE.h, d["t_rms"])
                s.activation(E.inv_at[:, ck:ck + 1], psA[:, 0:1], ACT.Ln,
                             bias=E.c_epsat[:], scale=1.0 / INTER)
                s.drain()
                s.activation(E.inv_at[:, ck:ck + 1], E.inv_at[:, ck:ck + 1],
                             ACT.Exp, scale=-0.5).then_inc(SAC.h, 1)
                s.wait_ge(SDV.h, d["v_y2"])
                tk = E.dst3.issue(
                    s, E.ynw_sc[ck].transpose([1, 0, 2]),
                    yfm[:].rearrange("p (a i) -> p a i", a=32))
                assert tk == sttok[ck], (tk, sttok[ck])

    _quiesce_stores(nc, E)
    ps_.close()


class _P1Work:

    def __init__(self, nc, e, E, nck):
        self.nc, self.E = nc, E
        TBF = E.p1_TBF
        self.TBF = TBF
        self.nttf = TBF // 128
        self.h_sc = nc.dram_tensor("p13_h", [NIT_DN, 128, TBF], BF16).ap()
        self.xnf = e(nc.sbuf_tensor("q1_xn", [128, NHT * TBF], BF16))
        self.wtg = [e(nc.sbuf_tensor(f"q1_wg{i}", [128, 1024], BF16))
                    for i in range(2)]
        self.wtu = [e(nc.sbuf_tensor(f"q1_wu{i}", [128, 1024], BF16))
                    for i in range(2)]
        self.sgst = [e(nc.sbuf_tensor(f"q1_sg{i}", [128, TBF], BF16))
                     for i in range(2)]
        self.hstg = [e(nc.sbuf_tensor(f"q1_hs{i}", [128, TBF], BF16))
                     for i in range(2)]
        self.wdn = e(nc.sbuf_tensor("q1_wdn", [128, NIT_DN * 512], BF16))
        self.hslab = [e(nc.sbuf_tensor(f"q1_hl{i}", [128, 22 * 128], BF16))
                      for i in range(2)]
        self.sqb = [e(nc.sbuf_tensor(f"q1_sqb{i}", [128, 512], F32))
                    for i in range(2)]
        self.sscol = e(nc.sbuf_tensor("q1_ssc", [128, 1], F32))
        self.ssq = e(nc.sbuf_tensor("q1_ssq", [128, 4], F32))
        self.fst = [e(nc.sbuf_tensor(f"q1_fs{i}", [128, 512], BF16))
                    for i in range(2)]
        self.pool = e(nc.psum_tensor("q1_pool", [128, 1536], F32))
        self.slot_state = [None, None, None]
        self.slot_i = 0

        items = []
        pending_inv = None
        for fb in range(1, E.p1_nfb):
            items.append(("xnf", fb))
            if pending_inv is not None:
                items.append(pending_inv)
                pending_inv = None
            for j in range(NPAIR_GU):
                items.append(("gu", fb, j))
            for oq in range(4):
                items.append(("dnl", fb, oq))
                for tt in range(self.nttf):
                    items.append(("dn", fb, oq, tt))
            pending_inv = ("inv", fb)
        if pending_inv is not None:
            items.append(("pad", 0))
            items.append(pending_inv)
        self.items = items
        self.G = nck * 8
        self.slices = []
        n = len(items)
        for ck in range(nck):
            lo = n * ck // nck
            hi = n * (ck + 1) // nck
            self.slices.append(items[lo:hi])

        QPE, QAC, QDV = E.s_qpe, E.s_qac, E.s_qdv
        self.sch = {}
        ndst2 = sum(1 for it in items if it[0] in ("gu", "dn"))
        self.d2tok = E.dst2.plan(ndst2)
        d2i = 0
        si = 0
        for it in items:
            d = {}
            if it[0] == "gu":
                d["slot_g"] = si % 3
                d["slot_u"] = (si + 1) % 3
                si += 2
                d["pe_gh"] = QPE.bump()
                d["pe_g"] = QPE.bump()
                d["pe_uh"] = QPE.bump()
                d["pe_u"] = QPE.bump()
                d["silu"] = QAC.bump()
                d["ucp"] = QAC.bump()
                d["htt"] = QDV.bump()
                d["hst"] = self.d2tok[d2i]
                d2i += 1
            elif it[0] == "dn":
                d["slot"] = si % 3
                si += 1
                d["pe_h"] = QPE.bump()
                d["pe"] = QPE.bump()
                d["sq"] = QAC.bump()
                d["fcp"] = QAC.bump()
                d["dv"] = QDV.bump(2)
                d["fst"] = self.d2tok[d2i]
                d2i += 1
            elif it[0] == "inv":
                d["inv"] = QAC.bump()
            self.sch[it] = d
        self.toks = {}
        self.xnf_tok = {}

    def items_for(self, ck):
        return self.slices[ck]

    def slot(self, i):
        return self.pool[:, 512 * i:512 * (i + 1)]

    def wait_slot(self, eng, i):
        st_ = self.slot_state[i]
        if st_ is not None:
            for sem, val in st_:
                eng.wait_ge(sem.h, val)

    def gslice(self, g):
        if g < 0 or g >= self.G:
            return []
        n = len(self.items)
        return self.items[n * g // self.G:n * (g + 1) // self.G]

    def emit_sync(self, sy, it):
        E = self.E
        if it[0] == "xnf":
            fb = it[1]
            if fb >= 2:
                pg = self.sch[("gu", fb - 1, NPAIR_GU - 1)]
                sy.wait_ge(E.s_qpe.h, pg["pe_u"])
            self.xnf_tok[fb] = E.dxp.issue(
                sy, self.xnf[:].rearrange("p (a t) -> p a t", a=NHT),
                E.xn_sc[:, :, fb * self.TBF:(fb + 1) * self.TBF]
                .transpose([1, 0, 2]))
        elif it[0] == "gu":
            fb, j = it[1], it[2]
            if j >= 1:
                dp = self.sch[("gu", fb, j - 1)]
            elif fb >= 2:
                dp = self.sch[("gu", fb - 1, NPAIR_GU - 1)]
            else:
                dp = None
            for half, gkey, wt, wsrc, strm in (
                    (0, "pe_gh", self.wtg, E.wgu_sc[j], E.dxp2),
                    (1, "pe_g", self.wtg, E.wgu_sc[j], E.dxp2),
                    (0, "pe_uh", self.wtu, E.wgu_sc[j + NPAIR_GU], E.dxp3),
                    (1, "pe_u", self.wtu, E.wgu_sc[j + NPAIR_GU], E.dxp3)):
                if dp is not None:
                    sy.wait_ge(E.s_qpe.h, dp[gkey])
                self.toks[(gkey, fb, j)] = strm.issue(
                    sy, wt[half][:], wsrc[:, half * 1024:(half + 1) * 1024])
        elif it[0] == "dnl":
            fb, oq = it[1], it[2]
            if oq >= 1:
                sy.wait_ge(E.s_qpe.h, self.sch[("dn", fb, oq - 1,
                                                self.nttf - 1)]["pe"])
            elif fb >= 2:
                sy.wait_ge(E.s_qpe.h, self.sch[("dn", fb - 1, 3,
                                                self.nttf - 1)]["pe"])
            if oq == 0:
                DmaStream.wait(sy, self.sch[("gu", fb, NPAIR_GU - 1)]["hst"])
                DmaStream.wait(sy, self.sch[("gu", fb, NPAIR_GU - 2)]["hst"])
            self.toks[("wdn", fb, oq)] = E.dxp.issue(
                sy, self.wdn[:].rearrange("p (a i) -> p a i", a=NIT_DN),
                E.wdn_sc[oq])
        elif it[0] == "dn":
            fb, oq, tt = it[1], it[2], it[3]
            k = oq * self.nttf + tt
            if k >= 1:
                oq2, tt2 = divmod(k - 1, self.nttf)
                dp = self.sch[("dn", fb, oq2, tt2)]
            elif fb >= 2:
                dp = self.sch[("dn", fb - 1, 3, self.nttf - 1)]
            else:
                dp = None
            hsrc = self.h_sc[:, :, tt * 128:(tt + 1) * 128].transpose([1, 0, 2])
            if dp is not None:
                sy.wait_ge(E.s_qpe.h, dp["pe_h"])
            self.toks[("hla", fb, oq, tt)] = E.dxp2.issue(
                sy, self.hslab[0][:].rearrange("p (a i) -> p a i", a=22),
                hsrc[:, 0:22])
            if dp is not None:
                sy.wait_ge(E.s_qpe.h, dp["pe"])
            self.toks[("hlb", fb, oq, tt)] = E.dxp2.issue(
                sy, self.hslab[1][:].rearrange("p (a i) -> p a i", a=22),
                hsrc[:, 22:44])

    def emit_pe(self, t, it):
        E = self.E
        if it[0] == "gu":
            fb, j = it[1], it[2]
            d = self.sch[it]
            DmaStream.wait(t, self.xnf_tok[fb])
            for slot_key, wt, keys in ((d["slot_g"], self.wtg,
                                        ("pe_gh", "pe_g")),
                                       (d["slot_u"], self.wtu,
                                        ("pe_uh", "pe_u"))):
                self.wait_slot(t, slot_key)
                for half in range(2):
                    DmaStream.wait(t, self.toks[(keys[half], fb, j)])
                    for h2 in range(NHT // 2):
                        ht = half * (NHT // 2) + h2
                        ins = t.matmul(
                            self.slot(slot_key),
                            wt[half][:, h2 * 128:(h2 + 1) * 128],
                            self.xnf[:, ht * self.TBF:(ht + 1) * self.TBF],
                            start=(ht == 0), stop=(ht == NHT - 1))
                    ins.then_inc(E.s_qpe.h, 1)
            self.slot_state[d["slot_g"]] = [(E.s_qac, d["silu"])]
            self.slot_state[d["slot_u"]] = [(E.s_qac, d["ucp"])]
        elif it[0] == "dn":
            fb, oq, tt = it[1], it[2], it[3]
            k = oq * self.nttf + tt
            d = self.sch[it]
            DmaStream.wait(t, self.toks[("wdn", fb, oq)])
            self.wait_slot(t, d["slot"])
            DmaStream.wait(t, self.toks[("hla", fb, oq, tt)])
            for i2 in range(22):
                ins = t.matmul(self.slot(d["slot"]),
                               self.hslab[0][:, i2 * 128:(i2 + 1) * 128],
                               self.wdn[:, i2 * 512:(i2 + 1) * 512],
                               start=(i2 == 0), stop=False)
            ins.then_inc(E.s_qpe.h, 1)
            DmaStream.wait(t, self.toks[("hlb", fb, oq, tt)])
            for i2 in range(22, NIT_DN):
                ins = t.matmul(self.slot(d["slot"]),
                               self.hslab[1][:, (i2 - 22) * 128:
                                             (i2 - 21) * 128],
                               self.wdn[:, i2 * 512:(i2 + 1) * 512],
                               start=False, stop=(i2 == NIT_DN - 1))
            ins.then_inc(E.s_qpe.h, 1)
            self.slot_state[d["slot"]] = [(E.s_qac, d["fcp"])]

    def emit_ac(self, s, it):
        E = self.E
        if it[0] == "gu":
            fb, j = it[1], it[2]
            d = self.sch[it]
            s.wait_ge(E.s_qpe.h, d["pe_g"])
            if j >= 2:
                s.wait_ge(E.s_qdv.h, self.sch[("gu", fb, j - 2)]["htt"])
            elif fb >= 2:
                s.wait_ge(E.s_qdv.h,
                          self.sch[("gu", fb - 1, NPAIR_GU - 2 + j)]["htt"])
            s.activation(self.sgst[j % 2][:], self.slot(d["slot_g"]),
                         ACT.Silu).then_inc(E.s_qac.h, 1)
            s.wait_ge(E.s_qpe.h, d["pe_u"])
            if j >= 2:
                DmaStream.wait(s, self.sch[("gu", fb, j - 2)]["hst"])
            elif fb >= 2:
                DmaStream.wait(
                    s, self.sch[("gu", fb - 1, NPAIR_GU - 2 + j)]["hst"])
            s.activation(self.hstg[j % 2][:], self.slot(d["slot_u"]),
                         ACT.Copy).then_inc(E.s_qac.h, 1)
        elif it[0] == "dn":
            fb, oq, tt = it[1], it[2], it[3]
            d = self.sch[it]
            s.wait_ge(E.s_qpe.h, d["pe"])
            k = oq * self.nttf + tt
            if k >= 2:
                oq2, tt2 = divmod(k - 2, self.nttf)
                s.wait_ge(E.s_qdv.h, self.sch[("dn", fb, oq2, tt2)]["dv"])
            elif fb >= 2:
                kp = 4 * self.nttf + (k - 2)
                oq2, tt2 = divmod(kp, self.nttf)
                s.wait_ge(E.s_qdv.h, self.sch[("dn", fb - 1, oq2, tt2)]["dv"])
            s.activation(self.sqb[k % 2][:], self.slot(d["slot"]), ACT.Square) \
                .then_inc(E.s_qac.h, 1)
            if k >= 2:
                oq2, tt2 = divmod(k - 2, self.nttf)
                DmaStream.wait(s, self.sch[("dn", fb, oq2, tt2)]["fst"])
            elif fb >= 2:
                kp = 4 * self.nttf + (k - 2)
                oq2, tt2 = divmod(kp, self.nttf)
                DmaStream.wait(s, self.sch[("dn", fb - 1, oq2, tt2)]["fst"])
            s.activation(self.fst[k % 2][:], self.slot(d["slot"]), ACT.Copy) \
                .then_inc(E.s_qac.h, 1)
        elif it[0] == "inv":
            fb = it[1]
            d = self.sch[it]
            for tt2 in range(self.nttf):
                s.wait_ge(E.s_qdv.h, self.sch[("dn", fb, 3, tt2)]["dv"])
            t0 = fb * self.nttf
            s.activation(E.inv_ff[:, t0:t0 + self.nttf],
                         self.ssq[:, 0:self.nttf], ACT.Ln,
                         bias=E.c_epsff[:], scale=1.0 / H)
            s.drain()
            s.activation(E.inv_ff[:, t0:t0 + self.nttf],
                         E.inv_ff[:, t0:t0 + self.nttf],
                         ACT.Exp, scale=-0.5).then_inc(E.s_qac.h, 1)

    def emit_dv(self, v, it):
        E = self.E
        if it[0] == "gu":
            fb, j = it[1], it[2]
            d = self.sch[it]
            v.wait_ge(E.s_qac.h, d["ucp"])
            v.tensor_tensor(self.hstg[j % 2][:], self.hstg[j % 2][:],
                            self.sgst[j % 2][:], ALU.mult) \
                .then_inc(E.s_qdv.h, 1)
        elif it[0] == "dn":
            fb, oq, tt = it[1], it[2], it[3]
            k = oq * self.nttf + tt
            d = self.sch[it]
            v.wait_ge(E.s_qac.h, d["sq"])
            v.reduce_sum(self.sscol[:], self.sqb[k % 2][:], axis=AX.X) \
                .then_inc(E.s_qdv.h, 1)
            v.drain()
            if oq == 0:
                if fb >= 2 and tt == 0:
                    iv = self.sch[("inv", fb - 1)]
                    v.wait_ge(E.s_qac.h, iv["inv"])
                v.tensor_copy(self.ssq[:, tt:tt + 1], self.sscol[:]) \
                    .then_inc(E.s_qdv.h, 1)
            else:
                v.tensor_tensor(self.ssq[:, tt:tt + 1], self.ssq[:, tt:tt + 1],
                                self.sscol[:], ALU.add).then_inc(E.s_qdv.h, 1)
            v.drain()

    def emit_gp(self, g, it):
        E = self.E
        if it[0] == "gu":
            fb, j = it[1], it[2]
            d = self.sch[it]
            g.wait_ge(E.s_qdv.h, d["htt"])
            if fb >= 2 and j == 0:
                DmaStream.wait(g, self.toks[("hlb", fb - 1, 3, self.nttf - 1)])
            tk = E.dst2.issue(g, self.h_sc[j], self.hstg[j % 2][:])
            assert tk == d["hst"], (tk, d["hst"])
        elif it[0] == "dn":
            fb, oq, tt = it[1], it[2], it[3]
            d = self.sch[it]
            g.wait_ge(E.s_qac.h, d["fcp"])
            ttg = fb * self.nttf + tt
            tk = E.dst2.issue(
                g, E.ffn_sc[ttg, :, oq * 512:(oq + 1) * 512],
                self.fst[(oq * self.nttf + tt) % 2][:])
            assert tk == d["fst"], (tk, d["fst"])


def _p3_yevac(s, E, d, yfm, psB, q, SPE, SAC, dprev):
    hq = q * 8
    s.wait_ge(SPE.h, d[f"t5_{q}"])
    s.activation(
        yfm[0:64, hq * 64:hq * 64 + 512].rearrange("p (a i) -> p a i", a=4),
        psB[0:64, :].rearrange("p (a b i) -> p a b i", a=4, b=2)[:, :, 0, :],
        ACT.Copy)
    s.activation(
        yfm[64:128, hq * 64:hq * 64 + 512].rearrange("p (a i) -> p a i", a=4),
        psB[0:64, :].rearrange("p (a b i) -> p a b i", a=4, b=2)[:, :, 1, :],
        ACT.Copy).then_inc(SAC.h, 2)


def _p4_out(nc, st, E):
    ps_ = ExitStack()
    e = ps_.enter_context
    T, ntt = E.T, E.ntt

    wout = e(nc.sbuf_tensor("p4_w", [128, 32 * 1024], BF16))
    ysb = [e(nc.sbuf_tensor(f"p4_y{i}", [128, 32 * 128], BF16)) for i in range(2)]
    xsb = [e(nc.sbuf_tensor(f"p4_x{i}", [128, 1024], F32)) for i in range(2)]
    fsb = [e(nc.sbuf_tensor(f"p4_f{i}", [128, 1024], BF16)) for i in range(2)]
    st1 = e(nc.sbuf_tensor("p4_s1", [128, 512], F32))
    ost = [e(nc.sbuf_tensor(f"p4_o{i}", [128, 512], F32)) for i in range(4)]
    psO = e(nc.psum_tensor("p4_ps", [128, 2048], F32))

    SPE, SDV, SAC = E.s_pe, E.s_dv, E.s_ac
    otoks = E.dst.plan(2 * ntt * 2)

    for oh in range(2):
        mm_d = []
        ev_d = []
        for tt in range(ntt):
            for ob in range(2):
                mm_d.append(E.s_pe.bump())
                ev_d.append(E.s_dv.bump(2))
        wtok, ytok, xtok, ftok2 = [None], [], [], []

        with nc.Block() as blk:
            @blk.gpsimd
            def _(g):
                wtok[0] = E.dgp.issue(
                    g, wout[:].rearrange("p (a f) -> p a f", a=32),
                    E.woutT[:, oh * 1024:(oh + 1) * 1024].rearrange(
                        "(a p) f -> p a f", p=128))

            @blk.sync
            def _(sy):
                for tt in range(ntt):
                    if tt >= 2:
                        sy.wait_ge(SPE.h, mm_d[2 * (tt - 2) + 1])
                        sy.wait_ge(SDV.h, ev_d[2 * (tt - 2) + 1])
                    ytok.append(E.dx.issue(
                        sy, ysb[tt % 2][:].rearrange("p (a i) -> p a i", a=32),
                        E.ynw_sc[tt].transpose([1, 0, 2])))
                    xtok.append(E.dx2.issue(
                        sy, xsb[tt % 2][:],
                        E.x_in[tt * 128:(tt + 1) * 128,
                               oh * 1024:(oh + 1) * 1024]))
                    ftok2.append(E.dcn.issue(
                        sy, fsb[tt % 2][:],
                        E.ffn_sc[tt, :, oh * 1024:(oh + 1) * 1024]))

            @blk.tensor
            def _(t):
                DmaStream.wait(t, wtok[0])
                for tt in range(ntt):
                    DmaStream.wait(t, ytok[tt])
                    for ob in range(2):
                        k = tt * 2 + ob
                        slot = k % 4
                        if k >= 4:
                            t.wait_ge(SDV.h, ev_d[k - 4])
                        for ft in range(32):
                            ins = t.matmul(
                                psO[:, slot * 512:(slot + 1) * 512],
                                ysb[tt % 2][:, ft * 128:(ft + 1) * 128],
                                wout[:, ft * 1024 + ob * 512:
                                     ft * 1024 + ob * 512 + 512],
                                start=(ft == 0), stop=(ft == 31))
                        ins.then_inc(SPE.h, 1)

            @blk.vector
            def _(v):
                for tt in range(ntt):
                    DmaStream.wait(v, xtok[tt])
                    DmaStream.wait(v, ftok2[tt])
                    for ob in range(2):
                        k = tt * 2 + ob
                        slot = k % 4
                        gidx = oh * 2 * ntt + k
                        v.wait_ge(SPE.h, mm_d[k])
                        if gidx >= 4:
                            DmaStream.wait(v, otoks[gidx - 4])
                        v.scalar_tensor_tensor(
                            st1[:], psO[:, slot * 512:(slot + 1) * 512],
                            E.inv_at[:, tt:tt + 1],
                            xsb[tt % 2][:, ob * 512:(ob + 1) * 512],
                            ALU.mult, ALU.add).then_inc(SDV.h, 1)
                        v.drain()
                        v.scalar_tensor_tensor(
                            ost[k % 4][:], fsb[tt % 2][:, ob * 512:(ob + 1) * 512],
                            E.inv_ff[:, tt:tt + 1], st1[:],
                            ALU.mult, ALU.add).then_inc(SDV.h, 1)
                        v.drain()

            @blk.scalar
            def _(s):
                for tt in range(ntt):
                    for ob in range(2):
                        k = tt * 2 + ob
                        gidx = oh * 2 * ntt + k
                        s.wait_ge(SDV.h, ev_d[k])
                        tk = E.dst.issue(
                            s, E.out[tt * 128:(tt + 1) * 128,
                                     oh * 1024 + ob * 512:
                                     oh * 1024 + (ob + 1) * 512],
                            ost[k % 4][:])
                        assert tk == otoks[gidx]

    with nc.Block() as blk:
        @blk.sync
        def _(sy):
            for tk in otoks[-3:]:
                DmaStream.wait(sy, tk)
    ps_.close()


_prog_cache = {}


def _host_prep(inputs):
    w = {}
    w["winT"] = np.ascontiguousarray(np.asarray(inputs["L0_attn_in_proj_w"]).T)
    w["woutT"] = np.ascontiguousarray(np.asarray(inputs["L0_attn_out_proj_w"]).T)
    cw = np.asarray(inputs["L0_attn_conv_w"]).reshape(34, 128, 4)
    w["convw"] = np.ascontiguousarray(cw.transpose(1, 0, 2).reshape(128, 136))
    w["convb"] = np.ascontiguousarray(
        np.asarray(inputs["L0_attn_conv_b"]).reshape(34, 128).T)
    w["dtb"] = np.asarray(inputs["L0_attn_dt_bias"]).reshape(1, 64)
    w["alog"] = np.asarray(inputs["L0_attn_A_log"]).reshape(1, 64)
    w["dvec"] = np.asarray(inputs["L0_attn_D"]).reshape(1, 64)
    w["nw"] = np.ascontiguousarray(
        np.asarray(inputs["L0_attn_norm_weight"]).reshape(32, 128).T)
    w["gup"] = np.asarray(inputs["L0_ffn_gate_up_packed"])
    w["gus"] = np.ascontiguousarray(np.asarray(inputs["L0_ffn_gate_up_scales"]).T)
    w["dnp"] = np.asarray(inputs["L0_ffn_down_packed"])
    w["dns"] = np.ascontiguousarray(np.asarray(inputs["L0_ffn_down_scales"]).T)
    return {k: np.ascontiguousarray(v) for k, v in w.items()}


def kernel(**inputs):
    x = np.asarray(inputs["L0_attn_hidden_states"])
    B = x.shape[0]
    w = _host_prep(inputs)
    key = ("full", 2048)
    if key not in _prog_cache:
        _prog_cache[key] = build(T=2048)
    nc = _prog_cache[key]
    in_maps = []
    for c in range(B):
        m = {"x_in": np.ascontiguousarray(np.asarray(x[c], dtype=np.float32))}
        m.update(w)
        in_maps.append(m)
    res = run_bass_kernel_spmd(nc, in_maps, core_ids=list(range(B)))
    outs = [res.results[c]["out"] for c in range(B)]
    return np.stack(outs, axis=0).astype(np.float32)


# version 24 (final)
# speedup vs baseline: 1.1588x; 1.0300x over previous
import numpy as np
from contextlib import ExitStack

import concourse.bass as bass
import concourse.mybir as mybir
from concourse.bass_utils import run_bass_kernel_spmd

F32 = mybir.dt.float32
BF16 = mybir.dt.bfloat16
F16 = mybir.dt.float16
I32 = mybir.dt.int32
U16 = mybir.dt.uint16
ALU = mybir.AluOpType
ACT = mybir.ActivationFunctionType
AX = mybir.AxisListType

H = 2048
PROJ = 8512
INTER = 4096
CONVD = 4352
NH = 64
HD = 64
NST = 128
FI = 5632
GUD = 11264
L = 128
NHT = H // 128
NFT_G = INTER // 128
NOT_GU = GUD // 128
NPAIR_GU = FI // 128
NOT_DN = H // 128
NIT_DN = FI // 128


class Sem:
    def __init__(self, st, nc, name):
        self.h = st.enter_context(nc.semaphore(name))
        self.n = 0

    def bump(self, k=1):
        self.n += k
        return self.n


class DmaStream:

    def __init__(self, st, nc, name, depth=2):
        self.sems = [Sem(st, nc, f"{name}{i}") for i in range(depth)]
        self.i = 0

    def issue(self, eng, out, in_, **kw):
        s = self.sems[self.i % len(self.sems)]
        self.i += 1
        if s.n > 0:
            eng.wait_ge(s.h, s.n)
        eng.dma_start(out, in_, **kw).then_inc(s.h, 16)
        s.n += 16
        return (s, s.n)

    def plan(self, n):
        sims = [s.n for s in self.sems]
        toks = []
        for k in range(n):
            j = (self.i + k) % len(self.sems)
            sims[j] += 16
            toks.append((self.sems[j], sims[j]))
        return toks

    @staticmethod
    def wait(eng, tok):
        eng.wait_ge(tok[0].h, tok[1])


class Env:
    pass


def _quiesce_stores(nc, E):
    with nc.Block() as blk:
        @blk.sync
        def _(sy):
            for strm in (E.dst, E.dst2, E.dst3):
                for s in strm.sems:
                    if s.n > 0:
                        sy.wait_ge(s.h, s.n)


def build(T=2048, phases=(0, 1, 2, 3, 4)):
    nck = T // L
    ntt = T // 128
    nc = bass.Bass("TRN2", target_bir_lowering=False)
    E = Env()
    E.T, E.nck, E.ntt = T, nck, ntt

    E.x_in = nc.dram_tensor("x_in", [T, H], F32, kind="ExternalInput").ap()
    E.winT = nc.dram_tensor("winT", [H, PROJ], F32, kind="ExternalInput").ap()
    E.woutT = nc.dram_tensor("woutT", [INTER, H], F32, kind="ExternalInput").ap()
    E.convw = nc.dram_tensor("convw", [128, 136], F32, kind="ExternalInput").ap()
    E.convb = nc.dram_tensor("convb", [128, 34], F32, kind="ExternalInput").ap()
    E.dtb_in = nc.dram_tensor("dtb", [1, NH], F32, kind="ExternalInput").ap()
    E.alog_in = nc.dram_tensor("alog", [1, NH], F32, kind="ExternalInput").ap()
    E.dvec_in = nc.dram_tensor("dvec", [1, NH], F32, kind="ExternalInput").ap()
    E.nw_in = nc.dram_tensor("nw", [128, 32], F32, kind="ExternalInput").ap()
    E.gup_in = nc.dram_tensor("gup", [GUD, H // 8], I32, kind="ExternalInput").ap()
    E.gus_in = nc.dram_tensor("gus", [GUD, 16], F16, kind="ExternalInput").ap()
    E.dnp_in = nc.dram_tensor("dnp", [H, FI // 8], I32, kind="ExternalInput").ap()
    E.dns_in = nc.dram_tensor("dns", [H, 44], F16, kind="ExternalInput").ap()
    E.out = nc.dram_tensor("out", [T, H], F32, kind="ExternalOutput").ap()

    E.xn_sc = nc.dram_tensor("xn_sc", [NHT, 128, T], BF16).ap()
    E.sg_sc = nc.dram_tensor("sg_sc", [NFT_G, 128, T], BF16).ap()
    E.hs_sc = nc.dram_tensor("hs_sc", [NFT_G, 128, T], BF16).ap()
    E.b_sc = nc.dram_tensor("b_sc", [128, T], BF16).ap()
    E.c_sc = nc.dram_tensor("c_sc", [128, T], BF16).ap()
    E.ynw_sc = nc.dram_tensor("ynw_sc", [nck, 32, 128, 128], BF16).ap()
    E.ffn_sc = nc.dram_tensor("ffn_sc", [ntt, 128, H], BF16).ap()
    E.wgu_sc = nc.dram_tensor("wgu_sc", [NOT_GU, 128, H], BF16).ap()
    E.wdn_sc = nc.dram_tensor("wdn_sc", [4, 128, NIT_DN, 512], BF16).ap()

    st = ExitStack()
    with st:
        e = st.enter_context

        E.iot_r = e(nc.sbuf_tensor("iot_r", [128, 128], F32))
        E.iot_c = e(nc.sbuf_tensor("iot_c", [128, 1], F32))
        E.ident_b = e(nc.sbuf_tensor("ident_b", [128, 128], BF16))
        E.ident_f = e(nc.sbuf_tensor("ident_f", [128, 128], F32))
        E.i64 = e(nc.sbuf_tensor("i64", [64, 64], F32))
        E.triu_f = e(nc.sbuf_tensor("triu_f", [128, 128], F32))
        E.negm_f = e(nc.sbuf_tensor("negm_f", [128, 128], F32))
        E.negm_b = e(nc.sbuf_tensor("negm_b", [128, 128], BF16))
        E.ones_b = e(nc.sbuf_tensor("ones_b", [128, 1], BF16))
        E.ones64 = e(nc.sbuf_tensor("ones64", [64, 128], F32))
        E.c_one = e(nc.sbuf_tensor("c_one", [128, 1], F32))
        E.c_epsln = e(nc.sbuf_tensor("c_epsln", [128, 1], F32))
        E.c_epsat = e(nc.sbuf_tensor("c_epsat", [128, 1], F32))
        E.c_epsff = e(nc.sbuf_tensor("c_epsff", [128, 1], F32))
        E.dtb_rep = e(nc.sbuf_tensor("dtb_rep", [128, NH], F32))
        E.nega_rep = e(nc.sbuf_tensor("nega_rep", [128, NH], F32))
        E.d_rep = e(nc.sbuf_tensor("d_rep", [128, NH], F32))
        E.dfull = e(nc.sbuf_tensor("dfull", [128, 32], BF16))
        E.convw_sb = e(nc.sbuf_tensor("convw_sb", [128, 136], F32))
        E.convb_sb = e(nc.sbuf_tensor("convb_sb", [128, 34], F32))
        E.nw_bf = e(nc.sbuf_tensor("nw_bf", [128, 32], BF16))
        E.dt_all = e(nc.sbuf_tensor("dt_all", [128, NH], F32))
        E.dtbf_all = e(nc.sbuf_tensor("dtbf_all", [128, nck * NH], BF16))
        E.sfm_all = e(nc.sbuf_tensor("sfm_all", [64, nck * 128], F32))
        E.inv_at = e(nc.sbuf_tensor("inv_at", [128, nck], F32))
        E.inv_ff = e(nc.sbuf_tensor("inv_ff", [128, ntt], F32))

        for nm in ("pe", "dv", "ac", "g2", "pe2", "dv2", "ac2", "pe3", "dv3",
                   "qpe", "qac", "qdv"):
            setattr(E, "s_" + nm, Sem(st, nc, "s_" + nm))
        E.dcn = DmaStream(st, nc, "dcn", 2)
        E.dgp = DmaStream(st, nc, "dgp", 2)
        E.dx = DmaStream(st, nc, "dx", 2)
        E.dx2 = DmaStream(st, nc, "dx2", 2)
        E.dst = DmaStream(st, nc, "dst", 3)
        E.dst2 = DmaStream(st, nc, "dst2", 2)
        E.dst3 = DmaStream(st, nc, "dst3", 2)
        E.dxp = DmaStream(st, nc, "dxp", 2)
        E.dxp2 = DmaStream(st, nc, "dxp2", 2)
        E.dxp3 = DmaStream(st, nc, "dxp3", 2)

        _consts(nc, E)
        if 0 in phases:
            _p0_ln(nc, st, E)
        if 1 in phases:
            _p1_ffn(nc, st, E, fb0_only=(3 in phases))
        if 2 in phases:
            _p2_inproj(nc, st, E)
        if 3 in phases:
            _p3_scan(nc, st, E, with_p1=(1 in phases))
        if 4 in phases:
            _p4_out(nc, st, E)

    nc.finalize()
    return nc


def _consts(nc, E):
    with nc.Block() as blk:
        @blk.gpsimd
        def _(g):
            g.iota(E.iot_r[:], [[1, 128]], channel_multiplier=0,
                   allow_small_or_imprecise_dtypes=True)
            g.iota(E.iot_c[:], [[0, 1]], channel_multiplier=1,
                   allow_small_or_imprecise_dtypes=True)
            E.tok_nw = E.dgp.issue(g, E.nw_bf[:], E.nw_in[:])

        @blk.sync
        def _(sy):
            E.tok_cw = E.dcn.issue(sy, E.convw_sb[:], E.convw[:])
            E.tok_cb = E.dcn.issue(sy, E.convb_sb[:], E.convb[:])

        @blk.vector
        def _(v):
            v.memset(E.ones64[:], 1.0)
            v.memset(E.c_one[:], 1.0)
            v.memset(E.c_epsln[:], 1e-5)
            v.memset(E.c_epsat[:], 1e-5)
            v.memset(E.c_epsff[:], 1e-6)
            v.memset(E.ones_b[:], 1.0)

    with nc.Block() as blk:
        @blk.vector
        def _(v):
            v.tensor_scalar(E.ident_b[:], E.iot_r[:], E.iot_c[:], None, ALU.is_equal)
            v.tensor_scalar(E.ident_f[:], E.iot_r[:], E.iot_c[:], None, ALU.is_equal)
            v.tensor_scalar(E.i64[:], E.iot_r[0:64, 0:64], E.iot_c[0:64, :], None,
                            ALU.is_equal)
            v.tensor_scalar(E.triu_f[:], E.iot_r[:], E.iot_c[:], None, ALU.is_ge)
            v.tensor_scalar(E.negm_f[:], E.iot_r[:], E.iot_c[:], -3.0e38,
                            ALU.is_lt, ALU.mult)
            v.tensor_scalar(E.negm_b[:], E.iot_r[:], E.iot_c[:], -3.0e38,
                            ALU.is_lt, ALU.mult)

    with (nc.sbuf_tensor("cn_vrow", [1, 3 * NH], F32) as vrow,
          nc.sbuf_tensor("cn_ones1f", [1, 128], F32) as ones1f,
          nc.psum_tensor("cn_ps", [128, 3 * NH], F32) as ps_rep):
        toks = []
        with nc.Block() as blk:
            @blk.sync
            def _(sy):
                toks.append(E.dcn.issue(sy, vrow[0:1, 0:NH], E.dtb_in[:]))
                toks.append(E.dcn.issue(sy, vrow[0:1, NH:2 * NH], E.alog_in[:]))
                toks.append(E.dcn.issue(sy, vrow[0:1, 2 * NH:3 * NH], E.dvec_in[:]))

            @blk.vector
            def _(v):
                v.memset(ones1f[:], 1.0)

        with nc.Block() as blk:
            @blk.tensor
            def _(t):
                for tk in toks:
                    DmaStream.wait(t, tk)
                t.matmul(ps_rep[:], ones1f[:], vrow[:], start=True, stop=True)

        with nc.Block() as blk:
            @blk.vector
            def _(v):
                v.tensor_copy(E.dtb_rep[:], ps_rep[:, 0:NH])
                v.tensor_copy(E.d_rep[:], ps_rep[:, 2 * NH:3 * NH])

            @blk.scalar
            def _(s):
                s.activation(E.nega_rep[:], ps_rep[:, NH:2 * NH], ACT.Exp,
                             scale=1.0)

        with nc.Block() as blk:
            @blk.vector
            def _(v):
                v.tensor_scalar(E.nega_rep[:], E.nega_rep[:], -1.0, None, ALU.mult)
                v.tensor_copy(E.dfull[0:64, :], E.d_rep[0:64, 0:NH:2])
                v.tensor_copy(E.dfull[64:128, :], E.d_rep[64:128, 1:NH:2])


def _p0_ln(nc, st, E):
    ps_ = ExitStack()
    e = ps_.enter_context
    ntt = E.ntt
    xt = [e(nc.sbuf_tensor(f"p0_x{i}", [128, H], F32)) for i in range(2)]
    sq = [e(nc.sbuf_tensor(f"p0_sq{i}", [128, H], F32)) for i in range(2)]
    mcol = e(nc.sbuf_tensor("p0_m", [128, 2], F32))
    m2col = e(nc.sbuf_tensor("p0_m2", [128, 2], F32))
    vcol = e(nc.sbuf_tensor("p0_v", [128, 2], F32))
    icol = e(nc.sbuf_tensor("p0_i", [128, 2], F32))
    xn_tm = [e(nc.sbuf_tensor(f"p0_xn{i}", [128, H], BF16)) for i in range(2)]
    stg = [e(nc.sbuf_tensor(f"p0_st{i}", [128, H], BF16)) for i in range(2)]
    pst = [e(nc.psum_tensor(f"p0_ps{i}", [128, 2048], BF16)) for i in range(2)]

    sqd = [E.s_ac.bump() for _ in range(ntt)]
    std = [E.s_dv.bump() for _ in range(ntt)]
    inv = [E.s_ac2.bump() for _ in range(ntt)]
    xnd = [E.s_dv2.bump() for _ in range(ntt)]
    tpd = [E.s_pe.bump() for _ in range(ntt)]
    evd = [E.s_g2.bump() for _ in range(ntt)]
    xtok = []
    sttok = []

    with nc.Block() as blk:
        @blk.sync
        def _(sy):
            for i in range(ntt):
                if i >= 2:
                    sy.wait_ge(E.s_dv2.h, xnd[i - 2])
                    sy.wait_ge(E.s_ac.h, sqd[i - 2])
                xtok.append(E.dx.issue(
                    sy, xt[i % 2][:], E.x_in[i * 128:(i + 1) * 128, :]))

        @blk.scalar
        def _(s):
            for i in range(ntt):
                p = i % 2
                DmaStream.wait(s, xtok[i])
                if i >= 2:
                    s.wait_ge(E.s_dv.h, std[i - 2])
                s.activation(sq[p][:], xt[p][:], ACT.Square).then_inc(E.s_ac.h, 1)
                if i >= 1:
                    s.wait_ge(E.s_g2.h, evd[i - 1])
                    sttok.append(E.dst.issue(
                        s,
                        E.xn_sc[:, :, (i - 1) * 128:i * 128].transpose([1, 0, 2]),
                        stg[(i - 1) % 2][:].rearrange("p (a t) -> p a t", a=NHT)))
                s.wait_ge(E.s_dv.h, std[i])
                s.activation(icol[:, p:p + 1], vcol[:, p:p + 1], ACT.Ln,
                             bias=E.c_epsln[:])
                s.drain()
                s.activation(icol[:, p:p + 1], icol[:, p:p + 1], ACT.Exp,
                             scale=-0.5).then_inc(E.s_ac2.h, 1)
            s.wait_ge(E.s_g2.h, evd[ntt - 1])
            sttok.append(E.dst.issue(
                s,
                E.xn_sc[:, :, (ntt - 1) * 128:ntt * 128].transpose([1, 0, 2]),
                stg[(ntt - 1) % 2][:].rearrange("p (a t) -> p a t", a=NHT)))

        @blk.vector
        def _(v):
            for i in range(ntt):
                p = i % 2
                if i >= 1:
                    v.wait_ge(E.s_pe.h, tpd[i - 1])
                    if i >= 3:
                        DmaStream.wait(v, sttok[i - 3])
                    v.tensor_copy(stg[(i - 1) % 2][:], pst[(i - 1) % 2][:]) \
                        .then_inc(E.s_g2.h, 1)
                DmaStream.wait(v, xtok[i])
                v.wait_ge(E.s_ac.h, sqd[i])
                if i >= 2:
                    v.wait_ge(E.s_ac2.h, inv[i - 2])
                    v.drain()
                v.reduce_sum(mcol[:, p:p + 1], xt[p][:], axis=AX.X)
                v.reduce_sum(m2col[:, p:p + 1], sq[p][:], axis=AX.X)
                v.drain()
                v.tensor_scalar(mcol[:, p:p + 1], mcol[:, p:p + 1], -1.0 / H,
                                None, ALU.mult)
                v.tensor_scalar(m2col[:, p:p + 1], m2col[:, p:p + 1], 1.0 / H,
                                None, ALU.mult)
                v.drain()
                v.tensor_scalar(vcol[:, p:p + 1], mcol[:, p:p + 1],
                                mcol[:, p:p + 1], -1.0, ALU.mult, ALU.mult)
                v.drain()
                v.tensor_tensor(vcol[:, p:p + 1], vcol[:, p:p + 1],
                                m2col[:, p:p + 1], ALU.add).then_inc(E.s_dv.h, 1)
                v.wait_ge(E.s_ac2.h, inv[i])
                if i >= 2:
                    v.wait_ge(E.s_pe.h, tpd[i - 2])
                v.tensor_scalar(xn_tm[p][:], xt[p][:], mcol[:, p:p + 1],
                                icol[:, p:p + 1], ALU.add, ALU.mult) \
                    .then_inc(E.s_dv2.h, 1)
            v.wait_ge(E.s_pe.h, tpd[ntt - 1])
            if ntt >= 3:
                DmaStream.wait(v, sttok[ntt - 3])
            v.tensor_copy(stg[(ntt - 1) % 2][:], pst[(ntt - 1) % 2][:]) \
                .then_inc(E.s_g2.h, 1)

        @blk.tensor
        def _(t):
            for i in range(ntt):
                p = i % 2
                t.wait_ge(E.s_dv2.h, xnd[i])
                if i >= 2:
                    t.wait_ge(E.s_g2.h, evd[i - 2])
                for ht in range(NHT):
                    ins = t.transpose(pst[p][:, ht * 128:(ht + 1) * 128],
                                      xn_tm[p][:, ht * 128:(ht + 1) * 128],
                                      E.ident_b[:])
                ins.then_inc(E.s_pe.h, 1)

    _quiesce_stores(nc, E)
    ps_.close()


def _p1_ffn(nc, st, E, fb0_only=False):
    ps_ = ExitStack()
    e = ps_.enter_context
    T = E.T
    TBF = min(512, T)
    nfb = T // TBF
    E.p1_TBF = TBF
    E.p1_nfb = nfb
    ntt_fb = TBF // 128

    h_fm = e(nc.sbuf_tensor("p1_h", [128, NIT_DN * TBF], BF16))
    ssq = e(nc.sbuf_tensor("p1_ssq", [128, ntt_fb], F32))
    ps_g = e(nc.psum_tensor("p1_psg", [128, TBF], F32))
    ps_u = e(nc.psum_tensor("p1_psu", [128, TBF], F32))
    psT = e(nc.psum_tensor("p1_psT", [128, 2048], BF16))
    psD = e(nc.psum_tensor("p1_psD", [128, 1024], F32))

    fbs = range(1) if (fb0_only and nfb > 1) else range(nfb)
    for fb in fbs:
        deq = (fb == 0)
        _p1_gate_up(nc, E, fb, TBF, deq, h_fm, ps_g, ps_u, psT)
        _p1_down(nc, E, fb, TBF, deq, h_fm, psD, psT, ssq)

    _quiesce_stores(nc, E)
    ps_.close()


def _p1_gate_up(nc, E, fb, TBF, deq, h_fm, ps_g, ps_u, psT):
    T = E.T
    with ExitStack() as ps_:
        e = ps_.enter_context
        xnf = e(nc.sbuf_tensor(f"p1_xn{fb}", [128, NHT * TBF], BF16))
        wt = [[e(nc.sbuf_tensor(f"p1_w{fb}_{a}{i}", [128, 2048], BF16))
               for i in range(2)] for a in range(2)]
        sgst = [e(nc.sbuf_tensor(f"p1_sg{fb}_{i}", [128, TBF], BF16)) for i in range(2)]
        if deq:
            pkb = [e(nc.sbuf_tensor(f"p1_pk{fb}_{i}", [128, 256], I32)) for i in range(2)]
            scb = [e(nc.sbuf_tensor(f"p1_sc{fb}_{i}", [128, 16], F16)) for i in range(2)]
            scf = [e(nc.sbuf_tensor(f"p1_scf{fb}_{i}", [128, 16], BF16))
                   for i in range(2)]
            nib = e(nc.sbuf_tensor(f"p1_nib{fb}", [128, 2048], U16))
            tq1 = e(nc.sbuf_tensor(f"p1_tq1{fb}", [128, 2048], U16))
            tq2 = [e(nc.sbuf_tensor(f"p1_tq2{fb}_{i}", [128, 2048], U16))
                   for i in range(2)]
            tq3 = e(nc.sbuf_tensor(f"p1_tq3{fb}", [128, 2048], U16))
            wbf = e(nc.sbuf_tensor(f"p1_wbf{fb}", [128, 2048], BF16))

        SPE, SDV, SAC, SGP = E.s_pe, E.s_dv, E.s_ac, E.s_g2
        ntile = 2 * NPAIR_GU
        ots = []
        for j in range(NPAIR_GU):
            ots.append(j)
            ots.append(j + NPAIR_GU)

        def wslot(i):
            return wt[i % 2][(i // 2) % 2]

        sch = [{} for _ in range(ntile)]
        for i in range(ntile):
            d = sch[i]
            if deq:
                d["gp_nib"] = SDV.bump(8)
                d["dv_pre"] = SDV.bump(1)
                d["g_deq"] = SGP.bump(1)
                if i >= 2 and (i - 2) % 2 == 1:
                    sch[i - 2]["dv_h"] = SDV.bump()
                d["t_tp"] = SPE.bump()
                d["a_ev"] = SAC.bump()
            d["t_mm"] = SPE.bump()
            if i % 2 == 1:
                d["a_silu"] = SAC.bump()
                if not deq:
                    d["dv_h"] = SDV.bump()
        if deq:
            sch[ntile - 1]["dv_h"] = SDV.bump()

        xtok = [None]
        ptok, sctok, wtok, wstok = [], [], [], []

        with nc.Block() as blk:
            @blk.sync
            def _(sy):
                xtok[0] = E.dx.issue(
                    sy, xnf[:].rearrange("p (a t) -> p a t", a=NHT),
                    E.xn_sc[:, :, fb * TBF:(fb + 1) * TBF].transpose([1, 0, 2]))
                for i, ot in enumerate(ots):
                    if deq:
                        if i >= 2:
                            sy.wait_ge(SDV.h, sch[i - 2]["gp_nib"])
                            sy.wait_ge(SDV.h, sch[i - 2]["dv_pre"])
                        ptok.append(E.dx2.issue(sy, pkb[i % 2][:],
                                                E.gup_in[ot * 128:(ot + 1) * 128, :]))
                        sctok.append(E.dcn.issue(sy, scb[i % 2][:],
                                                 E.gus_in[ot * 128:(ot + 1) * 128, :]))
                    else:
                        if fb == 1 and i == 0:
                            for tk in E.p1_gu_wstok[-3:]:
                                DmaStream.wait(sy, tk)
                        if i >= 4:
                            sy.wait_ge(SPE.h, sch[i - 4]["t_mm"])
                        wtok.append(E.dx2.issue(sy, wslot(i)[:], E.wgu_sc[ot]))

            @blk.gpsimd
            def _(g):
                if not deq:
                    return
                for i, ot in enumerate(ots):
                    g.wait_ge(SDV.h, sch[i]["dv_pre"])
                    if i >= 1:
                        g.wait_ge(SPE.h, sch[i - 1]["t_tp"])
                    scv = scf[i % 2][:].unsqueeze(2) \
                        .to_broadcast([128, 16, 128])
                    g.tensor_tensor(
                        wbf[:].rearrange("p (a i) -> p a i", a=16),
                        tq2[i % 2][:].bitcast(BF16).rearrange(
                            "p (a i) -> p a i", a=16),
                        scv, ALU.mult).then_inc(SGP.h, 1)
                    g.drain()

            @blk.vector
            def _(v):
                for i, ot in enumerate(ots):
                    if deq:
                        DmaStream.wait(v, ptok[i])
                        pku = pkb[i % 2][:].bitcast(U16)
                        for k in range(8):
                            v.tensor_scalar(nib[:, k::8],
                                            pku[:, (1 if k >= 4 else 0)::2],
                                            4 * (k % 4), 15,
                                            ALU.logical_shift_right,
                                            ALU.bitwise_and).then_inc(SDV.h, 1)
                        v.drain()
                        if i >= 2:
                            v.wait_ge(SGP.h, sch[i - 2]["g_deq"])
                        v.tensor_scalar(tq3[:], nib[:], 12, 0x8000,
                                        ALU.logical_shift_left, ALU.bitwise_and)
                        v.tensor_scalar(tq1[:], nib[:], 7, None,
                                        ALU.bitwise_and)
                        v.drain()
                        v.tensor_scalar(tq2[i % 2][:], tq1[:], 252, None,
                                        ALU.add)
                        v.drain()
                        v.scalar_tensor_tensor(tq1[:], tq1[:], 252.0,
                                               tq2[i % 2][:],
                                               ALU.mult, ALU.min)
                        v.drain()
                        v.scalar_tensor_tensor(tq2[i % 2][:], tq1[:], 64.0,
                                               tq3[:], ALU.mult, ALU.add)
                        v.drain()
                        DmaStream.wait(v, sctok[i])
                        v.tensor_copy(scf[i % 2][:], scb[i % 2][:]) \
                            .then_inc(SDV.h, 1)
                    ih = i - 2 if deq else i
                    if ih >= 0 and ih % 2 == 1:
                        j = ih // 2
                        v.wait_ge(SAC.h, sch[ih]["a_silu"])
                        v.wait_ge(SPE.h, sch[ih]["t_mm"])
                        v.tensor_tensor(h_fm[:, j * TBF:(j + 1) * TBF],
                                        sgst[j % 2][:], ps_u[:], ALU.mult) \
                            .then_inc(SDV.h, 1)
                    if deq and i == len(ots) - 1:
                        j = i // 2
                        v.wait_ge(SAC.h, sch[i]["a_silu"])
                        v.wait_ge(SPE.h, sch[i]["t_mm"])
                        v.tensor_tensor(h_fm[:, j * TBF:(j + 1) * TBF],
                                        sgst[j % 2][:], ps_u[:], ALU.mult) \
                            .then_inc(SDV.h, 1)

            @blk.scalar
            def _(s):
                for i, ot in enumerate(ots):
                    if deq:
                        s.wait_ge(SPE.h, sch[i]["t_tp"])
                        if i >= 4:
                            s.wait_ge(SPE.h, sch[i - 4]["t_mm"])
                        s.activation(wslot(i)[:], psT[:], ACT.Copy) \
                            .then_inc(SAC.h, 1)
                        s.drain()
                        wstok.append(E.dst.issue(s, E.wgu_sc[ot], wslot(i)[:]))
                    if i % 2 == 1:
                        j = i // 2
                        s.wait_ge(SPE.h, sch[i - 1]["t_mm"])
                        if j >= 2:
                            s.wait_ge(SDV.h, sch[2 * (j - 2) + 1]["dv_h"])
                        s.activation(sgst[j % 2][:], ps_g[:], ACT.Silu) \
                            .then_inc(SAC.h, 1)

            @blk.tensor
            def _(t):
                DmaStream.wait(t, xtok[0])
                for i, ot in enumerate(ots):
                    if deq:
                        t.wait_ge(SGP.h, sch[i]["g_deq"])
                        if i >= 1:
                            t.wait_ge(SAC.h, sch[i - 1]["a_ev"])
                        for ht in range(NHT):
                            ins = t.transpose(psT[:, ht * 128:(ht + 1) * 128],
                                              wbf[:, ht * 128:(ht + 1) * 128],
                                              E.ident_b[:])
                        ins.then_inc(SPE.h, 1)
                        t.wait_ge(SAC.h, sch[i]["a_ev"])
                    else:
                        DmaStream.wait(t, wtok[i])
                    if i % 2 == 0 and i >= 2:
                        t.wait_ge(SAC.h, sch[i - 1]["a_silu"])
                    if i % 2 == 1 and i >= 3:
                        t.wait_ge(SDV.h, sch[i - 2]["dv_h"])
                    psu = ps_g if (i % 2 == 0) else ps_u
                    for ht in range(NHT):
                        ins = t.matmul(psu[:],
                                       wslot(i)[:, ht * 128:(ht + 1) * 128],
                                       xnf[:, ht * TBF:(ht + 1) * TBF],
                                       start=(ht == 0), stop=(ht == NHT - 1))
                    ins.then_inc(SPE.h, 1)

        if deq:
            E.p1_gu_wstok = wstok
        _quiesce_stores(nc, E)


def _p1_down(nc, E, fb, TBF, deq, h_fm, psD, psT, ssq):
    T = E.T
    ntt_fb = TBF // 128
    with ExitStack() as ps_:
        e = ps_.enter_context
        wdn = e(nc.sbuf_tensor(f"p1_wdn{fb}", [128, NIT_DN * 512], BF16))
        sqb = e(nc.sbuf_tensor(f"p1_sqb{fb}", [128, 512], F32))
        sscol = e(nc.sbuf_tensor(f"p1_sscol{fb}", [128, 1], F32))
        fst = [e(nc.sbuf_tensor(f"p1_fs{fb}_{i}", [128, 512], BF16)) for i in range(2)]
        if deq:
            pkb = [e(nc.sbuf_tensor(f"p1_dpk{fb}_{i}", [128, 704], I32)) for i in range(2)]
            scb = [e(nc.sbuf_tensor(f"p1_dsc{fb}_{i}", [128, 44], F16)) for i in range(2)]
            scf = e(nc.sbuf_tensor(f"p1_dscf{fb}", [128, 44], F32))
            nib = e(nc.sbuf_tensor(f"p1_dnib{fb}", [128, 5632], U16))
            tq1 = e(nc.sbuf_tensor(f"p1_dtq1{fb}", [128, 5632], U16))
            tq2 = e(nc.sbuf_tensor(f"p1_dtq2{fb}", [128, 5632], U16))
            tq3 = e(nc.sbuf_tensor(f"p1_dtq3{fb}", [128, 5632], U16))
            wbf = e(nc.sbuf_tensor(f"p1_dwbf{fb}", [128, 5632], BF16))

        SPE, SDV, SAC, SGP = E.s_pe, E.s_dv, E.s_ac, E.s_g2
        if deq:
            E.p1_dn_wstok = {}
        ftoks_all = E.dst2.plan(4 * ntt_fb)

        for oq in range(4):
            sch_d = []
            if deq:
                for i in range(4):
                    d = {}
                    d["gp_nib"] = SDV.bump(8)
                    d["dv_deq"] = SDV.bump(6)
                    d["t_tp"] = SPE.bump(3)
                    d["a_ev"] = SAC.bump(3)
                    sch_d.append(d)
            mm_d = [E.s_pe.bump() for _ in range(ntt_fb)]
            sq_d = [E.s_ac.bump() for _ in range(ntt_fb)]
            ss_d, ev_d = [], []
            for _tt in range(ntt_fb):
                ss_d.append(E.s_dv.bump(2))
                ev_d.append(E.s_dv.bump())
            if oq == 3:
                inv_d = E.s_ac.bump()
            ptok, sctok, wltok, wstok2, ftok = [], [], [], [], []

            with nc.Block() as blk:
                @blk.sync
                def _(sy):
                    if deq:
                        for i in range(4):
                            if i >= 2:
                                sy.wait_ge(SDV.h, sch_d[i - 2]["gp_nib"])
                                sy.wait_ge(SDV.h, sch_d[i - 2]["dv_deq"] - 5)
                            ot = oq * 4 + i
                            ptok.append(E.dx2.issue(
                                sy, pkb[i % 2][:],
                                E.dnp_in[ot * 128:(ot + 1) * 128, :]))
                            sctok.append(E.dcn.issue(
                                sy, scb[i % 2][:],
                                E.dns_in[ot * 128:(ot + 1) * 128, :]))
                    else:
                        if fb == 1:
                            for _tk in E.p1_dn_wstok[oq]:
                                DmaStream.wait(sy, _tk)
                        wltok.append(E.dx2.issue(
                            sy, wdn[:].rearrange("p (a i) -> p a i", a=NIT_DN),
                            E.wdn_sc[oq]))

                @blk.gpsimd
                def _(g):
                    for tt in range(ntt_fb):
                        g.wait_ge(SDV.h, ev_d[tt])
                        ttg = fb * ntt_fb + tt
                        tk = E.dst2.issue(
                            g, E.ffn_sc[ttg, :, oq * 512:(oq + 1) * 512],
                            fst[tt % 2][:])
                        assert tk == ftoks_all[oq * ntt_fb + tt]
                        ftok.append(tk)

                @blk.vector
                def _(v):
                    if deq:
                        for i in range(4):
                            DmaStream.wait(v, ptok[i])
                            if i >= 1:
                                v.wait_ge(SDV.h, sch_d[i - 1]["dv_deq"])
                            pku = pkb[i % 2][:].bitcast(U16)
                            for k in range(8):
                                v.tensor_scalar(nib[:, k::8],
                                                pku[:, (1 if k >= 4 else 0)::2],
                                                4 * (k % 4), 15,
                                                ALU.logical_shift_right,
                                                ALU.bitwise_and) \
                                    .then_inc(SDV.h, 1)
                            v.drain()
                            DmaStream.wait(v, sctok[i])
                            if i >= 1:
                                v.wait_ge(SPE.h, sch_d[i - 1]["t_tp"])
                            v.tensor_copy(scf[:], scb[i % 2][:]).then_inc(SDV.h, 1)
                            v.tensor_scalar(tq3[:], nib[:], 12, 0x8000,
                                            ALU.logical_shift_left,
                                            ALU.bitwise_and)
                            v.tensor_scalar(tq1[:], nib[:], 7, None,
                                            ALU.bitwise_and)
                            v.drain()
                            v.tensor_scalar(tq2[:], tq1[:], 252, None, ALU.add)
                            v.drain()
                            v.tensor_scalar(tq1[:], tq1[:], 252, None, ALU.mult)
                            v.drain()
                            v.tensor_tensor(tq1[:], tq1[:], tq2[:], ALU.min)
                            v.drain()
                            v.scalar_tensor_tensor(tq2[:], tq1[:], 64.0, tq3[:],
                                                   ALU.mult, ALU.add)
                            v.drain()
                            scv = scf[:].unsqueeze(2).to_broadcast([128, 44, 128])
                            v.tensor_tensor(
                                wbf[:].rearrange("p (a i) -> p a i", a=44),
                                tq2[:].bitcast(BF16).rearrange(
                                    "p (a i) -> p a i", a=44),
                                scv, ALU.mult).then_inc(SDV.h, 5)
                    for tt in range(ntt_fb):
                        v.wait_ge(SAC.h, sq_d[tt])
                        v.reduce_sum(sscol[:], sqb[:], axis=AX.X) \
                            .then_inc(SDV.h, 1)
                        v.drain()
                        if oq == 0:
                            v.tensor_copy(ssq[:, tt:tt + 1], sscol[:]) \
                                .then_inc(SDV.h, 1)
                        else:
                            v.tensor_tensor(ssq[:, tt:tt + 1], ssq[:, tt:tt + 1],
                                            sscol[:], ALU.add).then_inc(SDV.h, 1)
                        v.drain()
                        v.wait_ge(SPE.h, mm_d[tt])
                        gidx = oq * ntt_fb + tt
                        if gidx >= 2:
                            DmaStream.wait(v, ftoks_all[gidx - 2])
                        v.tensor_copy(fst[tt % 2][:],
                                      psD[:, (tt % 2) * 512:(tt % 2) * 512 + 512]) \
                            .then_inc(SDV.h, 1)

                @blk.scalar
                def _(s):
                    if deq:
                        wstore = []
                        for i in range(4):
                            base = sch_d[i]["t_tp"] - 3
                            for w in range(3):
                                nt = 16 if w < 2 else 12
                                if oq >= 1 and i == 0:
                                    DmaStream.wait(
                                        s, E.p1_dn_wstok[oq - 1][w])
                                s.wait_ge(SPE.h, base + w + 1)
                                s.activation(
                                    wdn[:].rearrange(
                                        "p (a i) -> p a i", a=NIT_DN)
                                    [:, w * 16:w * 16 + nt, i * 128:(i + 1) * 128],
                                    psT[:].rearrange("p (a i) -> p a i", a=16)
                                    [:, 0:nt, :],
                                    ACT.Copy).then_inc(SAC.h, 1)
                                if i == 3:
                                    s.drain()
                                    wstore.append(E.dst.issue(
                                        s,
                                        E.wdn_sc[oq][:, w * 16:w * 16 + nt],
                                        wdn[:].rearrange(
                                            "p (a i) -> p a i", a=NIT_DN)
                                        [:, w * 16:w * 16 + nt, :]))
                        E.p1_dn_wstok[oq] = wstore
                    for tt in range(ntt_fb):
                        s.wait_ge(SPE.h, mm_d[tt])
                        if tt >= 1:
                            s.wait_ge(SDV.h, ss_d[tt - 1])
                        s.activation(sqb[:],
                                     psD[:, (tt % 2) * 512:(tt % 2) * 512 + 512],
                                     ACT.Square).then_inc(SAC.h, 1)
                    if oq == 3:
                        for tt in range(ntt_fb):
                            s.wait_ge(SDV.h, ss_d[tt])
                        ttg0 = fb * ntt_fb
                        s.activation(E.inv_ff[:, ttg0:ttg0 + ntt_fb],
                                     ssq[:, 0:ntt_fb], ACT.Ln,
                                     bias=E.c_epsff[:], scale=1.0 / H)
                        s.drain()
                        s.activation(E.inv_ff[:, ttg0:ttg0 + ntt_fb],
                                     E.inv_ff[:, ttg0:ttg0 + ntt_fb],
                                     ACT.Exp, scale=-0.5).then_inc(SAC.h, 1)

                @blk.tensor
                def _(t):
                    if deq:
                        for i in range(4):
                            t.wait_ge(SDV.h, sch_d[i]["dv_deq"])
                            base_pe = sch_d[i]["t_tp"] - 3
                            base_ac = sch_d[i]["a_ev"] - 3
                            for w in range(3):
                                nt = 16 if w < 2 else 12
                                if (i, w) != (0, 0):
                                    t.wait_ge(SAC.h, base_ac + w)
                                for k in range(nt):
                                    it = w * 16 + k
                                    ins = t.transpose(
                                        psT[:, k * 128:(k + 1) * 128],
                                        wbf[:, it * 128:(it + 1) * 128],
                                        E.ident_b[:])
                                ins.then_inc(SPE.h, 1)
                        t.wait_ge(SAC.h, sch_d[3]["a_ev"])
                    else:
                        DmaStream.wait(t, wltok[0])
                    for tt in range(ntt_fb):
                        if tt >= 2:
                            t.wait_ge(SDV.h, ev_d[tt - 2])
                            t.wait_ge(SAC.h, sq_d[tt - 2])
                        for it in range(NIT_DN):
                            ins = t.matmul(
                                psD[:, (tt % 2) * 512:(tt % 2) * 512 + 512],
                                h_fm[:, it * TBF + tt * 128:
                                     it * TBF + (tt + 1) * 128],
                                wdn[:, it * 512:(it + 1) * 512],
                                start=(it == 0), stop=(it == NIT_DN - 1))
                        ins.then_inc(SPE.h, 1)


            E.p1_last_toks = ftok
        _quiesce_stores(nc, E)


def _p2_inproj(nc, st, E):
    ps_ = ExitStack()
    e = ps_.enter_context
    T, nck = E.T, E.nck
    TB = min(512, T)
    ntb = T // TB

    xn_sb = e(nc.sbuf_tensor("p2_xn", [128, NHT * T], BF16))
    wdt_sb = e(nc.sbuf_tensor("p2_wdt", [128, NHT * 64], BF16))
    loop_st = ExitStack()
    e2 = loop_st.enter_context
    wtb = [e2(nc.sbuf_tensor(f"p2_wt{i}", [128, NHT * 128], BF16)) for i in range(2)]
    gst = [e2(nc.sbuf_tensor(f"p2_gs{i}", [128, T], BF16)) for i in range(2)]
    cvs = [e2(nc.sbuf_tensor(f"p2_cv{i}", [128, T + 3], BF16)) for i in range(2)]
    hst = [e2(nc.sbuf_tensor(f"p2_hs{i}", [128, T], BF16)) for i in range(2)]
    diag = [e2(nc.sbuf_tensor(f"p2_dg{i}", [128, 4 * 128], BF16)) for i in range(2)]
    ps_ip = e2(nc.psum_tensor("p2_psip", [128, T if T <= 2048 else 2048], F32))
    ps_cv = e2(nc.psum_tensor("p2_pscv", [128, T if T <= 2048 else 2048], F32))

    NFT = 66

    mmd = [E.s_pe.bump() for _ in range(NFT)]
    evg = {}
    for ft in range(NFT):
        if ft < 32:
            evg[ft] = ("ac", E.s_ac.bump())
        else:
            evg[ft] = ("dv", E.s_dv.bump())
    cvd = {ft: E.s_pe2.bump() for ft in range(32, NFT)}
    sld = {ft: E.s_ac2.bump() for ft in range(32, NFT)}
    dgd = {ft: E.s_dv2.bump(4) for ft in range(32, NFT)}
    wtok = []
    stok = []

    with nc.Block() as blk:
        @blk.sync
        def _(sy):
            E.p2_xntok = E.dx.issue(
                sy, xn_sb[:].rearrange("p (a t) -> p a t", a=NHT), E.xn_sc[:, :, :].transpose([1, 0, 2]))

        @blk.gpsimd
        def _(g):
            E.p2_wdtok = E.dgp.issue(
                g, wdt_sb[:].rearrange("p (a f) -> p a f", a=NHT),
                E.winT[:, PROJ - 64:PROJ].rearrange("(a p) f -> p a f", p=128))
            for ft in range(NFT):
                if ft >= 2:
                    g.wait_ge(E.s_pe.h, mmd[ft - 2])
                wtok.append(E.dgp.issue(
                    g, wtb[ft % 2][:].rearrange("p (a f) -> p a f", a=NHT),
                    E.winT[:, ft * 128:(ft + 1) * 128].rearrange(
                        "(a p) f -> p a f", p=128)))

        @blk.vector
        def _(v):
            v.memset(cvs[0][:, 0:3], 0.0)
            v.memset(cvs[1][:, 0:3], 0.0)
            for ft in range(32, NFT):
                p = ft % 2
                if ft >= 34:
                    v.wait_ge(E.s_pe2.h, cvd[ft - 2])
                for k in range(4):
                    v.tensor_scalar(diag[p][:, k * 128:(k + 1) * 128],
                                    E.ident_b[:],
                                    E.convw_sb[:, (ft - 32) * 4 + k:(ft - 32) * 4 + k + 1],
                                    None, ALU.mult).then_inc(E.s_dv2.h, 1)
                v.wait_ge(E.s_pe.h, mmd[ft])
                if ft >= 34:
                    v.wait_ge(E.s_ac2.h, sld[ft - 2])
                v.tensor_copy(cvs[p][:, 3:T + 3], ps_ip[:]).then_inc(E.s_dv.h, 1)

        @blk.tensor
        def _(t):
            DmaStream.wait(t, E.p2_xntok)
            for ft in range(NFT):
                p = ft % 2
                DmaStream.wait(t, wtok[ft])
                if ft >= 1:
                    kind, val = evg[ft - 1]
                    t.wait_ge(getattr(E, "s_" + kind).h, val)
                for tb in range(ntb):
                    for ht in range(NHT):
                        ins = t.matmul(
                            ps_ip[:, tb * TB:(tb + 1) * TB],
                            wtb[p][:, ht * 128:(ht + 1) * 128],
                            xn_sb[:, ht * T + tb * TB: ht * T + (tb + 1) * TB],
                            start=(ht == 0), stop=(ht == NHT - 1))
                ins.then_inc(E.s_pe.h, 1)
                if ft >= 32:
                    t.wait_ge(E.s_dv.h, evg[ft][1])
                    t.wait_ge(E.s_dv2.h, dgd[ft])
                    if ft >= 33:
                        t.wait_ge(E.s_ac2.h, sld[ft - 1])
                    for tb in range(ntb):
                        for k in range(4):
                            ins = t.matmul(
                                ps_cv[:, tb * TB:(tb + 1) * TB],
                                diag[p][:, k * 128:(k + 1) * 128],
                                cvs[p][:, tb * TB + k: tb * TB + k + TB],
                                start=(k == 0), stop=(k == 3))
                    ins.then_inc(E.s_pe2.h, 1)

        @blk.scalar
        def _(s):
            for ft in range(NFT):
                p = ft % 2
                if ft < 32:
                    s.wait_ge(E.s_pe.h, mmd[ft])
                    if ft >= 2:
                        DmaStream.wait(s, stok[ft - 2])
                    s.activation(gst[p][:], ps_ip[:], ACT.Silu) \
                        .then_inc(E.s_ac.h, 1)
                    s.drain()
                    stok.append(E.dst.issue(s, E.sg_sc[ft, :, :], gst[p][:]))
                else:
                    s.wait_ge(E.s_pe2.h, cvd[ft])
                    if ft >= 34:
                        DmaStream.wait(s, stok[ft - 2])
                    s.activation(hst[p][:], ps_cv[:], ACT.Silu,
                                 bias=E.convb_sb[:, ft - 32:ft - 31]) \
                        .then_inc(E.s_ac2.h, 1)
                    s.drain()
                    if ft < 64:
                        dst_ap = E.hs_sc[ft - 32, :, :]
                    elif ft == 64:
                        dst_ap = E.b_sc[:, :]
                    else:
                        dst_ap = E.c_sc[:, :]
                    stok.append(E.dst.issue(s, dst_ap, hst[p][:]))

    _quiesce_stores(nc, E)
    loop_st.close()
    dtw = e(nc.sbuf_tensor("p2_dtw", [128, 64], F32))
    ab3 = e(nc.sbuf_tensor("p2_ab", [128, 64], F32))
    rl = e(nc.sbuf_tensor("p2_rl", [128, 64], F32))
    atw = e(nc.sbuf_tensor("p2_at", [128, 64], F32))
    stw = e(nc.sbuf_tensor("p2_stw", [128, 64], F32))
    ps_dt = e(nc.psum_tensor("p2_psdt", [128, 64], F32))
    ps_sf = e(nc.psum_tensor("p2_pssf", [64, 128], F32))

    with nc.Block() as blk:
        dt_mm = [E.s_pe.bump() for _ in range(nck)]
        dt_z = [E.s_dv.bump() for _ in range(nck)]
        dt_sp = [E.s_ac.bump() for _ in range(nck)]
        dt_fin = [E.s_dv2.bump() for _ in range(nck)]
        s_mm = [E.s_pe2.bump() for _ in range(nck)]
        s_ev = [E.s_dv3.bump() for _ in range(nck)]
        sf_tp = [E.s_pe3.bump() for _ in range(nck)]
        sf_ev = [E.s_g2.bump() for _ in range(nck)]

        @blk.tensor
        def _(t):
            DmaStream.wait(t, E.p2_wdtok)
            for ck in range(nck):
                if ck >= 1:
                    t.wait_ge(E.s_dv3.h, s_ev[ck - 1])
                for ht in range(NHT):
                    ins = t.matmul(ps_dt[:],
                                   xn_sb[:, ht * T + ck * 128: ht * T + (ck + 1) * 128],
                                   wdt_sb[:, ht * 64:(ht + 1) * 64],
                                   start=(ht == 0), stop=(ht == NHT - 1))
                ins.then_inc(E.s_pe.h, 1)
                t.wait_ge(E.s_dv2.h, dt_fin[ck])
                ins = t.matmul(ps_dt[:], E.triu_f[:], atw[:], start=True,
                               stop=True)
                ins.then_inc(E.s_pe2.h, 1)
                t.wait_ge(E.s_dv3.h, s_ev[ck])
                if ck >= 1:
                    t.wait_ge(E.s_g2.h, sf_ev[ck - 1])
                t.transpose(ps_sf[:], stw[:], E.ident_f[:]).then_inc(E.s_pe3.h, 1)

        @blk.vector
        def _(v):
            for ck in range(nck):
                v.wait_ge(E.s_pe.h, dt_mm[ck])
                if ck >= 1:
                    v.wait_ge(E.s_ac.h, dt_sp[ck - 1])
                v.tensor_tensor(dtw[:], ps_dt[:], E.dtb_rep[:], ALU.add) \
                    .then_inc(E.s_dv.h, 1)
                v.wait_ge(E.s_ac.h, dt_sp[ck])
                v.drain()
                v.tensor_relu(rl[:], dtw[:])
                v.drain()
                v.tensor_tensor(E.dt_all[:], rl[:], ab3[:],
                                ALU.add)
                v.drain()
                v.tensor_copy(E.dtbf_all[:, ck * 64:(ck + 1) * 64],
                              E.dt_all[:])
                v.tensor_tensor(atw[:], E.dt_all[:],
                                E.nega_rep[:], ALU.mult).then_inc(E.s_dv2.h, 1)
                v.wait_ge(E.s_pe2.h, s_mm[ck])
                v.tensor_copy(stw[:], ps_dt[:]).then_inc(E.s_dv3.h, 1)
                v.wait_ge(E.s_pe3.h, sf_tp[ck])
                v.tensor_copy(E.sfm_all[:, ck * 128:(ck + 1) * 128], ps_sf[:]) \
                    .then_inc(E.s_g2.h, 1)

        @blk.scalar
        def _(s):
            for ck in range(nck):
                s.wait_ge(E.s_dv.h, dt_z[ck])
                if ck >= 1:
                    s.wait_ge(E.s_dv2.h, dt_fin[ck - 1])
                s.activation(ab3[:], dtw[:], ACT.Abs)
                s.drain()
                s.activation(ab3[:], ab3[:], ACT.Exp, scale=-1.0)
                s.drain()
                s.activation(ab3[:], ab3[:], ACT.Ln, bias=E.c_one[:]) \
                    .then_inc(E.s_ac.h, 1)

    _quiesce_stores(nc, E)
    ps_.close()


def _p3_scan(nc, st, E, with_p1=False):
    ps_ = ExitStack()
    e = ps_.enter_context
    T, nck = E.T, E.nck
    NB = 8

    hsb = [e(nc.sbuf_tensor(f"p3_hs{i}", [128, 4096], BF16)) for i in range(2)]
    sgb = [e(nc.sbuf_tensor(f"p3_sg{i}", [128, 4096], BF16)) for i in range(2)]
    bcb = [e(nc.sbuf_tensor(f"p3_bc{i}", [128, 256], BF16)) for i in range(2)]
    hs_tm = e(nc.sbuf_tensor("p3_hstm", [128, 4096], BF16))
    b_tm = e(nc.sbuf_tensor("p3_btm", [128, 128], BF16))
    cbtm = e(nc.sbuf_tensor("p3_cbtm", [128, 128], BF16))
    sneg = e(nc.sbuf_tensor("p3_sneg", [64, 128], F32))
    esr = e(nc.sbuf_tensor("p3_esr", [128, 1024], BF16))
    esl = e(nc.sbuf_tensor("p3_esl", [128, NH], F32))
    dexp = e(nc.sbuf_tensor("p3_dexp", [128, 1024], BF16))
    cbdt = e(nc.sbuf_tensor("p3_cbdt", [128, 1024], BF16))
    scq = [e(nc.sbuf_tensor(f"p3_scq{i}", [128, 1024], BF16)) for i in range(2)]
    cexq = [e(nc.sbuf_tensor(f"p3_cexq{i}", [128, 1024], BF16)) for i in range(2)]
    bdq = [e(nc.sbuf_tensor(f"p3_bdq{i}", [128, 1024], BF16)) for i in range(2)]
    wex = e(nc.sbuf_tensor("p3_wex", [128, 8], BF16))
    ssel = [e(nc.sbuf_tensor(f"p3_ssel{i}", [64, 1024], F32)) for i in range(2)]
    wcol = e(nc.sbuf_tensor("p3_wcol", [128, 8], BF16))
    s_all = e(nc.sbuf_tensor("p3_sall", [128, 4096], F32))
    s_bfc = e(nc.sbuf_tensor("p3_sbf", [128, 4096], BF16))
    yfm = e(nc.sbuf_tensor("p3_yfm", [128, 4096], BF16))
    psA = e(nc.psum_tensor("p3_psA", [128, 1024], F32))
    psB = e(nc.psum_tensor("p3_psB", [64, 1024], F32))
    psC = e(nc.psum_tensor("p3_psC", [128, 512], F32))

    htok, sgtok, bctok = [], [], []
    sttok = E.dst3.plan(E.nck)

    SPE, SDV, SAC, SGP = E.s_pe, E.s_dv, E.s_ac, E.s_g2
    sch = []
    for ck in range(nck):
        d = {}
        d["t_tp0"] = SPE.bump()
        d["v_etp0"] = SDV.bump()
        d["t_tp1"] = SPE.bump()
        d["v_sbf"] = SDV.bump()
        d["v_etp1"] = SDV.bump()
        d["t_cbt"] = SPE.bump()
        d["v_cbt"] = SDV.bump()
        d["v_sneg"] = SDV.bump()
        for s in range(NB + 1):
            if s < NB:
                d[f"t1_{s}"] = SPE.bump()
            if s >= 1:
                d[f"t5_{s - 1}"] = SPE.bump()
            if s < NB:
                d[f"t2_{s}"] = SPE.bump()
        for q in range(NB):
            d[f"a1_{q}"] = SAC.bump(2)
            d[f"a2_{q}"] = SAC.bump(2)
            if q >= 1:
                d[f"a3_{q - 1}"] = SAC.bump(2)
        d[f"a3_{NB - 1}"] = SAC.bump(2)
        for s in range(NB + 1):
            if s < NB:
                d[f"g0_{s}"] = SGP.bump()
            if s >= 1:
                d[f"g1_{s - 1}"] = SGP.bump()
        for q in range(NB):
            if q >= 1:
                d[f"v2_{q - 1}"] = SDV.bump(2)
            d[f"v1_{q}"] = SDV.bump(4)
        d[f"v2_{NB - 1}"] = SDV.bump(2)
        d["v_y2h0"] = SDV.bump(4)
        d["v_y2h1"] = SDV.bump(4)
        d["v_y2"] = SDV.bump(1)
        d["t_rms"] = SPE.bump()
        d["a_inv"] = SAC.bump()
        sch.append(d)

    def pget(dd, key, q, ck):
        if q >= 0:
            return sch[ck][f"{key}_{q}"]
        if ck >= 1:
            return sch[ck - 1][f"{key}_{q + NB}"]
        return None

    p1w = _P1Work(nc, e, E, nck) if (with_p1 and E.p1_nfb > 1) else None

    with nc.Block() as blk:
        @blk.sync
        def _(sy):
            for ck in range(nck):
                if ck >= 2:
                    sy.wait_ge(SPE.h, sch[ck - 2]["t_rms"])
                    sy.wait_ge(SDV.h, sch[ck - 2]["v_y2"])
                    sy.wait_ge(SAC.h, sch[ck - 2]["a_inv"])
                htok.append(E.dx.issue(
                    sy, hsb[ck % 2][:].rearrange("p (a i) -> p a i", a=32),
                    E.hs_sc[:, :, ck * 128:(ck + 1) * 128].transpose([1, 0, 2])))
                sgtok.append(E.dx2.issue(
                    sy, sgb[ck % 2][:].rearrange("p (a i) -> p a i", a=32),
                    E.sg_sc[:, :, ck * 128:(ck + 1) * 128].transpose([1, 0, 2])))
                bctok.append((
                    E.dcn.issue(sy, bcb[ck % 2][:, 0:128],
                                E.b_sc[:, ck * 128:(ck + 1) * 128]),
                    E.dcn.issue(sy, bcb[ck % 2][:, 128:256],
                                E.c_sc[:, ck * 128:(ck + 1) * 128])))
                if p1w:
                    for it in p1w.items_for(ck):
                        p1w.emit_sync(sy, it)

        @blk.gpsimd
        def _(g):
            for ck in range(nck):
                d = sch[ck]
                p = ck % 2
                p1it = p1w.items_for(ck) if p1w else []
                np1 = len(p1it)
                for s in range(NB + 1):
                    if s < NB:
                        q = s
                        hq = q * 8
                        pq = q % 2
                        t2p = pget(d, "t2", q - 2, ck)
                        if t2p is not None:
                            g.wait_ge(SPE.h, t2p)
                        i_v = E.i64[:, hq:hq + 8].unsqueeze(2) \
                            .to_broadcast([64, 8, 128])
                        s_v = E.sfm_all[0:64, ck * 128:(ck + 1) * 128] \
                            .unsqueeze(1).to_broadcast([64, 8, 128])
                        g.tensor_tensor(
                            ssel[pq][:].rearrange("p (a i) -> p a i", a=8),
                            i_v, s_v, ALU.mult).then_inc(SGP.h, 1)
                        g.drain()
                    if s >= 1:
                        q = s - 1
                        hq = q * 8
                        pq = q % 2
                        g.wait_ge(SAC.h, d[f"a1_{q}"])
                        t5p = pget(d, "t5", q - 2, ck)
                        if t5p is not None:
                            g.wait_ge(SPE.h, t5p)
                        c_v = bcb[p][:, 128:256].unsqueeze(1) \
                            .to_broadcast([128, 8, 128])
                        g.tensor_tensor(
                            cexq[pq][:].rearrange("p (a i) -> p a i", a=8),
                            c_v, esr[:].rearrange("p (a i) -> p a i", a=8),
                            ALU.mult).then_inc(SGP.h, 1)
                        g.drain()
                    if p1w and s < NB:
                        for it in p1w.gslice(ck * NB + s - 2):
                            p1w.emit_gp(g, it)
                if p1w and ck == nck - 1:
                    for gg in range(p1w.G - 2, p1w.G):
                        for it in p1w.gslice(gg):
                            p1w.emit_gp(g, it)

        @blk.tensor
        def _(t):
            for ck in range(nck):
                d = sch[ck]
                p = ck % 2
                p1it = p1w.items_for(ck) if p1w else []
                np1 = len(p1it)
                DmaStream.wait(t, htok[ck])
                DmaStream.wait(t, bctok[ck][0])
                DmaStream.wait(t, bctok[ck][1])
                if ck >= 1:
                    t.wait_ge(SAC.h, sch[ck - 1]["a_inv"])
                    t.wait_ge(SDV.h, sch[ck - 1]["v_y2"])
                    t.wait_ge(SAC.h, sch[ck - 1][f"a2_{NB - 1}"])
                    t.wait_ge(SDV.h, sch[ck - 1][f"v1_{NB - 1}"])
                for ft in range(16):
                    ins = t.transpose(psA[:].bitcast(BF16)
                                      [:, ft * 128:(ft + 1) * 128],
                                      hsb[p][:, ft * 128:(ft + 1) * 128],
                                      E.ident_b[:])
                ins.then_inc(SPE.h, 1)
                t.wait_ge(SDV.h, d["v_etp0"])
                for ft in range(16):
                    ins = t.transpose(psA[:].bitcast(BF16)
                                      [:, ft * 128:(ft + 1) * 128],
                                      hsb[p][:, 2048 + ft * 128:
                                             2048 + (ft + 1) * 128],
                                      E.ident_b[:])
                ins.then_inc(SPE.h, 1)
                t.wait_ge(SDV.h, d["v_etp1"])
                ins = t.transpose(psA[:].bitcast(BF16)[:, 0:128],
                                  bcb[p][:, 0:128], E.ident_b[:])
                t.matmul(psA[:, 64:192], bcb[p][:, 0:128],
                         bcb[p][:, 128:256], start=True, stop=True) \
                    .then_inc(SPE.h, 1)
                for s in range(NB + 1):
                    if s < NB:
                        q = s
                        pq = q % 2
                        a2p = pget(d, "a2", q - 1, ck)
                        if a2p is not None:
                            t.wait_ge(SAC.h, a2p)
                        if q == 0:
                            t.wait_ge(SDV.h, d["v_cbt"])
                        t.wait_ge(SGP.h, d[f"g0_{q}"])
                        for bb in range(2):
                            ins = t.matmul(psA[:, bb * 512:(bb + 1) * 512],
                                           E.ones64[:],
                                           ssel[pq][:, bb * 512:(bb + 1) * 512],
                                           start=True, stop=True)
                        ins.then_inc(SPE.h, 1)
                    if s >= 1:
                        q = s - 1
                        hq = q * 8
                        pq = q % 2
                        t.wait_ge(SDV.h, d[f"v1_{q}"])
                        t.wait_ge(SGP.h, d[f"g1_{q}"])
                        if q == 0:
                            t.wait_ge(SDV.h, d["v_sbf"])
                        a3p = pget(d, "a3", q - 1, ck)
                        if a3p is not None:
                            t.wait_ge(SAC.h, a3p)
                        v2p = pget(d, "v2", q - 1, ck)
                        if v2p is not None:
                            t.wait_ge(SDV.h, v2p)
                        for hh in range(8):
                            h = hq + hh
                            ysl = psB[0:64, hh * 128:(hh + 1) * 128]
                            t.matmul(ysl, s_bfc[:, h * 64:(h + 1) * 64],
                                     cexq[pq][:, hh * 128:(hh + 1) * 128],
                                     start=(hh in (0, 4)), stop=False)
                            t.matmul(ysl, hs_tm[:, h * 64:(h + 1) * 64],
                                     scq[pq][:, hh * 128:(hh + 1) * 128],
                                     start=False, stop=(hh in (3, 7)))
                            dsl = psC[:, hh * 64:(hh + 1) * 64]
                            ins = t.matmul(dsl,
                                           bdq[pq][:, hh * 128:(hh + 1) * 128],
                                           hs_tm[:, h * 64:(h + 1) * 64],
                                           start=(hh == 0), stop=(hh == 7))
                        ins.then_inc(SPE.h, 1)
                    if s < NB:
                        q = s
                        hq = q * 8
                        pq = q % 2
                        t.wait_ge(SAC.h, d[f"a1_{q}"])
                        if q == 0:
                            t.wait_ge(SDV.h, d["v_sneg"])
                        for bb in range(2):
                            t.matmul(psA[:, bb * 512:(bb + 1) * 512],
                                     E.ones64[:],
                                     ssel[pq][:, bb * 512:(bb + 1) * 512],
                                     start=True, stop=False)
                        for bb in range(2):
                            iv = E.i64[:, hq + bb * 4:hq + bb * 4 + 4] \
                                .unsqueeze(2).to_broadcast([64, 4, 128])
                            t.matmul(psA[:, bb * 512:(bb + 1) * 512],
                                     sneg[:], iv, start=False, stop=False)
                        for bb in range(2):
                            rhm = E.negm_b[:].unsqueeze(1) \
                                .to_broadcast([128, 4, 128])
                            ins = t.matmul(psA[:, bb * 512:(bb + 1) * 512],
                                           E.ident_b[:], rhm,
                                           start=False, stop=True)
                        ins.then_inc(SPE.h, 1)
                    if p1w and s < NB - 1:
                        for it in p1w.gslice(ck * NB + s):
                            p1w.emit_pe(t, it)
                last_sl = p1w.gslice(ck * NB + NB - 1) if p1w else []
                for it in last_sl[:1]:
                    p1w.emit_pe(t, it)
                for hf in range(2):
                    t.wait_ge(SDV.h, d[f"v_y2h{hf}"])
                    for ft in range(hf * 16, hf * 16 + 16):
                        ins = t.matmul(psA[:, 0:1],
                                       sgb[p][:, ft * 128:(ft + 1) * 128],
                                       E.ones_b[:],
                                       start=(ft == 0), stop=(ft == 31))
                ins.then_inc(SPE.h, 1)
                for it in last_sl[1:]:
                    p1w.emit_pe(t, it)

        @blk.vector
        def _(v):
            for ck in range(nck):
                d = sch[ck]
                p = ck % 2
                p1it = p1w.items_for(ck) if p1w else []
                np1 = len(p1it)
                if ck == 0:
                    v.memset(s_all[:], 0.0)
                    v.drain()
                if p1w:
                    for qq in (0, 1):
                        for it in p1w.gslice(ck * NB + qq - 2):
                            p1w.emit_dv(v, it)
                v.wait_ge(SPE.h, d["t_tp0"])
                v.tensor_copy(hs_tm[:, 0:2048], psA[:].bitcast(BF16)) \
                    .then_inc(SDV.h, 1)
                v.wait_ge(SPE.h, d["t_tp1"])
                v.tensor_copy(s_bfc[:], s_all[:]).then_inc(SDV.h, 1)
                v.tensor_copy(hs_tm[:, 2048:4096], psA[:].bitcast(BF16)) \
                    .then_inc(SDV.h, 1)
                v.wait_ge(SPE.h, d["t_cbt"])
                v.tensor_copy(b_tm[:], psA[:].bitcast(BF16)[:, 0:128])
                v.tensor_tensor(cbtm[:], psA[:, 64:192], E.triu_f[:],
                                ALU.mult).then_inc(SDV.h, 1)
                v.tensor_scalar(sneg[:], E.sfm_all[0:64, ck * 128:(ck + 1) * 128],
                                -1.0, None, ALU.mult).then_inc(SDV.h, 1)
                def emit_ssl(qq):
                    hq2 = qq * 8
                    v.wait_ge(SPE.h, d[f"t5_{qq}"])
                    esl_v = esl[:, hq2:hq2 + 8].unsqueeze(2) \
                        .to_broadcast([128, 8, 64])
                    ssl = s_all[:, hq2 * 64:hq2 * 64 + 512]
                    v.tensor_tensor(ssl.rearrange("p (a i) -> p a i", a=8),
                                    ssl.rearrange("p (a i) -> p a i", a=8),
                                    esl_v, ALU.mult).then_inc(SDV.h, 1)
                    v.drain()
                    v.tensor_tensor(ssl, ssl, psC[:], ALU.add) \
                        .then_inc(SDV.h, 1)
                    v.drain()

                for q in range(NB):
                    if p1w and q >= 2:
                        for it in p1w.gslice(ck * NB + q - 2):
                            p1w.emit_dv(v, it)
                    if q >= 1:
                        emit_ssl(q - 1)
                    hq = q * 8
                    pq = q % 2
                    v.wait_ge(SAC.h, d[f"a2_{q}"])
                    t5p = pget(d, "t5", q - 2, ck)
                    if t5p is not None:
                        v.wait_ge(SPE.h, t5p)
                    dtv_b = E.dtbf_all[:, ck * NH + hq:ck * NH + hq + 8] \
                        .unsqueeze(2).to_broadcast([128, 8, 128])
                    cb_v = cbtm[:].unsqueeze(1).to_broadcast([128, 8, 128])
                    v.tensor_tensor(cbdt[:].rearrange("p (a i) -> p a i", a=8),
                                    cb_v, dtv_b, ALU.mult).then_inc(SDV.h, 1)
                    v.drain()
                    v.tensor_tensor(scq[pq][:], dexp[:], cbdt[:], ALU.mult) \
                        .then_inc(SDV.h, 1)
                    v.tensor_tensor(wcol[:], wex[:],
                                    E.dtbf_all[:, ck * NH + hq:ck * NH + hq + 8],
                                    ALU.mult).then_inc(SDV.h, 1)
                    v.drain()
                    wc_v = wcol[:].unsqueeze(2).to_broadcast([128, 8, 128])
                    bt_v = b_tm[:].unsqueeze(1).to_broadcast([128, 8, 128])
                    v.tensor_tensor(bdq[pq][:].rearrange("p (a i) -> p a i", a=8),
                                    bt_v, wc_v, ALU.mult).then_inc(SDV.h, 1)
                emit_ssl(NB - 1)
                DmaStream.wait(v, sgtok[ck])
                v.wait_ge(SAC.h, d[f"a3_{NB - 1}"])
                for hf in range(2):
                    h0, h1 = hf * 2048, (hf + 1) * 2048
                    df_v = E.dfull[:, hf * 16:(hf + 1) * 16].unsqueeze(2) \
                        .to_broadcast([128, 16, 128])
                    v.tensor_tensor(
                        hsb[p][:, h0:h1].rearrange("p (a i) -> p a i", a=16),
                        hsb[p][:, h0:h1].rearrange("p (a i) -> p a i", a=16),
                        df_v, ALU.mult).then_inc(SDV.h, 1)
                    v.drain()
                    v.tensor_tensor(yfm[:, h0:h1], yfm[:, h0:h1],
                                    hsb[p][:, h0:h1], ALU.add) \
                        .then_inc(SDV.h, 1)
                    v.drain()
                    v.tensor_tensor(yfm[:, h0:h1], yfm[:, h0:h1],
                                    sgb[p][:, h0:h1], ALU.mult) \
                        .then_inc(SDV.h, 1)
                    v.drain()
                    v.tensor_tensor(sgb[p][:, h0:h1], yfm[:, h0:h1],
                                    yfm[:, h0:h1], ALU.mult) \
                        .then_inc(SDV.h, 1)
                    v.drain()
                nw_v = E.nw_bf[:].unsqueeze(2).to_broadcast([128, 32, 128])
                v.tensor_tensor(yfm[:].rearrange("p (a i) -> p a i", a=32),
                                yfm[:].rearrange("p (a i) -> p a i", a=32),
                                nw_v, ALU.mult).then_inc(SDV.h, 1)
                if p1w and ck == nck - 1:
                    for g in range(p1w.G - 2, p1w.G):
                        for it in p1w.gslice(g):
                            p1w.emit_dv(v, it)

        @blk.scalar
        def _(s):
            for ck in range(nck):
                d = sch[ck]
                p1it = p1w.items_for(ck) if p1w else []
                np1 = len(p1it)
                for q in range(NB):
                    if p1w:
                        for it in p1w.gslice(ck * NB + q - 1):
                            p1w.emit_ac(s, it)
                    hq = q * 8
                    pq = q % 2
                    s.wait_ge(SPE.h, d[f"t1_{q}"])
                    g1p = pget(d, "g1", q - 1, ck)
                    if g1p is not None:
                        s.wait_ge(SGP.h, g1p)
                    v1p = pget(d, "v1", q - 1, ck)
                    if v1p is not None:
                        s.wait_ge(SDV.h, v1p)
                    if ck >= 1:
                        s.wait_ge(SDV.h, sch[ck - 1][f"v2_{q}"])
                    s.activation(esr[:], psA[:], ACT.Exp)
                    s.activation(esl[:, hq:hq + 8], psA[:, 127::128],
                                 ACT.Exp).then_inc(SAC.h, 2)
                    s.wait_ge(SPE.h, d[f"t2_{q}"])
                    s.activation(dexp[:], psA[:], ACT.Exp)
                    s.activation(wex[:], psA[:, 127::128], ACT.Exp) \
                        .then_inc(SAC.h, 2)
                    qq = q - 1
                    if qq >= 0:
                        if qq == 0 and ck >= 1:
                            DmaStream.wait(s, sttok[ck - 1])
                        _p3_yevac(s, E, d, yfm, psB, qq, SPE, SAC, None)
                _p3_yevac(s, E, d, yfm, psB, NB - 1, SPE, SAC, None)
                if p1w and ck == nck - 1:
                    for g in range(p1w.G - 1, p1w.G):
                        for it in p1w.gslice(g):
                            p1w.emit_ac(s, it)
                s.wait_ge(SPE.h, d["t_rms"])
                s.activation(E.inv_at[:, ck:ck + 1], psA[:, 0:1], ACT.Ln,
                             bias=E.c_epsat[:], scale=1.0 / INTER)
                s.drain()
                s.activation(E.inv_at[:, ck:ck + 1], E.inv_at[:, ck:ck + 1],
                             ACT.Exp, scale=-0.5).then_inc(SAC.h, 1)
                s.wait_ge(SDV.h, d["v_y2"])
                tk = E.dst3.issue(
                    s, E.ynw_sc[ck].transpose([1, 0, 2]),
                    yfm[:].rearrange("p (a i) -> p a i", a=32))
                assert tk == sttok[ck], (tk, sttok[ck])

    _quiesce_stores(nc, E)
    ps_.close()


class _P1Work:

    def __init__(self, nc, e, E, nck):
        self.nc, self.E = nc, E
        TBF = E.p1_TBF
        self.TBF = TBF
        self.nttf = TBF // 128
        self.h_sc = nc.dram_tensor("p13_h", [NIT_DN, 128, TBF], BF16).ap()
        self.xnf = e(nc.sbuf_tensor("q1_xn", [128, NHT * TBF], BF16))
        self.wtg = [e(nc.sbuf_tensor(f"q1_wg{i}", [128, 1024], BF16))
                    for i in range(2)]
        self.wtu = [e(nc.sbuf_tensor(f"q1_wu{i}", [128, 1024], BF16))
                    for i in range(2)]
        self.sgst = [e(nc.sbuf_tensor(f"q1_sg{i}", [128, TBF], BF16))
                     for i in range(2)]
        self.hstg = [e(nc.sbuf_tensor(f"q1_hs{i}", [128, TBF], BF16))
                     for i in range(2)]
        self.wdn = e(nc.sbuf_tensor("q1_wdn", [128, NIT_DN * 512], BF16))
        self.hslab = [e(nc.sbuf_tensor(f"q1_hl{i}", [128, 22 * 128], BF16))
                      for i in range(2)]
        self.sqb = [e(nc.sbuf_tensor(f"q1_sqb{i}", [128, 512], F32))
                    for i in range(2)]
        self.sscol = e(nc.sbuf_tensor("q1_ssc", [128, 1], F32))
        self.ssq = e(nc.sbuf_tensor("q1_ssq", [128, 4], F32))
        self.fst = [e(nc.sbuf_tensor(f"q1_fs{i}", [128, 512], BF16))
                    for i in range(2)]
        self.pool = e(nc.psum_tensor("q1_pool", [128, 1536], F32))
        self.slot_state = [None, None, None]
        self.slot_i = 0

        items = []
        pending_inv = None
        for fb in range(1, E.p1_nfb):
            items.append(("xnf", fb))
            if pending_inv is not None:
                items.append(pending_inv)
                pending_inv = None
            for j in range(NPAIR_GU):
                items.append(("gu", fb, j))
            for oq in range(4):
                items.append(("dnl", fb, oq))
                for tt in range(self.nttf):
                    items.append(("dn", fb, oq, tt))
            pending_inv = ("inv", fb)
        if pending_inv is not None:
            items.append(("pad", 0))
            items.append(pending_inv)
        self.items = items
        self.G = nck * 8
        self.slices = []
        n = len(items)
        for ck in range(nck):
            lo = n * ck // nck
            hi = n * (ck + 1) // nck
            self.slices.append(items[lo:hi])

        QPE, QAC, QDV = E.s_qpe, E.s_qac, E.s_qdv
        self.sch = {}
        ndst2 = sum(1 for it in items if it[0] in ("gu", "dn"))
        self.d2tok = E.dst2.plan(ndst2)
        d2i = 0
        si = 0
        for it in items:
            d = {}
            if it[0] == "gu":
                d["slot_g"] = si % 3
                d["slot_u"] = (si + 1) % 3
                si += 2
                d["pe_gh"] = QPE.bump()
                d["pe_g"] = QPE.bump()
                d["pe_uh"] = QPE.bump()
                d["pe_u"] = QPE.bump()
                d["silu"] = QAC.bump()
                d["ucp"] = QAC.bump()
                d["htt"] = QDV.bump()
                d["hst"] = self.d2tok[d2i]
                d2i += 1
            elif it[0] == "dn":
                d["slot"] = si % 3
                si += 1
                d["pe_h"] = QPE.bump()
                d["pe"] = QPE.bump()
                d["sq"] = QAC.bump()
                d["fcp"] = QAC.bump()
                d["dv"] = QDV.bump(2)
                d["fst"] = self.d2tok[d2i]
                d2i += 1
            elif it[0] == "inv":
                d["inv"] = QAC.bump()
            self.sch[it] = d
        self.toks = {}
        self.xnf_tok = {}

    def items_for(self, ck):
        return self.slices[ck]

    def slot(self, i):
        return self.pool[:, 512 * i:512 * (i + 1)]

    def wait_slot(self, eng, i):
        st_ = self.slot_state[i]
        if st_ is not None:
            for sem, val in st_:
                eng.wait_ge(sem.h, val)

    def gslice(self, g):
        if g < 0 or g >= self.G:
            return []
        n = len(self.items)
        return self.items[n * g // self.G:n * (g + 1) // self.G]

    def emit_sync(self, sy, it):
        E = self.E
        if it[0] == "xnf":
            fb = it[1]
            if fb >= 2:
                pg = self.sch[("gu", fb - 1, NPAIR_GU - 1)]
                sy.wait_ge(E.s_qpe.h, pg["pe_u"])
            self.xnf_tok[fb] = E.dxp.issue(
                sy, self.xnf[:].rearrange("p (a t) -> p a t", a=NHT),
                E.xn_sc[:, :, fb * self.TBF:(fb + 1) * self.TBF]
                .transpose([1, 0, 2]))
        elif it[0] == "gu":
            fb, j = it[1], it[2]
            if j >= 1:
                dp = self.sch[("gu", fb, j - 1)]
            elif fb >= 2:
                dp = self.sch[("gu", fb - 1, NPAIR_GU - 1)]
            else:
                dp = None
            for half, gkey, wt, wsrc, strm in (
                    (0, "pe_gh", self.wtg, E.wgu_sc[j], E.dxp2),
                    (1, "pe_g", self.wtg, E.wgu_sc[j], E.dxp2),
                    (0, "pe_uh", self.wtu, E.wgu_sc[j + NPAIR_GU], E.dxp3),
                    (1, "pe_u", self.wtu, E.wgu_sc[j + NPAIR_GU], E.dxp3)):
                if dp is not None:
                    sy.wait_ge(E.s_qpe.h, dp[gkey])
                self.toks[(gkey, fb, j)] = strm.issue(
                    sy, wt[half][:], wsrc[:, half * 1024:(half + 1) * 1024])
        elif it[0] == "dnl":
            fb, oq = it[1], it[2]
            if oq >= 1:
                sy.wait_ge(E.s_qpe.h, self.sch[("dn", fb, oq - 1,
                                                self.nttf - 1)]["pe"])
            elif fb >= 2:
                sy.wait_ge(E.s_qpe.h, self.sch[("dn", fb - 1, 3,
                                                self.nttf - 1)]["pe"])
            if oq == 0:
                DmaStream.wait(sy, self.sch[("gu", fb, NPAIR_GU - 1)]["hst"])
                DmaStream.wait(sy, self.sch[("gu", fb, NPAIR_GU - 2)]["hst"])
            self.toks[("wdn", fb, oq)] = E.dxp.issue(
                sy, self.wdn[:].rearrange("p (a i) -> p a i", a=NIT_DN),
                E.wdn_sc[oq])
        elif it[0] == "dn":
            fb, oq, tt = it[1], it[2], it[3]
            k = oq * self.nttf + tt
            if k >= 1:
                oq2, tt2 = divmod(k - 1, self.nttf)
                dp = self.sch[("dn", fb, oq2, tt2)]
            elif fb >= 2:
                dp = self.sch[("dn", fb - 1, 3, self.nttf - 1)]
            else:
                dp = None
            hsrc = self.h_sc[:, :, tt * 128:(tt + 1) * 128].transpose([1, 0, 2])
            if dp is not None:
                sy.wait_ge(E.s_qpe.h, dp["pe_h"])
            self.toks[("hla", fb, oq, tt)] = E.dxp2.issue(
                sy, self.hslab[0][:].rearrange("p (a i) -> p a i", a=22),
                hsrc[:, 0:22])
            if dp is not None:
                sy.wait_ge(E.s_qpe.h, dp["pe"])
            self.toks[("hlb", fb, oq, tt)] = E.dxp2.issue(
                sy, self.hslab[1][:].rearrange("p (a i) -> p a i", a=22),
                hsrc[:, 22:44])

    def emit_pe(self, t, it):
        E = self.E
        if it[0] == "gu":
            fb, j = it[1], it[2]
            d = self.sch[it]
            DmaStream.wait(t, self.xnf_tok[fb])
            for slot_key, wt, keys in ((d["slot_g"], self.wtg,
                                        ("pe_gh", "pe_g")),
                                       (d["slot_u"], self.wtu,
                                        ("pe_uh", "pe_u"))):
                self.wait_slot(t, slot_key)
                for half in range(2):
                    DmaStream.wait(t, self.toks[(keys[half], fb, j)])
                    for h2 in range(NHT // 2):
                        ht = half * (NHT // 2) + h2
                        ins = t.matmul(
                            self.slot(slot_key),
                            wt[half][:, h2 * 128:(h2 + 1) * 128],
                            self.xnf[:, ht * self.TBF:(ht + 1) * self.TBF],
                            start=(ht == 0), stop=(ht == NHT - 1))
                    ins.then_inc(E.s_qpe.h, 1)
            self.slot_state[d["slot_g"]] = [(E.s_qac, d["silu"])]
            self.slot_state[d["slot_u"]] = [(E.s_qac, d["ucp"])]
        elif it[0] == "dn":
            fb, oq, tt = it[1], it[2], it[3]
            k = oq * self.nttf + tt
            d = self.sch[it]
            DmaStream.wait(t, self.toks[("wdn", fb, oq)])
            self.wait_slot(t, d["slot"])
            DmaStream.wait(t, self.toks[("hla", fb, oq, tt)])
            for i2 in range(22):
                ins = t.matmul(self.slot(d["slot"]),
                               self.hslab[0][:, i2 * 128:(i2 + 1) * 128],
                               self.wdn[:, i2 * 512:(i2 + 1) * 512],
                               start=(i2 == 0), stop=False)
            ins.then_inc(E.s_qpe.h, 1)
            DmaStream.wait(t, self.toks[("hlb", fb, oq, tt)])
            for i2 in range(22, NIT_DN):
                ins = t.matmul(self.slot(d["slot"]),
                               self.hslab[1][:, (i2 - 22) * 128:
                                             (i2 - 21) * 128],
                               self.wdn[:, i2 * 512:(i2 + 1) * 512],
                               start=False, stop=(i2 == NIT_DN - 1))
            ins.then_inc(E.s_qpe.h, 1)
            self.slot_state[d["slot"]] = [(E.s_qac, d["fcp"])]

    def emit_ac(self, s, it):
        E = self.E
        if it[0] == "gu":
            fb, j = it[1], it[2]
            d = self.sch[it]
            s.wait_ge(E.s_qpe.h, d["pe_g"])
            if j >= 2:
                s.wait_ge(E.s_qdv.h, self.sch[("gu", fb, j - 2)]["htt"])
            elif fb >= 2:
                s.wait_ge(E.s_qdv.h,
                          self.sch[("gu", fb - 1, NPAIR_GU - 2 + j)]["htt"])
            s.activation(self.sgst[j % 2][:], self.slot(d["slot_g"]),
                         ACT.Silu).then_inc(E.s_qac.h, 1)
            s.wait_ge(E.s_qpe.h, d["pe_u"])
            if j >= 2:
                DmaStream.wait(s, self.sch[("gu", fb, j - 2)]["hst"])
            elif fb >= 2:
                DmaStream.wait(
                    s, self.sch[("gu", fb - 1, NPAIR_GU - 2 + j)]["hst"])
            s.activation(self.hstg[j % 2][:], self.slot(d["slot_u"]),
                         ACT.Copy).then_inc(E.s_qac.h, 1)
        elif it[0] == "dn":
            fb, oq, tt = it[1], it[2], it[3]
            d = self.sch[it]
            s.wait_ge(E.s_qpe.h, d["pe"])
            k = oq * self.nttf + tt
            if k >= 2:
                oq2, tt2 = divmod(k - 2, self.nttf)
                s.wait_ge(E.s_qdv.h, self.sch[("dn", fb, oq2, tt2)]["dv"])
            elif fb >= 2:
                kp = 4 * self.nttf + (k - 2)
                oq2, tt2 = divmod(kp, self.nttf)
                s.wait_ge(E.s_qdv.h, self.sch[("dn", fb - 1, oq2, tt2)]["dv"])
            s.activation(self.sqb[k % 2][:], self.slot(d["slot"]), ACT.Square) \
                .then_inc(E.s_qac.h, 1)
            if k >= 2:
                oq2, tt2 = divmod(k - 2, self.nttf)
                DmaStream.wait(s, self.sch[("dn", fb, oq2, tt2)]["fst"])
            elif fb >= 2:
                kp = 4 * self.nttf + (k - 2)
                oq2, tt2 = divmod(kp, self.nttf)
                DmaStream.wait(s, self.sch[("dn", fb - 1, oq2, tt2)]["fst"])
            s.activation(self.fst[k % 2][:], self.slot(d["slot"]), ACT.Copy) \
                .then_inc(E.s_qac.h, 1)
        elif it[0] == "inv":
            fb = it[1]
            d = self.sch[it]
            for tt2 in range(self.nttf):
                s.wait_ge(E.s_qdv.h, self.sch[("dn", fb, 3, tt2)]["dv"])
            t0 = fb * self.nttf
            s.activation(E.inv_ff[:, t0:t0 + self.nttf],
                         self.ssq[:, 0:self.nttf], ACT.Ln,
                         bias=E.c_epsff[:], scale=1.0 / H)
            s.drain()
            s.activation(E.inv_ff[:, t0:t0 + self.nttf],
                         E.inv_ff[:, t0:t0 + self.nttf],
                         ACT.Exp, scale=-0.5).then_inc(E.s_qac.h, 1)

    def emit_dv(self, v, it):
        E = self.E
        if it[0] == "gu":
            fb, j = it[1], it[2]
            d = self.sch[it]
            v.wait_ge(E.s_qac.h, d["ucp"])
            v.tensor_tensor(self.hstg[j % 2][:], self.hstg[j % 2][:],
                            self.sgst[j % 2][:], ALU.mult) \
                .then_inc(E.s_qdv.h, 1)
        elif it[0] == "dn":
            fb, oq, tt = it[1], it[2], it[3]
            k = oq * self.nttf + tt
            d = self.sch[it]
            v.wait_ge(E.s_qac.h, d["sq"])
            v.reduce_sum(self.sscol[:], self.sqb[k % 2][:], axis=AX.X) \
                .then_inc(E.s_qdv.h, 1)
            v.drain()
            if oq == 0:
                if fb >= 2 and tt == 0:
                    iv = self.sch[("inv", fb - 1)]
                    v.wait_ge(E.s_qac.h, iv["inv"])
                v.tensor_copy(self.ssq[:, tt:tt + 1], self.sscol[:]) \
                    .then_inc(E.s_qdv.h, 1)
            else:
                v.tensor_tensor(self.ssq[:, tt:tt + 1], self.ssq[:, tt:tt + 1],
                                self.sscol[:], ALU.add).then_inc(E.s_qdv.h, 1)
            v.drain()

    def emit_gp(self, g, it):
        E = self.E
        if it[0] == "gu":
            fb, j = it[1], it[2]
            d = self.sch[it]
            g.wait_ge(E.s_qdv.h, d["htt"])
            if fb >= 2 and j == 0:
                DmaStream.wait(g, self.toks[("hlb", fb - 1, 3, self.nttf - 1)])
            tk = E.dst2.issue(g, self.h_sc[j], self.hstg[j % 2][:])
            assert tk == d["hst"], (tk, d["hst"])
        elif it[0] == "dn":
            fb, oq, tt = it[1], it[2], it[3]
            d = self.sch[it]
            g.wait_ge(E.s_qac.h, d["fcp"])
            ttg = fb * self.nttf + tt
            tk = E.dst2.issue(
                g, E.ffn_sc[ttg, :, oq * 512:(oq + 1) * 512],
                self.fst[(oq * self.nttf + tt) % 2][:])
            assert tk == d["fst"], (tk, d["fst"])


def _p3_yevac(s, E, d, yfm, psB, q, SPE, SAC, dprev):
    hq = q * 8
    s.wait_ge(SPE.h, d[f"t5_{q}"])
    s.activation(
        yfm[0:64, hq * 64:hq * 64 + 512].rearrange("p (a i) -> p a i", a=4),
        psB[0:64, :].rearrange("p (a b i) -> p a b i", a=4, b=2)[:, :, 0, :],
        ACT.Copy)
    s.activation(
        yfm[64:128, hq * 64:hq * 64 + 512].rearrange("p (a i) -> p a i", a=4),
        psB[0:64, :].rearrange("p (a b i) -> p a b i", a=4, b=2)[:, :, 1, :],
        ACT.Copy).then_inc(SAC.h, 2)


def _p4_out(nc, st, E):
    ps_ = ExitStack()
    e = ps_.enter_context
    T, ntt = E.T, E.ntt

    wout = e(nc.sbuf_tensor("p4_w", [128, 32 * 1024], BF16))
    ysb = [e(nc.sbuf_tensor(f"p4_y{i}", [128, 32 * 128], BF16)) for i in range(2)]
    xsb = [e(nc.sbuf_tensor(f"p4_x{i}", [128, 1024], F32)) for i in range(2)]
    fsb = [e(nc.sbuf_tensor(f"p4_f{i}", [128, 1024], BF16)) for i in range(2)]
    st1 = e(nc.sbuf_tensor("p4_s1", [128, 512], F32))
    ost = [e(nc.sbuf_tensor(f"p4_o{i}", [128, 512], F32)) for i in range(4)]
    psO = e(nc.psum_tensor("p4_ps", [128, 2048], F32))

    SPE, SDV, SAC = E.s_pe, E.s_dv, E.s_ac
    otoks = E.dst.plan(2 * ntt * 2)

    for oh in range(2):
        mm_d = []
        ev_d = []
        for tt in range(ntt):
            for ob in range(2):
                mm_d.append(E.s_pe.bump())
                ev_d.append(E.s_dv.bump(2))
        wtok, ytok, xtok, ftok2 = [None], [], [], []

        with nc.Block() as blk:
            @blk.gpsimd
            def _(g):
                wtok[0] = E.dgp.issue(
                    g, wout[:].rearrange("p (a f) -> p a f", a=32),
                    E.woutT[:, oh * 1024:(oh + 1) * 1024].rearrange(
                        "(a p) f -> p a f", p=128))

            @blk.sync
            def _(sy):
                for tt in range(ntt):
                    if tt >= 2:
                        sy.wait_ge(SPE.h, mm_d[2 * (tt - 2) + 1])
                        sy.wait_ge(SDV.h, ev_d[2 * (tt - 2) + 1])
                    ytok.append(E.dx.issue(
                        sy, ysb[tt % 2][:].rearrange("p (a i) -> p a i", a=32),
                        E.ynw_sc[tt].transpose([1, 0, 2])))
                    xtok.append(E.dx2.issue(
                        sy, xsb[tt % 2][:],
                        E.x_in[tt * 128:(tt + 1) * 128,
                               oh * 1024:(oh + 1) * 1024]))
                    ftok2.append(E.dcn.issue(
                        sy, fsb[tt % 2][:],
                        E.ffn_sc[tt, :, oh * 1024:(oh + 1) * 1024]))

            @blk.tensor
            def _(t):
                DmaStream.wait(t, wtok[0])
                for tt in range(ntt):
                    DmaStream.wait(t, ytok[tt])
                    for ob in range(2):
                        k = tt * 2 + ob
                        slot = k % 4
                        if k >= 4:
                            t.wait_ge(SDV.h, ev_d[k - 4])
                        for ft in range(32):
                            ins = t.matmul(
                                psO[:, slot * 512:(slot + 1) * 512],
                                ysb[tt % 2][:, ft * 128:(ft + 1) * 128],
                                wout[:, ft * 1024 + ob * 512:
                                     ft * 1024 + ob * 512 + 512],
                                start=(ft == 0), stop=(ft == 31))
                        ins.then_inc(SPE.h, 1)

            @blk.vector
            def _(v):
                for tt in range(ntt):
                    DmaStream.wait(v, xtok[tt])
                    DmaStream.wait(v, ftok2[tt])
                    for ob in range(2):
                        k = tt * 2 + ob
                        slot = k % 4
                        gidx = oh * 2 * ntt + k
                        v.wait_ge(SPE.h, mm_d[k])
                        if gidx >= 4:
                            DmaStream.wait(v, otoks[gidx - 4])
                        v.scalar_tensor_tensor(
                            st1[:], psO[:, slot * 512:(slot + 1) * 512],
                            E.inv_at[:, tt:tt + 1],
                            xsb[tt % 2][:, ob * 512:(ob + 1) * 512],
                            ALU.mult, ALU.add).then_inc(SDV.h, 1)
                        v.drain()
                        v.scalar_tensor_tensor(
                            ost[k % 4][:], fsb[tt % 2][:, ob * 512:(ob + 1) * 512],
                            E.inv_ff[:, tt:tt + 1], st1[:],
                            ALU.mult, ALU.add).then_inc(SDV.h, 1)
                        v.drain()

            @blk.scalar
            def _(s):
                for tt in range(ntt):
                    for ob in range(2):
                        k = tt * 2 + ob
                        gidx = oh * 2 * ntt + k
                        s.wait_ge(SDV.h, ev_d[k])
                        tk = E.dst.issue(
                            s, E.out[tt * 128:(tt + 1) * 128,
                                     oh * 1024 + ob * 512:
                                     oh * 1024 + (ob + 1) * 512],
                            ost[k % 4][:])
                        assert tk == otoks[gidx]

    with nc.Block() as blk:
        @blk.sync
        def _(sy):
            for tk in otoks[-3:]:
                DmaStream.wait(sy, tk)
    ps_.close()


_prog_cache = {}


def _host_prep(inputs):
    w = {}
    w["winT"] = np.ascontiguousarray(np.asarray(inputs["L0_attn_in_proj_w"]).T)
    w["woutT"] = np.ascontiguousarray(np.asarray(inputs["L0_attn_out_proj_w"]).T)
    cw = np.asarray(inputs["L0_attn_conv_w"]).reshape(34, 128, 4)
    w["convw"] = np.ascontiguousarray(cw.transpose(1, 0, 2).reshape(128, 136))
    w["convb"] = np.ascontiguousarray(
        np.asarray(inputs["L0_attn_conv_b"]).reshape(34, 128).T)
    w["dtb"] = np.asarray(inputs["L0_attn_dt_bias"]).reshape(1, 64)
    w["alog"] = np.asarray(inputs["L0_attn_A_log"]).reshape(1, 64)
    w["dvec"] = np.asarray(inputs["L0_attn_D"]).reshape(1, 64)
    w["nw"] = np.ascontiguousarray(
        np.asarray(inputs["L0_attn_norm_weight"]).reshape(32, 128).T)
    w["gup"] = np.asarray(inputs["L0_ffn_gate_up_packed"])
    w["gus"] = np.ascontiguousarray(np.asarray(inputs["L0_ffn_gate_up_scales"]).T)
    w["dnp"] = np.asarray(inputs["L0_ffn_down_packed"])
    w["dns"] = np.ascontiguousarray(np.asarray(inputs["L0_ffn_down_scales"]).T)
    return {k: np.ascontiguousarray(v) for k, v in w.items()}


def kernel(**inputs):
    x = np.asarray(inputs["L0_attn_hidden_states"])
    B = x.shape[0]
    w = _host_prep(inputs)
    key = ("full", 2048)
    if key not in _prog_cache:
        _prog_cache[key] = build(T=2048)
    nc = _prog_cache[key]
    in_maps = []
    for c in range(B):
        m = {"x_in": np.ascontiguousarray(np.asarray(x[c], dtype=np.float32))}
        m.update(w)
        in_maps.append(m)
    res = run_bass_kernel_spmd(nc, in_maps, core_ids=list(range(B)))
    outs = [res.results[c]["out"] for c in range(B)]
    return np.stack(outs, axis=0).astype(np.float32)
